# version 02
# baseline (speedup 1.0000x reference)
import numpy as np
import ml_dtypes
import concourse.bass as bass
import concourse.mybir as mybir
from concourse.bass_utils import run_bass_kernel_spmd

F32 = mybir.dt.float32
BF16 = mybir.dt.bfloat16
AF = mybir.ActivationFunctionType
ALU = mybir.AluOpType
AX = mybir.AxisListType

NCORES = 8
SEQ_PER_CORE = 4
S = 2048
D = 1024
NCHUNK = 8
NT512 = S // 512
NTSUB = S // 128
LN_EPS = 1e-5

BF = ml_dtypes.bfloat16



def repack_slab_a(W):
    *lead, K, N = W.shape
    A = W.reshape(*lead, K // 128, 128, N // 128, 128)
    nl = len(lead)
    perm = list(range(nl)) + [nl + 2, nl + 1, nl + 0, nl + 3]
    return np.ascontiguousarray(A.transpose(*perm))


def repack_resident_b(W):
    *lead, K, N = W.shape
    A = W.reshape(*lead, K // 128, 128, N)
    nl = len(lead)
    perm = list(range(nl)) + [nl + 1, nl + 0, nl + 2]
    return np.ascontiguousarray(A.transpose(*perm))


def repack_cols(v):
    *lead, N = v.shape
    A = v.reshape(*lead, N // 128, 128)
    nl = len(lead)
    perm = list(range(nl)) + [nl + 1, nl + 0]
    return np.ascontiguousarray(A.transpose(*perm))


def build_host_consts():
    C = {}
    ident = np.eye(128, dtype=np.float32)
    C["c_ident"] = ident.astype(BF)
    C["c_negident"] = (-ident).astype(BF)
    k = np.arange(128)[:, None]
    q = np.arange(128)[None, :]
    C["c_maskT_diag"] = (k <= q).astype(np.float32).astype(BF)
    il = np.arange(128)[:, None]
    jl = np.arange(128)[None, :]
    blk0 = np.where(il <= jl, 0.0, -1e9)
    blk1 = np.zeros((128, 128))
    blk2 = np.where(jl <= il, 0.0, -1e9)
    C["c_negmask_local"] = np.concatenate([blk0, blk1, blk2], axis=1).astype(np.float32).astype(BF)
    C["c_negmask_diag"] = blk2.astype(np.float32).astype(BF)
    C["c_negmaskT_local01"] = np.concatenate(
        [np.where(k <= q, 0.0, -1e9), np.where(q <= k, 0.0, -1e9)], axis=1
    ).astype(np.float32).astype(BF)
    pat = np.zeros((128, 33), np.float32)
    pat[0:64, 0] = -0.0625
    pat[64:128, 32] = -0.0625
    C["c_favor_pat"] = pat.astype(BF)
    indA = np.zeros((1, 128), np.float32); indA[0, 0:64] = 1.0
    indB = np.zeros((1, 128), np.float32); indB[0, 64:128] = 1.0
    C["c_indA"] = indA.astype(BF)
    C["c_indB"] = indB.astype(BF)
    C["c_ones_1x128"] = np.ones((1, 128), np.float32).astype(BF)
    C["c_ones_1x512"] = np.ones((1, 512), np.float32).astype(BF)
    C["c_ones_128x1"] = np.ones((128, 1), np.float32).astype(BF)
    st0 = np.zeros((128, 36), np.float32); st0[:, 0] = 1.0
    st32 = np.zeros((128, 36), np.float32); st32[:, 32] = 1.0
    C["c_stI0"] = st0.astype(BF)
    C["c_stI32"] = st32.astype(BF)
    DIM_HEAD, WINDOW = 64, 256
    half = np.arange(0, DIM_HEAD, 2, dtype=np.float32)
    inv_freq = (1.0 / 10000.0 ** (half / DIM_HEAD)).astype(np.float32)
    t = np.arange(S, dtype=np.float32)
    freqs = (t[:, None] * inv_freq[None, :]).astype(np.float32)
    freqs = np.concatenate([freqs, freqs], -1)
    base_scale = ((half + 0.4 * DIM_HEAD) / (1.4 * DIM_HEAD)).astype(np.float32)
    power = ((t - S // 2) / (WINDOW // 2)).astype(np.float32)
    scale = (base_scale[None, :] ** power[:, None]).astype(np.float32)
    scale = np.concatenate([scale, scale], -1)
    cos = np.cos(freqs).astype(np.float32)
    sin = np.sin(freqs).astype(np.float32)
    inv = (1.0 / scale).astype(np.float32)

    def lay(tab):
        return np.ascontiguousarray(
            tab.reshape(16, 128, 64).transpose(1, 0, 2)).astype(np.float32)
    sign = np.concatenate([-np.ones(32, np.float32), np.ones(32, np.float32)])
    QK_SCALE = np.float32(8.0)
    C["c_rot_cs_q"] = lay(cos * scale * QK_SCALE)
    C["c_rot_sn_q"] = lay(sin * scale * sign * QK_SCALE)
    C["c_rot_cs_k"] = lay(cos * inv)
    C["c_rot_sn_k"] = lay(sin * inv * sign)
    return C


def prep_inputs(I):
    H = {}
    H["aWq"] = repack_slab_a(I["L0_attn_a_Wq"])
    H["aWk"] = repack_slab_a(I["L0_attn_a_Wk"])
    H["aWv"] = repack_resident_b(I["L0_attn_a_Wv"])
    H["aWo"] = repack_resident_b(I["L0_attn_a_Wo"])
    H["aW1"] = repack_slab_a(I["L0_attn_a_W1"])
    H["aW2"] = repack_slab_a(I["L0_attn_a_W2"])
    H["abq"] = repack_cols(I["L0_attn_a_bq"])
    H["abk"] = repack_cols(I["L0_attn_a_bk"])
    H["abv_row"] = np.ascontiguousarray(I["L0_attn_a_bv"])[:, None, :]
    H["abo"] = repack_cols(I["L0_attn_a_bo"])
    H["abf1"] = repack_cols(I["L0_attn_a_bf1"])
    H["abf2"] = repack_cols(I["L0_attn_a_bf2"])
    H["aln1g"] = repack_cols(I["L0_attn_a_ln1_g"])
    H["aln1b"] = repack_cols(I["L0_attn_a_ln1_b"])
    H["aln2g"] = repack_cols(I["L0_attn_a_ln2_g"])
    H["aln2b"] = repack_cols(I["L0_attn_a_ln2_b"])
    H["aomega"] = np.ascontiguousarray(I["L0_attn_a_omega"])
    H["fWqkv"] = repack_resident_b(I["L0_ffn_Wqkv"])
    H["fWo"] = repack_slab_a(I["L0_ffn_Wo"])
    W1 = I["L0_ffn_W1"]
    a_part = W1[:, :, :2730]
    g_part = W1[:, :, 2730:]
    pad = np.zeros((6, 1024, 86), np.float32)
    W1p = np.concatenate([a_part, pad, g_part, pad], axis=2)
    w1s = repack_slab_a(W1p)
    H["fW1"] = np.ascontiguousarray(
        np.concatenate([w1s[:, :22], w1s[:, 22:]], axis=4))
    b1 = I["L0_ffn_b1"]
    b1p = np.concatenate([b1[:, :2730], np.zeros((6, 86), np.float32),
                          b1[:, 2730:], np.zeros((6, 86), np.float32)], axis=1)
    H["fb1"] = repack_cols(b1p)
    W2 = I["L0_ffn_W2"]
    W2p = np.concatenate([W2, np.zeros((6, 86, 1024), np.float32)], axis=1)
    H["fW2"] = repack_slab_a(W2p)
    H["fb2"] = repack_cols(I["L0_ffn_b2"])
    H["flng_col"] = repack_cols(I["L0_ffn_ln_g"])
    H["flng_row"] = np.ascontiguousarray(
        I["L0_ffn_ln_g"].reshape(6, 1, 1024))
    H["flnb_col"] = repack_cols(I["L0_ffn_ln_b"])
    H["fqs_row"] = np.ascontiguousarray(I["L0_ffn_q_scale"])[:, None, :]
    H["fks_row"] = np.ascontiguousarray(I["L0_ffn_k_scale"])[:, None, :]
    wq = I["L0_attn_b_w_q"]
    wqT = np.ascontiguousarray(wq.transpose(2, 0, 1).reshape(1024, 1024))
    wk = I["L0_attn_b_w_k"]
    wkT = np.ascontiguousarray(wk.transpose(2, 0, 1).reshape(1024, 1024))
    wv = I["L0_attn_b_w_v"]
    wvT = np.ascontiguousarray(wv.transpose(2, 0, 1).reshape(1024, 1024))
    H["bWq"] = repack_resident_b(wqT)
    H["bWk"] = repack_resident_b(wkT)
    H["bWv"] = repack_resident_b(wvT)
    wo = I["L0_attn_b_w_o"]
    H["bWo"] = repack_resident_b(np.ascontiguousarray(wo.T))
    H.update(build_host_consts())
    return H



ENGINES = ("pe", "dve", "act", "pool", "sync")
ENG_SEM = {"pe": "sPE", "dve": "sDV", "act": "sAC", "pool": "sPL", "sync": "sSY"}


class Prog:
    def __init__(self):
        self.ops = {e: [] for e in ENGINES}
        self.counts = {}
        self.handles = {}
        self.last_wait = {}
        self.dma_n = {"pool": 0, "sync": 0, "act": 0}
        sems = ["sPE", "sDV", "sAC", "sPL", "sSY"]
        for q in ("qW", "qX", "qS"):
            sems += [f"{q}{i}" for i in range(4)]
        for s in sems:
            self.counts[s] = 0

    DMA_SEM = {"pool": "qW", "sync": "qX", "act": "qS"}

    def wait(self, eng, tok):
        if tok is None:
            return
        sem, val = tok
        if val <= 0:
            return
        key = (eng, sem)
        if self.last_wait.get(key, 0) >= val:
            return
        self.last_wait[key] = val
        self.ops[eng].append(("w", sem, val))

    def emit(self, eng, fn, waits=(), inc=None, inc_n=1):
        for t in waits:
            self.wait(eng, t)
        tok = None
        if inc is not None:
            self.counts[inc] += inc_n
            tok = (inc, self.counts[inc])
            self.ops[eng].append(("i", fn, inc, inc_n))
        else:
            self.ops[eng].append(("n", fn))
        return tok

    def dma(self, eng, out_ap, in_ap, waits=()):
        fam = self.DMA_SEM[eng]
        idx = self.dma_n[eng] % 4
        self.dma_n[eng] += 1
        sem = f"{fam}{idx}"
        waits = list(waits) + [(sem, self.counts[sem])]
        return self.emit(eng, lambda e, o=out_ap, i=in_ap: e.dma_start(out=o, in_=i),
                         waits=waits, inc=sem, inc_n=16)

    def run_stream(self, eng_name, engobj):
        for op in self.ops[eng_name]:
            kind = op[0]
            if kind == "w":
                engobj.wait_ge(self.handles[op[1]], op[2])
            elif kind == "i":
                inst = op[1](engobj)
                inst.then_inc(self.handles[op[2]], op[3])
            else:
                op[1](engobj)


class Slot:
    __slots__ = ("ready", "cons")

    def __init__(self):
        self.ready = None
        self.cons = []

    def war(self):
        out = list(self.cons)
        if self.ready is not None:
            out.append(self.ready)
        return out

    def set(self, tok):
        self.ready = tok
        self.cons = []

    def read(self, tok):
        if tok is not None:
            self.cons.append(tok)


class Arena:

    def __init__(self, nc):
        self.nc = nc
        self.off = (nc.sbuf_base + 63) // 64 * 64
        self.budget = nc.sbuf_top
        self.marks = {}

    def alloc(self, name, shape, dtype, align=64):
        per_part = int(np.prod(shape[1:])) * mybir.dt.size(dtype)
        self.off = (self.off + align - 1) // align * align
        t = self.nc.alloc_sbuf_tensor_at(name, list(shape), dtype, offset=self.off)
        self.off += per_part
        assert self.off <= self.budget, f"SBUF overflow at {name}: {self.off}"
        return t

    def mark(self, name):
        self.marks[name] = self.off

    def reset_to(self, name):
        self.off = self.marks[name]


def rep_mid(ap, nrep):
    s = [list(x) for x in ap.ap]
    assert len(s) == 2
    return bass.AP(ap.tensor, ap.offset, [s[0], [0, nrep], s[1]])


def rep_inner(ap, nrep):
    s = [list(x) for x in ap.ap]
    assert len(s) == 2
    return bass.AP(ap.tensor, ap.offset, [s[0], s[1], [0, nrep]])



CONST_SPECS = {
    "c_ident": ([128, 128], BF16), "c_negident": ([128, 128], BF16),
    "c_maskT_diag": ([128, 128], BF16),
    "c_negmask_local": ([128, 384], BF16), "c_negmask_diag": ([128, 128], BF16),
    "c_negmaskT_local01": ([128, 256], BF16),
    "c_favor_pat": ([128, 33], BF16),
    "c_indA": ([1, 128], BF16), "c_indB": ([1, 128], BF16),
    "c_ones_1x128": ([1, 128], BF16), "c_ones_1x512": ([1, 512], BF16),
    "c_ones_128x1": ([128, 1], BF16),
    "c_stI0": ([128, 36], BF16), "c_stI32": ([128, 36], BF16),
}

ROT_SPECS = {
    "c_rot_cs_q": ([128, 16, 64], F32), "c_rot_sn_q": ([128, 16, 64], F32),
    "c_rot_cs_k": ([128, 16, 64], F32), "c_rot_sn_k": ([128, 16, 64], F32),
}

WEIGHT_SPECS = {
    "aWq": [3, 8, 128, 8, 128], "aWk": [3, 8, 128, 8, 128],
    "aWv": [3, 128, 8, 1024], "aWo": [3, 128, 8, 1024],
    "aW1": [3, 32, 128, 8, 128], "aW2": [3, 8, 128, 32, 128],
    "abq": [3, 128, 8], "abk": [3, 128, 8], "abv_row": [3, 1, 1024],
    "abo": [3, 128, 8], "abf1": [3, 128, 32], "abf2": [3, 128, 8],
    "aln1g": [3, 128, 8], "aln1b": [3, 128, 8],
    "aln2g": [3, 128, 8], "aln2b": [3, 128, 8],
    "aomega": [3, 64, 32],
    "fWqkv": [6, 128, 8, 3072], "fWo": [6, 8, 128, 8, 128],
    "fW1": [6, 22, 128, 8, 256], "fb1": [6, 128, 44],
    "fW2": [6, 8, 128, 22, 128], "fb2": [6, 128, 8],
    "flng_col": [6, 128, 8], "flng_row": [6, 1, 1024], "flnb_col": [6, 128, 8],
    "fqs_row": [6, 1, 64], "fks_row": [6, 1, 64],
    "bWq": [128, 8, 1024], "bWk": [128, 8, 1024], "bWv": [128, 8, 1024],
    "bWo": [128, 8, 1024],
}


class MK:
    def __init__(self, nseq=SEQ_PER_CORE, phases="ABCD", debug=()):
        self.nseq = nseq
        self.phases = phases
        self.debug = set(debug)
        nc = bass.Bass("TRN2", target_bir_lowering=False)
        self.nc = nc
        self.P = Prog()
        self.xT = nc.dram_tensor("xT", [nseq, D, S], F32, kind="ExternalInput").ap()
        self.outT = nc.dram_tensor("outT", [nseq, D, S], F32, kind="ExternalOutput").ap()
        self.xr = [nc.dram_tensor(f"xr{i}", [D, S], F32, kind="Internal").ap()
                   for i in range(2)]
        self.W = {k: nc.dram_tensor(k, list(v if isinstance(v, list) else v[0]),
                                    F32, kind="ExternalInput").ap()
                  for k, v in WEIGHT_SPECS.items()}
        self.C = {k: nc.dram_tensor(k, list(sh), dt, kind="ExternalInput").ap()
                  for k, (sh, dt) in {**CONST_SPECS, **ROT_SPECS}.items()}
        self.dbg = {}
        for name in self.debug:
            if isinstance(name, tuple):
                nm, shape = name
                self.dbg[nm] = nc.dram_tensor(
                    "dbg_" + nm, list(shape), F32, kind="ExternalOutput").ap()
            else:
                self.dbg[name] = nc.dram_tensor(
                    "dbg_" + name, [D, S], F32, kind="ExternalOutput").ap()

        A = Arena(nc)
        self.arena = A
        self.XB = A.alloc("XB", [128, 8 * 2052], BF16)
        self.XL = A.alloc("XL", [128, 8 * 2052], BF16)
        self.cs = {k: A.alloc("s_" + k, list(sh), dt)
                   for k, (sh, dt) in CONST_SPECS.items()}
        self.ccols = A.alloc("ccols", [128, 8], F32)
        self.pcol = A.alloc("pcol", [128, 96], F32)
        self.prow = A.alloc("prow", [1, 1024], BF16)
        self.t2 = A.alloc("t2", [128, 8, 512], F32)
        self.t2b = A.alloc("t2b", [128, 512], BF16)
        self.x2b = A.alloc("x2b", [128, 512], BF16)
        self.srow = A.alloc("srow", [1, 512], BF16)
        self.s2row = A.alloc("s2row", [1, 512], BF16)
        self.negmu_row = A.alloc("negmu_row", [1, 512], BF16)
        self.mu_t = A.alloc("mu_t", [128, 512], F32)
        self.sq_t = A.alloc("sq_t", [128, 512], F32)
        self.rb_t = A.alloc("rb_t", [128, 512], F32)
        self.d1_t = A.alloc("d1_t", [128, 512], F32)
        self.hs_t = A.alloc("hs_t", [128, 512], F32)
        self.xprev = A.alloc("xprev", [128, 512], F32)
        A.mark("phase")

        self.ps = [nc.alloc_psum_tensor(f"psb{i}", [128, 512], F32) for i in range(7)]
        self.psbf = nc.alloc_psum_tensor("psbf", [128, 1024], BF16)
        self.ps_slot = [Slot() for _ in range(8)]

        self.sXB = [[Slot() for _ in range(NT512)] for _ in range(NCHUNK)]
        self.sXL = [[Slot() for _ in range(NT512)] for _ in range(NCHUNK)]
        self.slot = {}
        if "A" in self.phases:
            self.alloc_zone_a()
        if "B" in self.phases or "D" in self.phases:
            self.alloc_zone_b()
        if "C" in self.phases:
            self.alloc_zone_c()

    def S_(self, name):
        if name not in self.slot:
            self.slot[name] = Slot()
        return self.slot[name]

    def xb_ap(self, c, t0, n, shifted=False, lo=False):
        t = self.XL if lo else self.XB
        tile, within = divmod(t0, 512)
        assert within + n <= 512
        off = c * 2052 + tile * 513 + (0 if shifted else 1) + within
        return t[:, off: off + n]

    def prologue(self):
        P = self.P
        toks = []
        for k in self.cs:
            toks.append(P.dma("sync", self.cs[k][:], self.C[k]))
        P.emit("pool", lambda e: e.memset(self.ccols[:, 0:1], LN_EPS))
        P.emit("pool", lambda e: e.memset(self.ccols[:, 1:2], -2.0794415416798357))
        t = P.emit("pool", lambda e: e.memset(self.ccols[:, 2:8], 0.0), inc="sPL")
        P.emit("pool", lambda e: e.memset(
            bass.AP(self.XB, 0, [[8 * 2052, 128], [2052, 8], [513, 4], [1, 1]]), 0.0))
        P.emit("pool", lambda e: e.memset(
            bass.AP(self.XL, 0, [[8 * 2052, 128], [2052, 8], [513, 4], [1, 1]]), 0.0))
        tz = P.emit("pool", lambda e: e.engine_nop(), inc="sPL")
        self.ccols_tok = t
        self.pad_tok = tz
        ctoks = [(f"qX{i}", P.counts[f"qX{i}"]) for i in range(4)]
        for e in ENGINES:
            for ct in ctoks:
                P.wait(e, ct)
            P.wait(e, tz)

    def ln_rowmath(self, stats_bank, bcast_bank, eps_col=None):
        P = self.P
        ps = self.ps[stats_bank]
        pslot = self.ps_slot[stats_bank]
        bb = self.ps[bcast_bank]
        bslot = self.ps_slot[bcast_bank]
        if eps_col is None:
            eps_col = self.ccols[:, 0:1]
        tsr = P.emit("dve", lambda e: e.tensor_copy(self.srow[:], ps[0:1, :]),
                     waits=self.S_("srow").war() + [pslot.ready], inc="sDV")
        self.S_("srow").set(tsr)
        ts2 = P.emit("dve", lambda e: e.tensor_copy(self.s2row[:], ps[32:33, :]),
                     waits=self.S_("s2row").war(), inc="sDV")
        self.S_("s2row").set(ts2)
        pslot.read(ts2)
        tnm = P.emit("dve", lambda e: e.tensor_scalar_mul(
            self.negmu_row[:], self.srow[:], -1.0 / 1024.0),
            waits=self.S_("negmu_row").war() + [tsr], inc="sDV")
        self.S_("negmu_row").set(tnm)
        tba = P.emit("pe", lambda e: e.matmul(
            bb[:, :], self.cs["c_ones_1x128"][:], self.srow[:],
            start=True, stop=True),
            waits=[tsr] + bslot.war(), inc="sPE")
        bslot.set(tba)
        tmu = P.emit("act", lambda e: e.activation(
            self.mu_t[:], bb[:, :], AF.Copy, bias=0.0, scale=1.0 / 1024.0),
            waits=[tba] + self.S_("mu_t").war(), inc="sAC")
        self.S_("mu_t").set(tmu)
        tsq = P.emit("act", lambda e: e.activation(
            self.sq_t[:], bb[:, :], AF.Square, bias=0.0, scale=1.0 / 1024.0),
            waits=[tba] + self.S_("sq_t").war(), inc="sAC")
        self.S_("sq_t").set(tsq)
        bslot.read(tmu)
        bslot.read(tsq)
        tbb = P.emit("pe", lambda e: e.matmul(
            bb[:, :], self.cs["c_ones_1x128"][:], self.s2row[:],
            start=True, stop=True),
            waits=[ts2] + bslot.war(), inc="sPE")
        bslot.set(tbb)
        tv = P.emit("dve", lambda e: e.scalar_tensor_tensor(
            self.sq_t[:], bb[:, :], 1.0 / 1024.0, self.sq_t[:],
            ALU.mult, ALU.subtract),
            waits=[tbb, tsq], inc="sDV")
        self.S_("sq_t").set(tv)
        bslot.read(tv)
        tl = P.emit("act", lambda e: e.activation(
            self.sq_t[:], self.sq_t[:], AF.Ln, bias=eps_col, scale=1.0),
            waits=[tv], inc="sAC")
        self.S_("sq_t").set(tl)
        tr = P.emit("act", lambda e: e.activation(
            self.rb_t[:], self.sq_t[:], AF.Exp, bias=0.0, scale=-0.5),
            waits=[tl] + self.S_("rb_t").war(), inc="sAC")
        self.S_("sq_t").read(tr)
        self.S_("rb_t").set(tr)
        return tmu, tr, tnm

    def stats_mms(self, bank, nc_idx, src_b_ap, src_sq_ap, toks):
        P = self.P
        ps = self.ps[bank]
        first = nc_idx == 0
        t1 = P.emit("pe", lambda e: e.matmul(
            ps[0:36, :], self.cs["c_stI0"][:], src_b_ap,
            start=first, stop=False),
            waits=toks + (self.ps_slot[bank].war() if first else []), inc="sPE")
        if first:
            self.ps_slot[bank].set(t1)
        t2 = P.emit("pe", lambda e: e.matmul(
            ps[0:36, :], self.cs["c_stI32"][:], src_sq_ap,
            start=False, stop=(nc_idx == NCHUNK - 1)),
            waits=toks, inc="sPE")
        self.ps_slot[bank].set(t2)
        return t2


    def stream_split(self, c, t, src_tok):
        P = self.P
        xb = self.xb_ap(c, t * 512, 512)
        xl = self.xb_ap(c, t * 512, 512, lo=True)
        sb, sl = self.sXB[c][t], self.sXL[c][t]
        tb = P.emit("act", lambda e, xb=xb: e.activation(
            xb, self.hs_t[:], AF.Copy, bias=0.0, scale=1.0),
            waits=[src_tok] + sb.war(), inc="sAC")
        if t < NT512 - 1:
            pad = self.XB[:, c * 2052 + (t + 1) * 513: c * 2052 + (t + 1) * 513 + 1]
            tb = P.emit("act", lambda e, pad=pad: e.activation(
                pad, self.hs_t[:, 511:512], AF.Copy, bias=0.0, scale=1.0),
                waits=[src_tok] + self.sXB[c][t + 1].war(), inc="sAC")
            self.sXB[c][t + 1].read(tb)
        sb.set(tb)
        tl = P.emit("pool", lambda e, xb=xb, xl=xl: e.tensor_sub(
            xl, self.hs_t[:], xb),
            waits=[src_tok, tb] + sl.war(), inc="sPL")
        sl.set(tl)
        self.S_("hs_t").read(tl)
        return tb, tl

    def tile_stats(self, c, t2c_ap, t2tok, bank=4):
        P = self.P
        s1 = self.S_("t2b")
        tc = P.emit("act", lambda e, a=t2c_ap: e.activation(
            self.t2b[:], a, AF.Copy, bias=0.0, scale=1.0),
            waits=[t2tok] + s1.war(), inc="sAC")
        s1.set(tc)
        s2 = self.S_("x2b")
        tq = P.emit("act", lambda e, a=t2c_ap: e.activation(
            self.x2b[:], a, AF.Square, bias=0.0, scale=1.0),
            waits=[t2tok] + s2.war(), inc="sAC")
        s2.set(tq)
        tk = self.stats_mms(bank, c, self.t2b[:], self.x2b[:], [tc, tq])
        s1.read(tk)
        s2.read(tk)
        return tk

    def emit_boundary(self, seq, x_prev, x_new, mode):
        P = self.P
        last_out = None
        for t in range(NT512):
            t2tok = [None] * NCHUNK
            for c in range(NCHUNK):
                t2c = self.t2[:, c, :]
                st2 = self.S_(f"t2_{c}")
                xp_ap = x_prev[c * 128:(c + 1) * 128, t * 512:(t + 1) * 512]
                if mode == "init":
                    tok = P.dma("sync", t2c, xp_ap, waits=st2.war())
                else:
                    sxp = self.S_("xprev")
                    tld = P.dma("sync", self.xprev[:], xp_ap, waits=sxp.war())
                    sxp.set(tld)
                    xb = self.xb_ap(c, t * 512, 512)
                    xl = self.xb_ap(c, t * 512, 512, lo=True)
                    sd = self.S_("d1_t")
                    ta = P.emit("dve", lambda e, t2c=t2c, xb=xb, c=c: e.tensor_add(
                        self.d1_t[:], self.xprev[:], xb),
                        waits=[tld, self.sXB[c][t].ready] + sd.war(), inc="sDV")
                    sd.set(ta)
                    sxp.read(ta)
                    self.sXB[c][t].read(ta)
                    tok = P.emit("pool", lambda e, t2c=t2c, xl=xl: e.tensor_add(
                        t2c, self.d1_t[:], xl),
                        waits=[ta, self.sXL[c][t].ready] + st2.war(), inc="sPL")
                    sd.read(tok)
                    self.sXL[c][t].read(tok)
                st2.set(tok)
                if mode != "init":
                    xn_ap = x_new[c * 128:(c + 1) * 128, t * 512:(t + 1) * 512]
                    last_out = P.dma("act", xn_ap, t2c, waits=[tok])
                    st2.read(last_out)
                t2tok[c] = tok
                if mode != "final":
                    tk = self.tile_stats(c, t2c, tok)
                    st2.read(tk)
            if mode == "final":
                continue
            tmu, trb, _ = self.ln_rowmath(4, 5)
            for c in range(NCHUNK):
                t2c = self.t2[:, c, :]
                st2 = self.S_(f"t2_{c}")
                sd = self.S_("d1_t")
                t1 = P.emit("dve", lambda e, t2c=t2c: e.tensor_sub(
                    self.d1_t[:], t2c, self.mu_t[:]),
                    waits=[t2tok[c], tmu] + sd.war(), inc="sDV")
                sd.set(t1)
                st2.read(t1)
                self.S_("mu_t").read(t1)
                sh = self.S_("hs_t")
                t2_ = P.emit("dve", lambda e: e.tensor_mul(
                    self.hs_t[:], self.d1_t[:], self.rb_t[:]),
                    waits=[t1, trb] + sh.war(), inc="sDV")
                sh.set(t2_)
                sd.read(t2_)
                self.S_("rb_t").read(t2_)
                self.stream_split(c, t, t2_)
        return last_out

    def dump_stream(self, name):
        if name not in self.dbg:
            return
        P = self.P
        dst = self.dbg[name]
        for c in range(NCHUNK):
            for t in range(NT512):
                sd = self.S_("d1_t")
                xb = self.xb_ap(c, t * 512, 512)
                xl = self.xb_ap(c, t * 512, 512, lo=True)
                tk = P.emit("dve", lambda e, xb=xb, xl=xl: e.tensor_add(
                    self.d1_t[:], xb, xl),
                    waits=[self.sXB[c][t].ready, self.sXL[c][t].ready] + sd.war(),
                    inc="sDV")
                sd.set(tk)
                self.sXB[c][t].read(tk)
                self.sXL[c][t].read(tk)
                td = P.dma("act", dst[c * 128:(c + 1) * 128, t * 512:(t + 1) * 512],
                           self.d1_t[:], waits=[tk])
                sd.read(td)
                self.last_dbg = td

    def dump_tile(self, name, src_ap, waits, slots=()):
        if name not in self.dbg:
            return
        td = self.P.dma("pool", self.dbg[name][:], src_ap, waits=list(waits))
        for s in slots:
            self.S_(s).read(td)
        self.last_dbg = td


    def build(self):
        P = self.P
        self.last_dbg = None
        self.prologue()
        final_toks = []
        for seq in range(self.nseq):
            xin = self.xT[seq]
            self.emit_boundary(seq, xin, None, "init")
            if seq == 0:
                self.dump_stream("a_in")
            if "A" in self.phases:
                self.phase_attn_a(seq)
                if seq == 0:
                    self.dump_stream("a_out")
            self.emit_boundary(seq, xin, self.xr[0], "stream")
            if seq == 0:
                self.dump_stream("b_in")
            if "B" in self.phases:
                self.phase_ffn(seq, first=True)
            self.emit_boundary(seq, self.xr[0], self.xr[1], "stream")
            if seq == 0:
                self.dump_stream("c_in")
            if "C" in self.phases:
                self.phase_attn_b(seq)
            else:
                self.emit_boundary(seq, self.xr[1], self.xr[0], "stream")
            if seq == 0:
                self.dump_stream("d_in")
            if "D" in self.phases:
                self.phase_ffn(seq, first=False)
            tk = self.emit_boundary(seq, self.xr[0], self.outT[seq], "final")
            final_toks.append(tk)
        P.wait("act", final_toks[-1])
        if self.last_dbg is not None:
            P.wait("act", self.last_dbg)

    def barrier(self):
        toks = [(s, v) for s, v in self.P.counts.items()]
        for e in ENGINES:
            for t in toks:
                self.P.wait(e, t)

    def alloc_zone_a(self):
        A = self.arena
        A.reset_to("phase")
        z = {}
        z["wv"] = A.alloc("za_wv", [128, 8, 1024], BF16)
        z["wo"] = A.alloc("za_wo", [128, 8, 1024], BF16)
        z["slab"] = A.alloc("za_slab", [128, 2, 8, 128], BF16)
        z["bdiag"] = A.alloc("za_bdiag", [128, 128], BF16)
        z["omega"] = A.alloc("za_omega", [64, 32], F32)
        z["qfm"] = A.alloc("za_qfm", [128, 2, 1024], BF16)
        z["kfm"] = A.alloc("za_kfm", [128, 2, 1024], BF16)
        z["Qf"] = A.alloc("za_Qf", [128, 8, 1024], BF16)
        z["Kf"] = A.alloc("za_Kf", [128, 8, 1024], BF16)
        z["vaug"] = A.alloc("za_vaug", [128, 2, 16, 65], BF16)
        z["abuf"] = A.alloc("za_abuf", [128, 8, 1024], BF16)
        z["negh"] = A.alloc("za_negh", [1, 512], BF16)
        z["ktm"] = A.alloc("za_ktm", [128, 128], BF16)
        z["state"] = A.alloc("za_state", [128, 65], BF16)
        z["state_sb"] = A.alloc("za_state_sb", [128, 8, 65], F32)
        z["rec"] = A.alloc("za_rec", [1, 256], BF16)
        z["rb"] = A.alloc("za_rb", [64, 256], BF16)
        z["atsb"] = A.alloc("za_atsb", [128, 256], BF16)
        attn_end = A.off
        A.reset_to("phase")
        z["fslab"] = A.alloc("zaf_slab", [128, 2, 8, 128], BF16)
        z["fslab2"] = A.alloc("zaf_slab2", [128, 2, 32, 128], BF16)
        z["hgr"] = A.alloc("zaf_hgr", [128, 32, 512], BF16)
        A.off = max(A.off, attn_end)
        self.za = z

    def alloc_zone_b(self):
        A = self.arena
        A.reset_to("phase")
        z = {}
        z["wqkv"] = A.alloc("zb_wqkv", [128, 8, 3072], BF16)
        A.mark("bpool")
        z["rot"] = A.alloc("zb_rot", [128, 4, 16, 64], F32)
        z["kroll"] = A.alloc("zb_kroll", [128, 8, 3, 128], BF16)
        z["vroll"] = A.alloc("zb_vroll", [128, 3, 16, 65], BF16)
        z["qcur"] = A.alloc("zb_qcur", [128, 8, 128], BF16)
        z["qhat"] = A.alloc("zb_qhat", [128, 2048], BF16)
        z["sq"] = A.alloc("zb_sq", [128, 1024], BF16)
        z["rsc1"] = A.alloc("zb_rsc1", [128, 1024], F32)
        z["attout"] = A.alloc("zb_attout", [128, 8, 512], BF16)
        z["pt"] = A.alloc("zb_pt", [128, 2, 384], BF16)
        z["sm"] = A.alloc("zb_sm", [128, 768], BF16)
        z["norms"] = A.alloc("zb_norms", [128, 32], F32)
        z["rstd"] = A.alloc("zb_rstd", [128, 32], F32)
        z["mcolf"] = A.alloc("zb_mcolf", [128, 2], F32)
        z["mcol"] = A.alloc("zb_mcol", [128, 2], BF16)
        z["qstile"] = A.alloc("zb_qstile", [128, 128], BF16)
        z["woslab"] = A.alloc("zb_woslab", [128, 2, 8, 128], BF16)
        attn_end = A.off
        A.reset_to("bpool")
        z["gg"] = A.alloc("zbf_gg", [128, 22, 512], BF16)
        z["w1slab"] = A.alloc("zbf_w1slab", [128, 2, 8, 256], BF16)
        z["w2slab"] = A.alloc("zbf_w2slab", [128, 2, 22, 128], BF16)
        z["xg"] = A.alloc("zbf_xg", [128, 8, 512], BF16)
        z["cb2"] = A.alloc("zbf_cb", [128, 44], F32)
        A.off = max(A.off, attn_end)
        self.zb = z

    def alloc_zone_c(self):
        A = self.arena
        A.reset_to("phase")
        z = {}
        z["w1"] = A.alloc("zc_w1", [128, 8, 1024], BF16)
        off_w2 = A.off
        z["w2"] = A.alloc("zc_w2", [128, 8, 1024], BF16)
        z["K"] = A.alloc("zc_K", [128, 8, 2048], BF16)
        z["V"] = A.alloc("zc_V", [128, 16, 16, 65], BF16)
        z["Q"] = A.alloc("zc_Q", [128, 8, 128], BF16)
        z["ao"] = A.alloc("zc_ao", [128, 8, 128], BF16)
        z["pt"] = A.alloc("zc_pt", [128, 2, 128], BF16)
        z["m"] = A.alloc("zc_m", [128, 4], F32)
        z["mb"] = A.alloc("zc_mb", [128, 2], BF16)
        z["xp"] = A.alloc("zc_xp", [128, 128], F32)
        end = A.off
        A.off = off_w2
        z["woslab"] = A.alloc("zc_woslab", [128, 2, 8, 128], BF16)
        A.off = max(A.off, end)
        self.zc = z


    def phase_attn_a(self, seq):
        for l in range(3):
            self.perf_layer(seq, l)

    def perf_layer(self, seq, l):
        P = self.P
        z = self.za
        self.barrier()
        twv = P.dma("pool", z["wv"][:], self.W["aWv"][l])
        two = P.dma("pool", z["wo"][:], self.W["aWo"][l])
        tbv = P.dma("pool", self.prow[:], self.W["abv_row"][l])
        tom = P.dma("sync", z["omega"][:], self.W["aomega"][l])
        pc = []
        for i, nm in enumerate(("abq", "abk", "abo", "aln1g", "aln1b",
                                "aln2g", "aln2b", "abf2")):
            pc.append(P.dma("sync", self.pcol[:, 8 * i: 8 * i + 8], self.W[nm][l]))
        tbf1 = P.dma("sync", self.pcol[:, 64:96], self.W["abf1"][l])
        tpc = pc[-1]
        self.ptoks = pc + [tbf1]
        sc = 0.3535533905932738
        tm0 = P.emit("pool", lambda e: e.memset(z["bdiag"][:], 0.0),
                     waits=[], inc="sPL")
        tb1 = P.emit("act", lambda e: e.activation(
            z["bdiag"][0:64, 0:32], z["omega"][:], AF.Copy, bias=0.0, scale=sc),
            waits=[tom, tm0], inc="sAC")
        tb2 = P.emit("act", lambda e: e.activation(
            z["bdiag"][0:64, 32:64], z["omega"][:], AF.Copy, bias=0.0, scale=-sc),
            waits=[tom, tm0], inc="sAC")
        tb3 = P.emit("act", lambda e: e.activation(
            z["bdiag"][64:128, 64:96], z["omega"][:], AF.Copy, bias=0.0, scale=sc),
            waits=[tom, tm0], inc="sAC")
        tb4 = P.emit("act", lambda e: e.activation(
            z["bdiag"][64:128, 96:128], z["omega"][:], AF.Copy, bias=0.0, scale=-sc),
            waits=[tom, tm0], inc="sAC")
        tbd = tb4
        tvo = P.emit("pool", lambda e: e.memset(
            bass.AP(z["vaug"], 64, [[2080, 128], [65, 2 * 16], [1, 1]]), 1.0),
            inc="sPL")
        tsz = P.emit("pool", lambda e: e.memset(z["state_sb"][:], 0.0),
                     waits=self.S_("a_state_sb").war(), inc="sPL")
        self.S_("a_state_sb").set(tsz)

        for half in range(2):
            hb = half * 1024
            for proj, wname, bcol0, fm, Ff in (
                    ("q", "aWq", 0, z["qfm"], z["Qf"]),
                    ("k", "aWk", 8, z["kfm"], z["Kf"])):
                for ncx in range(NCHUNK):
                    sl = ncx % 2
                    slab = z["slab"][:, sl, :, :]
                    ssl = self.S_(f"a_slab{sl}")
                    tw = P.dma("pool", slab, self.W[wname][l, ncx], waits=ssl.war())
                    ssl.set(tw)
                    for t2 in range(2):
                        bank = 4 + t2
                        ps = self.ps[bank]
                        pslot = self.ps_slot[bank]
                        mmtok = None
                        for f in range(NCHUNK):
                            first, last = f == 0, f == NCHUNK - 1
                            rhs = self.xb_ap(f, hb + t2 * 512, 512)
                            w = [tw, self.sXB[f][2 * half + t2].ready]
                            if first:
                                w += pslot.war()
                            mmtok = P.emit("pe", lambda e, ps=ps, slab=slab, f=f,
                                           rhs=rhs, first=first, last=last: e.matmul(
                                ps[:, :], slab[:, f, :], rhs,
                                start=first, stop=last),
                                waits=w, inc="sPE" if last else None)
                            if first:
                                pslot.set(mmtok)
                        pslot.set(mmtok)
                        ssl.read(mmtok)
                        for f in range(NCHUNK):
                            self.sXB[f][2 * half + t2].read(mmtok)
                        sfm = self.S_(f"a_fm_{proj}{ncx % 2}")
                        dst = fm[:, ncx % 2, t2 * 512:(t2 + 1) * 512]
                        te = P.emit("act", lambda e, dst=dst, ps=ps, ncx=ncx,
                                    bcol0=bcol0: e.activation(
                            dst, ps[:, :], AF.Identity,
                            bias=self.pcol[:, bcol0 + ncx: bcol0 + ncx + 1],
                            scale=1.0),
                            waits=[mmtok, tpc] + sfm.war(), inc="sAC")
                        sfm.set(te)
                        pslot.read(te)
                        sq2 = self.S_("x2b")
                        tq2 = P.emit("act", lambda e, dst=dst: e.activation(
                            self.x2b[:], dst, AF.Square, bias=0.0, scale=1.0),
                            waits=[te] + sq2.war(), inc="sAC")
                        sq2.set(tq2)
                        ps6 = self.ps[6]
                        ps6s = self.ps_slot[6]
                        th = P.emit("pe", lambda e: e.matmul(
                            ps6[0:33, 0:512], self.cs["c_favor_pat"][:],
                            self.x2b[:], start=True, stop=True),
                            waits=[tq2] + ps6s.war(), inc="sPE")
                        ps6s.set(th)
                        sq2.read(th)
                        snh = self.S_("a_negh")
                        tnA = P.emit("dve", lambda e: e.tensor_copy(
                            self.za["negh"][:], ps6[0:1, 0:512]),
                            waits=[th] + snh.war(), inc="sDV")
                        snh.set(tnA)
                        tu0 = P.emit("pe", lambda e, ps=ps, dst=dst: e.matmul(
                            ps[:, :], self.za["bdiag"][:], dst,
                            start=True, stop=False),
                            waits=[te, tbd] + pslot.war(), inc=None)
                        tu1 = P.emit("pe", lambda e, ps=ps: e.matmul(
                            ps[:, :], self.cs["c_indA"][:],
                            self.za["negh"][:], start=False, stop=False),
                            waits=[tnA], inc="sPE")
                        snh.read(tu1)
                        tnB = P.emit("dve", lambda e: e.tensor_copy(
                            self.za["negh"][:], ps6[32:33, 0:512]),
                            waits=[th, tu1] + snh.war(), inc="sDV")
                        snh.set(tnB)
                        ps6s.read(tnB)
                        tu2 = P.emit("pe", lambda e, ps=ps: e.matmul(
                            ps[:, :], self.cs["c_indB"][:],
                            self.za["negh"][:], start=False, stop=True),
                            waits=[tnB], inc="sPE")
                        pslot.set(tu2)
                        snh.read(tu2)
                        sF = self.S_(f"a_F_{proj}{ncx}")
                        tF = P.emit("act", lambda e, ps=ps, Ff=Ff, ncx=ncx,
                                    t2=t2: e.activation(
                            Ff[:, ncx, t2 * 512:(t2 + 1) * 512], ps[:, :], AF.Exp,
                            bias=self.ccols[:, 1:2], scale=1.0),
                            waits=[tu2, self.ccols_tok] + sF.war(), inc="sAC")
                        sF.set(tF)
                        pslot.read(tF)
                        sfm.read(tq2)
                        sfm.read(tu2)
            self.perf_scan_half(seq, l, half, twv, tbv, tvo)
            for t2 in range(2):
                self.perf_wo_ln1(seq, l, half, t2, two)
        self.barrier()
        for t in range(NT512):
            self.perf_ffn_tile(seq, l, t)

    def perf_scan_half(self, seq, l, half, twv, tbv, tvo):
        P = self.P
        z = self.za
        hb = half * 1024
        ps4, ps5, ps6 = self.ps[4], self.ps[5], self.ps[6]
        s4, s5, s6 = self.ps_slot[4], self.ps_slot[5], self.ps_slot[6]
        psbf, sbf = self.psbf, self.ps_slot[7]
        for c in range(8):
            ts_abs = 8 * half + c
            vslot = c % 2
            sv = self.S_(f"a_vaug{vslot}")
            for nh in range(2):
                mmtok = None
                for f in range(NCHUNK):
                    first = f == 0
                    lhs = self.xb_ap(f, hb + c * 128, 128)
                    w = [twv, self.sXB[f][ts_abs // 4].ready]
                    if first:
                        w += s6.war()
                    mmtok = P.emit("pe", lambda e, f=f, lhs=lhs, nh=nh: e.matmul(
                        ps6[:, 0:512], lhs, z["wv"][:, f, nh * 512:(nh + 1) * 512],
                        start=f == 0, stop=False), waits=w, inc=None)
                    if first:
                        s6.set(mmtok)
                    self.sXB[f][ts_abs // 4].read(mmtok)
                tb = P.emit("pe", lambda e, nh=nh: e.matmul(
                    ps6[:, 0:512], self.cs["c_ones_1x128"][:],
                    self.prow[0:1, nh * 512:(nh + 1) * 512],
                    start=False, stop=True), waits=[tbv], inc="sPE")
                s6.set(tb)
                for f in range(NCHUNK):
                    self.sXB[f][ts_abs // 4].read(tb)
                te = P.emit("dve", lambda e, nh=nh, vslot=vslot: e.tensor_copy(
                    z["vaug"][:, vslot, nh * 8:(nh + 1) * 8, 0:64],
                    ps6[:, 0:512].rearrange("p (h d) -> p h d", h=8)),
                    waits=[tb, tvo] + (sv.war() if nh == 0 else []), inc="sDV")
                if nh == 0:
                    sv.set(te)
                s6.read(te)
            sv.set(te)
            for hp in range(8):
                sQ = self.S_(f"a_F_q{hp}")
                sK = self.S_(f"a_F_k{hp}")
                qA = z["Qf"][0:64, hp, c * 128:(c + 1) * 128]
                qB = z["Qf"][64:128, hp, c * 128:(c + 1) * 128]
                kA = z["Kf"][0:64, hp, c * 128:(c + 1) * 128]
                kB = z["Kf"][64:128, hp, c * 128:(c + 1) * 128]
                first_chunk = (half == 0 and c == 0)
                tt_ = P.emit("pe", lambda e, hp=hp, c=c: e.transpose(
                    psbf[:, hp * 128:(hp + 1) * 128],
                    z["Kf"][:, hp, c * 128:(c + 1) * 128], self.cs["c_ident"][:]),
                    waits=[sK.ready] + sbf.war(), inc="sPE")
                sbf.set(tt_)
                sK.read(tt_)
                sktm = self.S_("a_ktm")
                tk = P.emit("dve", lambda e, hp=hp: e.tensor_copy(
                    z["ktm"][:], psbf[:, hp * 128:(hp + 1) * 128]),
                    waits=[tt_] + sktm.war(), inc="sDV")
                sktm.set(tk)
                sbf.read(tk)
                sst = self.S_("a_state")
                ssb = self.S_("a_state_sb")
                if not first_chunk:
                    tse = P.emit("dve", lambda e, hp=hp: e.tensor_copy(
                        z["state"][:], z["state_sb"][:, hp, :]),
                        waits=[ssb.ready] + sst.war(), inc="sDV")
                    sst.set(tse)
                    ssb.read(tse)
                su1 = P.emit("pe", lambda e, vslot=vslot, hp=hp: e.matmul(
                    ps5[0:64, 256:321], z["ktm"][:, 0:64],
                    z["vaug"][:, vslot, 2 * hp, :],
                    start=True, stop=True),
                    waits=[tk, sv.ready] + s5.war(), inc=None)
                su2 = P.emit("pe", lambda e, vslot=vslot, hp=hp: e.matmul(
                    ps5[0:64, 321:386], z["ktm"][:, 64:128],
                    z["vaug"][:, vslot, 2 * hp + 1, :],
                    start=True, stop=True),
                    waits=[], inc="sPE")
                s5.set(su2)
                sktm.read(su2)
                sv.read(su2)
                tua = P.emit("dve", lambda e, hp=hp: e.tensor_add(
                    z["state_sb"][0:64, hp, :], z["state_sb"][0:64, hp, :],
                    ps5[0:64, 256:321]),
                    waits=[su2] + ([sst.ready] if not first_chunk else []) +
                          ssb.war(), inc="sDV")
                tub = P.emit("dve", lambda e, hp=hp: e.tensor_add(
                    z["state_sb"][64:128, hp, :], z["state_sb"][64:128, hp, :],
                    ps5[0:64, 321:386]),
                    waits=[su2], inc="sDV")
                ssb.set(tub)
                s5.read(tub)
                ta1 = P.emit("pe", lambda e, kA=kA, qA=qA: e.matmul(
                    ps4[:, 0:128], kA, qA, start=True, stop=True,
                    tile_position=(0, 0)),
                    waits=[sK.ready, sQ.ready] + s4.war(), inc="sPE")
                s4.set(ta1)
                ta2 = P.emit("pe", lambda e, kB=kB, qB=qB: e.matmul(
                    ps5[:, 0:128], kB, qB, start=True, stop=True,
                    tile_position=(64, 0)),
                    waits=[sK.ready, sQ.ready] + s5.war(), inc="sPE")
                s5.set(ta2)
                sK.read(ta1); sK.read(ta2); sQ.read(ta1); sQ.read(ta2)
                sat = self.S_("a_atsb")
                tm1 = P.emit("dve", lambda e: e.tensor_mul(
                    z["atsb"][:, 0:128], ps4[:, 0:128], self.cs["c_maskT_diag"][:]),
                    waits=[ta1] + sat.war(), inc="sDV")
                sat.set(tm1)
                s4.read(tm1)
                tm2 = P.emit("dve", lambda e: e.tensor_mul(
                    z["atsb"][:, 128:256], ps5[:, 0:128], self.cs["c_maskT_diag"][:]),
                    waits=[ta2], inc="sDV")
                sat.set(tm2)
                s5.read(tm2)
                tfin = None
                for hh in range(2):
                    c0 = 256 + 128 * hh
                    vh = 2 * hp + hh
                    at0 = 128 * hh
                    qh = qA if hh == 0 else qB
                    last = (hh == 1) and first_chunk
                    tfin = P.emit("pe", lambda e, c0=c0, vh=vh, at0=at0,
                                  vslot=vslot, fc=first_chunk: e.matmul(
                        ps4[0:65, c0:c0 + 128], z["vaug"][:, vslot, vh, :],
                        z["atsb"][:, at0:at0 + 128],
                        start=True, stop=fc),
                        waits=[tm1, tm2], inc="sPE" if last else None)
                    if not first_chunk:
                        last = (hh == 1)
                        tfin = P.emit("pe", lambda e, c0=c0, hh=hh, qh=qh: e.matmul(
                            ps4[0:65, c0:c0 + 128],
                            z["state"][64 * hh:64 * hh + 64, :],
                            qh, start=False, stop=True),
                            waits=[sst.ready], inc="sPE" if last else None)
                sv.read(tfin)
                sat.read(tfin)
                sQ.read(tfin)
                if not first_chunk:
                    sst.read(tfin)
                s4.set(tfin)
                sden = self.S_("sq_t")
                td = P.emit("dve", lambda e: e.tensor_scalar_add(
                    self.sq_t[0:1, 0:256], ps4[64:65, 256:512], 1e-6),
                    waits=[tfin] + sden.war(), inc="sDV")
                sden.set(td)
                s4.read(td)
                trc0 = P.emit("dve", lambda e: e.reciprocal(
                    self.sq_t[0:1, 256:512], self.sq_t[0:1, 0:256]),
                    waits=[td], inc="sDV")
                sden.set(trc0)
                srec = self.S_("a_rec")
                trc = P.emit("dve", lambda e: e.tensor_copy(
                    z["rec"][:], self.sq_t[0:1, 256:512]),
                    waits=[trc0] + srec.war(), inc="sDV")
                srec.set(trc)
                sden.read(trc)
                trb = P.emit("pe", lambda e: e.matmul(
                    ps5[0:64, 128:384], self.cs["c_ones_1x128"][0:1, 0:64],
                    z["rec"][:], start=True, stop=True),
                    waits=[trc] + s5.war(), inc="sPE")
                s5.set(trb)
                srec.read(trb)
                srb = self.S_("a_rb")
                trbe = P.emit("dve", lambda e: e.tensor_copy(
                    z["rb"][:], ps5[0:64, 128:384]),
                    waits=[trb] + srb.war(), inc="sDV")
                srb.set(trbe)
                s5.read(trbe)
                sab = self.S_(f"a_abuf{hp}")
                tf1 = P.emit("dve", lambda e, hp=hp, c=c: e.tensor_mul(
                    z["abuf"][0:64, hp, c * 128:(c + 1) * 128],
                    ps4[0:64, 256:384], z["rb"][0:64, 0:128]),
                    waits=[trbe] + sab.war(), inc="sDV")
                s4.read(tf1)
                tf2 = P.emit("dve", lambda e, hp=hp, c=c: e.tensor_mul(
                    z["abuf"][64:128, hp, c * 128:(c + 1) * 128],
                    ps4[0:64, 384:512], z["rb"][0:64, 128:256]),
                    waits=[trbe], inc="sDV")
                s4.read(tf2)
                srb.read(tf2)
                sab.set(tf2)

    def perf_wo_ln1(self, seq, l, half, t2, two):
        P = self.P
        z = self.za
        t_abs = 2 * half + t2
        ps4, s4 = self.ps[4], self.ps_slot[4]
        for ncx in range(NCHUNK):
            mmtok = None
            for f in range(NCHUNK):
                first, last = f == 0, f == NCHUNK - 1
                w = [two, self.S_(f"a_abuf{f}").ready]
                if first:
                    w += s4.war()
                mmtok = P.emit("pe", lambda e, f=f, ncx=ncx, t2=t2: e.matmul(
                    ps4[:, 0:512], z["wo"][:, f, ncx * 128:(ncx + 1) * 128],
                    z["abuf"][:, f, t2 * 512:(t2 + 1) * 512],
                    start=f == 0, stop=f == NCHUNK - 1),
                    waits=w, inc="sPE" if last else None)
                if first:
                    s4.set(mmtok)
                self.S_(f"a_abuf{f}").read(mmtok)
            s4.set(mmtok)
            sd = self.S_("d1_t")
            xb = self.xb_ap(ncx, t_abs * 512, 512)
            xl = self.xb_ap(ncx, t_abs * 512, 512, lo=True)
            p1 = P.emit("dve", lambda e, ncx=ncx, xb=xb: e.scalar_tensor_tensor(
                self.d1_t[:], ps4[:, 0:512], self.pcol[:, 16 + ncx:17 + ncx],
                xb, ALU.add, ALU.add),
                waits=[mmtok, self.sXB[ncx][t_abs].ready] + self.ptoks + sd.war(),
                inc="sDV")
            sd.set(p1)
            s4.read(p1)
            self.sXB[ncx][t_abs].read(p1)
            st2 = self.S_(f"t2_{ncx}")
            p2 = P.emit("pool", lambda e, ncx=ncx, xl=xl: e.tensor_add(
                self.t2[:, ncx, :], self.d1_t[:], xl),
                waits=[p1, self.sXL[ncx][t_abs].ready] + st2.war(), inc="sPL")
            st2.set(p2)
            sd.read(p2)
            self.sXL[ncx][t_abs].read(p2)
            tk = self.tile_stats(ncx, self.t2[:, ncx, :], p2, bank=6)
            st2.read(tk)
        tmu, trb, _ = self.ln_rowmath(6, 5)
        for ncx in range(NCHUNK):
            st2 = self.S_(f"t2_{ncx}")
            sd = self.S_("d1_t")
            ta = P.emit("dve", lambda e, ncx=ncx: e.tensor_sub(
                self.d1_t[:], self.t2[:, ncx, :], self.mu_t[:]),
                waits=[st2.ready, tmu] + sd.war(), inc="sDV")
            sd.set(ta)
            st2.read(ta)
            self.S_("mu_t").read(ta)
            sh = self.S_("hs_t")
            tb = P.emit("dve", lambda e: e.tensor_mul(
                self.hs_t[:], self.d1_t[:], self.rb_t[:]),
                waits=[ta, trb] + sh.war(), inc="sDV")
            sd.read(tb)
            self.S_("rb_t").read(tb)
            tc = P.emit("pool", lambda e, ncx=ncx: e.tensor_scalar(
                self.hs_t[:], self.hs_t[:], self.pcol[:, 24 + ncx:25 + ncx],
                self.pcol[:, 32 + ncx:33 + ncx], ALU.mult, ALU.add),
                waits=[tb] + self.ptoks, inc="sPL")
            sh.set(tc)
            self.stream_split(ncx, t_abs, tc)

    def perf_ffn_tile(self, seq, l, t):
        P = self.P
        z = self.za
        for g in range(32):
            sl = g % 2
            slab = z["fslab"][:, sl, :, :]
            ssl = self.S_(f"af1_slab{sl}")
            tw = P.dma("pool", slab, self.W["aW1"][l, g], waits=ssl.war())
            ssl.set(tw)
            bank = g % 2
            ps, pslot = self.ps[bank], self.ps_slot[bank]
            mmtok = None
            for f in range(NCHUNK):
                first, last = f == 0, f == NCHUNK - 1
                rhs = self.xb_ap(f, t * 512, 512)
                w = [tw, self.sXB[f][t].ready]
                if first:
                    w += pslot.war()
                mmtok = P.emit("pe", lambda e, slab=slab, f=f, rhs=rhs,
                               ps=ps: e.matmul(
                    ps[:, 0:512], slab[:, f, :], rhs,
                    start=f == 0, stop=f == NCHUNK - 1),
                    waits=w, inc="sPE" if last else None)
                if first:
                    pslot.set(mmtok)
                self.sXB[f][t].read(mmtok)
            pslot.set(mmtok)
            ssl.read(mmtok)
            sh = self.S_(f"a_hgr{g}")
            te = P.emit("act", lambda e, g=g, ps=ps: e.activation(
                z["hgr"][:, g, :], ps[:, 0:512], AF.Relu,
                bias=self.pcol[:, 64 + g:65 + g], scale=1.0),
                waits=[mmtok] + self.ptoks + sh.war(), inc="sAC")
            sh.set(te)
            pslot.read(te)
        for ncx in range(NCHUNK):
            sl = ncx % 2
            slab2 = z["fslab2"][:, sl, :, :]
            ssl = self.S_(f"af2_slab{sl}")
            tw = P.dma("pool", slab2, self.W["aW2"][l, ncx], waits=ssl.war())
            ssl.set(tw)
            bank = 2 + ncx % 2
            ps, pslot = self.ps[bank], self.ps_slot[bank]
            mmtok = None
            for g in range(32):
                first, last = g == 0, g == 31
                w = [tw, self.S_(f"a_hgr{g}").ready]
                if first:
                    w += pslot.war()
                mmtok = P.emit("pe", lambda e, slab2=slab2, g=g, ps=ps: e.matmul(
                    ps[:, 0:512], slab2[:, g, :], z["hgr"][:, g, :],
                    start=g == 0, stop=g == 31),
                    waits=w, inc="sPE" if last else None)
                if first:
                    pslot.set(mmtok)
                self.S_(f"a_hgr{g}").read(mmtok)
            pslot.set(mmtok)
            ssl.read(mmtok)
            sd = self.S_("d1_t")
            xb = self.xb_ap(ncx, t * 512, 512)
            xl = self.xb_ap(ncx, t * 512, 512, lo=True)
            p1 = P.emit("dve", lambda e, ncx=ncx, xb=xb, ps=ps: e.scalar_tensor_tensor(
                self.d1_t[:], ps[:, 0:512], self.pcol[:, 56 + ncx:57 + ncx],
                xb, ALU.add, ALU.add),
                waits=[mmtok, self.sXB[ncx][t].ready] + self.ptoks + sd.war(),
                inc="sDV")
            sd.set(p1)
            pslot.read(p1)
            self.sXB[ncx][t].read(p1)
            st2 = self.S_(f"t2_{ncx}")
            p2 = P.emit("pool", lambda e, ncx=ncx, xl=xl: e.tensor_add(
                self.t2[:, ncx, :], self.d1_t[:], xl),
                waits=[p1, self.sXL[ncx][t].ready] + st2.war(), inc="sPL")
            st2.set(p2)
            sd.read(p2)
            self.sXL[ncx][t].read(p2)
            tk = self.tile_stats(ncx, self.t2[:, ncx, :], p2, bank=6)
            st2.read(tk)
        tmu, trb, _ = self.ln_rowmath(6, 5)
        for ncx in range(NCHUNK):
            st2 = self.S_(f"t2_{ncx}")
            sd = self.S_("d1_t")
            ta = P.emit("dve", lambda e, ncx=ncx: e.tensor_sub(
                self.d1_t[:], self.t2[:, ncx, :], self.mu_t[:]),
                waits=[st2.ready, tmu] + sd.war(), inc="sDV")
            sd.set(ta)
            st2.read(ta)
            self.S_("mu_t").read(ta)
            sh = self.S_("hs_t")
            tb = P.emit("dve", lambda e: e.tensor_mul(
                self.hs_t[:], self.d1_t[:], self.rb_t[:]),
                waits=[ta, trb] + sh.war(), inc="sDV")
            sd.read(tb)
            self.S_("rb_t").read(tb)
            tc = P.emit("pool", lambda e, ncx=ncx: e.tensor_scalar(
                self.hs_t[:], self.hs_t[:], self.pcol[:, 40 + ncx:41 + ncx],
                self.pcol[:, 48 + ncx:49 + ncx], ALU.mult, ALU.add),
                waits=[tb] + self.ptoks, inc="sPL")
            sh.set(tc)
            self.stream_split(ncx, t, tc)

    def phase_ffn(self, seq, first):
        for l in range(6):
            self.ffn_layer(seq, l)

    def ffn_layer(self, seq, l):
        P = self.P
        z = self.zb
        self.barrier()
        wq_toks = []
        for f in range(NCHUNK):
            wq_toks.append(P.dma("pool", z["wqkv"][:, f, :],
                                 self.W["fWqkv"][l, :, f, :]))
        rot_toks = [P.dma("sync", z["rot"][:, i, :, :], self.C[nm])
                    for i, nm in enumerate(("c_rot_cs_q", "c_rot_sn_q",
                                            "c_rot_cs_k", "c_rot_sn_k"))]
        pc = [P.dma("sync", self.pcol[:, 0:44], self.W["fb1"][l]),
              P.dma("sync", self.pcol[:, 44:52], self.W["fb2"][l]),
              P.dma("sync", self.pcol[:, 52:60], self.W["flng_col"][l]),
              P.dma("sync", self.pcol[:, 60:68], self.W["flnb_col"][l])]
        tgr = P.dma("pool", self.prow[:, 0:1024], self.W["flng_row"][l])
        tlb = P.dma("pool", self.t2b[:, 0:8], self.W["flnb_col"][l])
        self.S_("t2b").set(tlb)
        self.ptoks = pc + [tgr, tlb]
        tq1 = P.dma("pool", self.x2b[0:1, 0:64], self.W["fqs_row"][l])
        tq2 = P.dma("pool", self.x2b[0:1, 64:128], self.W["fks_row"][l])
        ps6, s6 = self.ps[6], self.ps_slot[6]
        tb1 = P.emit("pe", lambda e: e.matmul(
            ps6[:, 0:128], self.cs["c_ones_1x128"][:], self.x2b[0:1, 0:128],
            start=True, stop=True), waits=[tq1, tq2] + s6.war(), inc="sPE")
        s6.set(tb1)
        sqs = self.S_("b_qstile")
        te1 = P.emit("dve", lambda e: e.tensor_copy(z["qstile"][:], ps6[:, 0:128]),
                     waits=[tb1] + sqs.war(), inc="sDV")
        sqs.set(te1)
        s6.read(te1)
        srs = self.S_("b_rsc1")
        tsh = None
        for half0 in (0, 1):
            for j0 in (0, 32):
                src_c = half0 * 64 + ((j0 + 32) % 64)
                dst_c = half0 * 64 + j0
                tsh = P.emit("pool", lambda e, src_c=src_c, dst_c=dst_c: e.tensor_copy(
                    self.zb["rsc1"][:, dst_c:dst_c + 32],
                    z["qstile"][:, src_c:src_c + 32]),
                    waits=[te1] + (srs.war() if (half0 == 0 and j0 == 0) else []),
                    inc="sPL")
        srs.set(tsh)
        sqs.read(tsh)
        srot = self.S_("b_rot")
        trot = None
        for i, (col0, from_sh) in enumerate(((0, False), (0, True),
                                             (64, False), (64, True))):
            srctile = self.zb["rsc1"] if from_sh else z["qstile"]
            trot = P.emit("pool", lambda e, i=i, col0=col0, srctile=srctile: e.tensor_tensor(
                z["rot"][:, i, :, :], z["rot"][:, i, :, :],
                bass.AP(srctile, col0, [[srctile.shape[1], 128], [0, 16], [1, 64]]),
                ALU.mult),
                waits=[rot_toks[i], te1, tsh] + (srot.war() if i == 0 else []),
                inc="sPL")
        srot.set(trot)
        srs.read(trot)
        sqs.read(trot)
        tvo = P.emit("pool", lambda e: e.memset(
            bass.AP(z["vroll"], 64, [[3120, 128], [65, 3 * 16], [1, 1]]), 1.0),
            waits=self.S_("b_vroll").war(), inc="sPL")
        self.S_("b_vroll").set(tvo)
        tsm0 = P.emit("pool", lambda e: e.memset(z["sm"][:], 0.0),
                      waits=self.S_("b_sm").war(), inc="sPL")
        self.S_("b_sm").set(tsm0)
        tpt0 = P.emit("pool", lambda e: e.memset(z["pt"][:], 0.0),
                      waits=self.S_("b_pt").war(), inc="sPL")
        self.S_("b_pt").set(tpt0)

        for ts in range(NTSUB):
            cb = None
            if ts % 4 == 0 and ts > 0:
                cb = (lambda t=ts // 4 - 1: self.ffn_wo_resid(seq, l, t))
            self.ffn_attn_tsub(seq, l, ts, wq_toks, trot, mid_cb=cb)
        self.ffn_wo_resid(seq, l, 3)
        if seq == 0 and l == 0:
            self.dump_stream("b_l0a")
        self.barrier()
        for t in range(NT512 - 1, -1, -1):
            self.ffn_geglu_tile(seq, l, t)
        if seq == 0 and l == 0:
            self.dump_stream("b_l0")

    def phase_attn_b(self, seq):
        P = self.P
        z = self.zc
        self.barrier()
        wk_toks = [P.dma("pool", z["w1"][:, f, :], self.W["bWk"][:, f, :])
                   for f in range(NCHUNK)]
        wv_toks = [P.dma("pool", z["w2"][:, f, :], self.W["bWv"][:, f, :])
                   for f in range(NCHUNK)]
        sK = self.S_("c_K")
        for nc in range(NCHUNK):
            for t in range(NT512):
                bank = (nc * NT512 + t) % 2
                ps, pslot = self.ps[bank], self.ps_slot[bank]
                mmtok = None
                for f in range(NCHUNK):
                    w = [wk_toks[f], self.sXB[f][t].ready]
                    if f == 0:
                        w += pslot.war()
                    mmtok = P.emit("pe", lambda e, f=f, nc=nc, t=t, ps=ps: e.matmul(
                        ps[:, 0:512], z["w1"][:, f, nc * 128:(nc + 1) * 128],
                        self.xb_ap(f, t * 512, 512),
                        start=f == 0, stop=f == NCHUNK - 1),
                        waits=w, inc="sPE" if f == NCHUNK - 1 else None)
                    if f == 0:
                        pslot.set(mmtok)
                pslot.set(mmtok)
                for f in range(NCHUNK):
                    self.sXB[f][t].read(mmtok)
                te = P.emit("act", lambda e, nc=nc, t=t, ps=ps: e.activation(
                    z["K"][:, nc, t * 512:(t + 1) * 512], ps[:, 0:512],
                    AF.Identity, bias=0.0, scale=1.0),
                    waits=[mmtok] + (sK.war() if (nc == 0 and t == 0) else []),
                    inc="sAC")
                pslot.read(te)
        sK.set(te)
        sV = self.S_("c_V")
        tvo = P.emit("pool", lambda e: e.memset(
            bass.AP(z["V"], 64, [[16640, 128], [65, 16 * 16], [1, 1]]), 1.0),
            waits=sV.war(), inc="sPL")
        sV.set(tvo)
        for ts in range(NTSUB):
            for nh in range(2):
                bank = 2 + (ts * 2 + nh) % 2
                ps, pslot = self.ps[bank], self.ps_slot[bank]
                mmtok = None
                for f in range(NCHUNK):
                    w = [wv_toks[f], self.sXB[f][ts // 4].ready]
                    if f == 0:
                        w += pslot.war()
                    mmtok = P.emit("pe", lambda e, f=f, ts=ts, nh=nh, ps=ps: e.matmul(
                        ps[:, 0:512], self.xb_ap(f, ts * 128, 128),
                        z["w2"][:, f, nh * 512:(nh + 1) * 512],
                        start=f == 0, stop=f == NCHUNK - 1),
                        waits=w, inc="sPE" if f == NCHUNK - 1 else None)
                    if f == 0:
                        pslot.set(mmtok)
                pslot.set(mmtok)
                for f in range(NCHUNK):
                    self.sXB[f][ts // 4].read(mmtok)
                tev = P.emit("dve", lambda e, ts=ts, nh=nh, ps=ps: e.tensor_copy(
                    z["V"][:, ts, nh * 8:(nh + 1) * 8, 0:64],
                    ps[:, 0:512].rearrange("p (h d) -> p h d", h=8)),
                    waits=[mmtok, tvo], inc="sDV")
                pslot.read(tev)
                sV.read(tev)
        sV.set(tev)
        self.barrier()
        wq_toks = [P.dma("pool", z["w1"][:, f, :], self.W["bWq"][:, f, :])
                   for f in range(NCHUNK)]
        for T in range(NTSUB):
            t512 = T // 4
            sQ = self.S_("c_Q")
            for nc in range(NCHUNK):
                bank = nc % 2
                ps, pslot = self.ps[bank], self.ps_slot[bank]
                mmtok = None
                for f in range(NCHUNK):
                    w = [wq_toks[f], self.sXB[f][t512].ready]
                    if f == 0:
                        w += pslot.war()
                    mmtok = P.emit("pe", lambda e, f=f, nc=nc, T=T, ps=ps: e.matmul(
                        ps[:, 0:128], z["w1"][:, f, nc * 128:(nc + 1) * 128],
                        self.xb_ap(f, T * 128, 128),
                        start=f == 0, stop=f == NCHUNK - 1),
                        waits=w, inc="sPE" if f == NCHUNK - 1 else None)
                    if f == 0:
                        pslot.set(mmtok)
                pslot.set(mmtok)
                for f in range(NCHUNK):
                    self.sXB[f][t512].read(mmtok)
                tqe = P.emit("act", lambda e, nc=nc, ps=ps: e.activation(
                    z["Q"][:, nc, :], ps[:, 0:128], AF.Identity,
                    bias=0.0, scale=0.125),
                    waits=[mmtok] + (sQ.war() if nc == 0 else []), inc="sAC")
                pslot.read(tqe)
            sQ.set(tqe)
            sao = self.S_("c_ao")
            for hp in range(NCHUNK):
                sm_col = self.S_("c_m")
                ngrp = (T + 4) // 4
                tred = None
                for g in range(ngrp):
                    kb0 = g * 4
                    nk = min(4, T + 1 - kb0)
                    smm = [None, None]
                    for hh in range(2):
                        bank = hh
                        ps, pslot = self.ps[bank], self.ps_slot[bank]
                        mmtok = None
                        for i in range(nk):
                            kb = kb0 + i
                            w = [tqe, sK.ready]
                            if i == 0:
                                w += pslot.war()
                            mmtok = P.emit("pe", lambda e, hh=hh, hp=hp, kb=kb,
                                           i=i, ps=ps: e.matmul(
                                ps[:, i * 128:(i + 1) * 128],
                                z["Q"][64 * hh:64 * hh + 64, hp, :],
                                z["K"][64 * hh:64 * hh + 64, hp,
                                       kb * 128:(kb + 1) * 128],
                                start=True, stop=True,
                                tile_position=(64 * hh, 0)),
                                waits=w, inc="sPE" if i == nk - 1 else None)
                            if i == 0:
                                pslot.set(mmtok)
                        pslot.set(mmtok)
                        sQ.read(mmtok)
                        sK.read(mmtok)
                        smm[hh] = mmtok
                    if kb0 + nk - 1 == T:
                        for hh in range(2):
                            ps, pslot = self.ps[hh], self.ps_slot[hh]
                            tmsk = P.emit("dve", lambda e, hh=hh, nk=nk, ps=ps:
                                          e.scalar_tensor_tensor(
                                ps[:, (nk - 1) * 128:nk * 128],
                                ps[:, (nk - 1) * 128:nk * 128], 1.0,
                                self.cs["c_negmask_diag"][:],
                                ALU.mult, ALU.add),
                                waits=[smm[hh]], inc="sDV")
                            self.ps_slot[hh].set(tmsk)
                            smm[hh] = tmsk
                    for hh in range(2):
                        ps = self.ps[hh]
                        dst = z["m"][:, 2 + hh:3 + hh] if g > 0 else z["m"][:, hh:hh + 1]
                        tred = P.emit("dve", lambda e, hh=hh, nk=nk, g=g,
                                      dst=dst, ps=ps: e.tensor_reduce(
                            dst, ps[:, 0:nk * 128], AX.X, ALU.max),
                            waits=[smm[hh]] + (sm_col.war() if (g == 0 and hh == 0)
                                               else []), inc="sDV")
                        self.ps_slot[hh].read(tred)
                        if g > 0:
                            tred = P.emit("dve", lambda e, hh=hh: e.tensor_max(
                                z["m"][:, hh:hh + 1], z["m"][:, hh:hh + 1],
                                z["m"][:, 2 + hh:3 + hh]),
                                waits=[tred], inc="sDV")
                    sm_col.set(tred)
                tcb = P.emit("pool", lambda e: e.tensor_copy(z["mb"][:], z["m"][:, 0:2]),
                             waits=[tred] + self.S_("c_mb").war(), inc="sPL")
                self.S_("c_mb").set(tcb)
                sm_col.read(tcb)
                ps2, s2 = self.ps[2], self.ps_slot[2]
                tr1 = P.emit("pe", lambda e: e.matmul(
                    ps2[0:1, 0:128], z["mb"][:, 0:1], self.cs["c_negident"][:],
                    start=True, stop=True), waits=[tcb] + s2.war(), inc=None)
                tr2 = P.emit("pe", lambda e: e.matmul(
                    ps2[0:1, 128:256], z["mb"][:, 1:2], self.cs["c_negident"][:],
                    start=True, stop=True), waits=[], inc="sPE")
                s2.set(tr2)
                self.S_("c_mb").read(tr2)
                smr = self.S_("x2b")
                tmr = P.emit("dve", lambda e: e.tensor_copy(
                    self.x2b[0:1, 256:512], ps2[0:1, 0:256]),
                    waits=[tr2] + smr.war(), inc="sDV")
                smr.set(tmr)
                s2.read(tmr)
                ps4, s4 = self.ps[4], self.ps_slot[4]
                ps6b, s6b = self.ps[6], self.ps_slot[6]
                avps = (ps4, ps6b)
                avsl = (s4, s6b)
                spt = self.S_("c_pt")
                av_end = None
                first_av = [True, True]
                for kb in range(T + 1):
                    ex_toks = []
                    for hh in range(2):
                        bank = 2 + hh
                        ps, pslot = self.ps[bank], self.ps_slot[bank]
                        t1 = P.emit("pe", lambda e, hh=hh, hp=hp, kb=kb,
                                    ps=ps: e.matmul(
                            ps[:, 0:128],
                            z["K"][64 * hh:64 * hh + 64, hp,
                                   kb * 128:(kb + 1) * 128],
                            z["Q"][64 * hh:64 * hh + 64, hp, :],
                            start=True, stop=False,
                            tile_position=(64 * hh, 0)),
                            waits=[tqe, sK.ready, tmr] + pslot.war(), inc=None)
                        t2 = P.emit("pe", lambda e, hh=hh, ps=ps: e.matmul(
                            ps[:, 0:128], self.cs["c_ones_1x128"][:],
                            self.x2b[0:1, 256 + hh * 128:384 + hh * 128],
                            start=False, stop=True),
                            waits=[], inc="sPE")
                        pslot.set(t2)
                        sQ.read(t2)
                        sK.read(t2)
                        smr.read(t2)
                        tmask = t2
                        if kb == T:
                            tmask = P.emit("dve", lambda e, ps=ps: e.scalar_tensor_tensor(
                                ps[:, 0:128], ps[:, 0:128], 1.0,
                                self.cs["c_negmaskT_local01"][:, 0:128],
                                ALU.mult, ALU.add),
                                waits=[t2], inc="sDV")
                            pslot.set(tmask)
                        tex = P.emit("act", lambda e, hh=hh, ps=ps: e.activation(
                            z["pt"][:, hh, :], ps[:, 0:128], AF.Exp,
                            bias=0.0, scale=1.0),
                            waits=[tmask] + (spt.war() if hh == 0 else []),
                            inc="sAC")
                        pslot.read(tex)
                        ex_toks.append(tex)
                    spt.set(ex_toks[-1])
                    for hh in range(2):
                        w = [ex_toks[hh], sV.ready]
                        if first_av[hh]:
                            w += avsl[hh].war()
                        av_end = P.emit("pe", lambda e, hh=hh, hp=hp, kb=kb,
                                        T=T, pp=avps[hh]: e.matmul(
                            pp[0:65, 0:128],
                            z["V"][:, kb, 2 * hp + hh, :],
                            z["pt"][:, hh, :],
                            start=kb == 0, stop=kb == T),
                            waits=w, inc="sPE" if (kb == T and hh == 1) else None)
                        if first_av[hh]:
                            avsl[hh].set(av_end)
                            first_av[hh] = False
                    spt.read(av_end)
                    sV.read(av_end)
                sden = self.S_("sq_t")
                tdA = P.emit("dve", lambda e: e.tensor_copy(
                    self.sq_t[0:1, 0:128], ps4[64:65, 0:128]),
                    waits=[av_end] + sden.war(), inc="sDV")
                s4.read(tdA)
                td = P.emit("dve", lambda e: e.tensor_copy(
                    self.sq_t[0:1, 128:256], ps6b[64:65, 0:128]),
                    waits=[av_end], inc="sDV")
                sden.set(td)
                s6b.read(td)
                trc0 = P.emit("dve", lambda e: e.reciprocal(
                    self.sq_t[0:1, 256:512], self.sq_t[0:1, 0:256]),
                    waits=[td], inc="sDV")
                sden.set(trc0)
                srec = self.S_("t2b")
                trc = P.emit("dve", lambda e: e.tensor_copy(
                    self.t2b[0:1, 0:256], self.sq_t[0:1, 256:512]),
                    waits=[trc0] + srec.war(), inc="sDV")
                srec.set(trc)
                sden.read(trc)
                ps5, s5 = self.ps[5], self.ps_slot[5]
                trb = P.emit("pe", lambda e: e.matmul(
                    ps5[0:64, 0:256], self.cs["c_ones_1x128"][0:1, 0:64],
                    self.t2b[0:1, 0:256], start=True, stop=True),
                    waits=[trc] + s5.war(), inc="sPE")
                s5.set(trb)
                srec.read(trb)
                srb = self.S_("t2b")
                trbe = P.emit("dve", lambda e: e.tensor_copy(
                    self.t2b[0:64, 256:512], ps5[0:64, 0:256]),
                    waits=[trb] + srb.war(), inc="sDV")
                srb.set(trbe)
                s5.read(trbe)
                for hh in range(2):
                    tf = P.emit("dve", lambda e, hh=hh, hp=hp, pp=avps[hh]:
                                e.tensor_mul(
                        z["ao"][64 * hh:64 * hh + 64, hp, :],
                        pp[0:64, 0:128],
                        self.t2b[0:64, 256 + hh * 128:384 + hh * 128]),
                        waits=[trbe] + (sao.war() if (hp == 0 and hh == 0) else []),
                        inc="sDV")
                    avsl[hh].read(tf)
                    srb.read(tf)
            sao.set(tf)
            for nc in range(NCHUNK):
                sl = nc % 2
                slab = z["woslab"][:, sl, :, :]
                ssl = self.S_(f"c_woslab{sl}")
                tw = P.dma("pool", slab,
                           self.W["bWo"][:, :, nc * 128:(nc + 1) * 128],
                           waits=ssl.war())
                ssl.set(tw)
                ps5, s5 = self.ps[5], self.ps_slot[5]
                mmtok = None
                for f in range(NCHUNK):
                    w = [tw, sao.ready]
                    if f == 0:
                        w += s5.war()
                    mmtok = P.emit("pe", lambda e, f=f, slab=slab: e.matmul(
                        ps5[:, 0:128], slab[:, f, :], z["ao"][:, f, :],
                        start=f == 0, stop=f == NCHUNK - 1),
                        waits=w, inc="sPE" if f == NCHUNK - 1 else None)
                    if f == 0:
                        s5.set(mmtok)
                s5.set(mmtok)
                ssl.read(mmtok)
                sao.read(mmtok)
                sxp = self.S_("c_xp")
                tld = P.dma("sync", z["xp"][:],
                            self.xr[1][nc * 128:(nc + 1) * 128,
                                       T * 128:(T + 1) * 128], waits=sxp.war())
                sxp.set(tld)
                st2 = self.S_(f"t2_{nc}")
                tt2 = P.emit("dve", lambda e, nc=nc, T=T: e.tensor_add(
                    self.t2[:, nc, (T % 4) * 128:(T % 4 + 1) * 128],
                    z["xp"][:], ps5[:, 0:128]),
                    waits=[mmtok, tld] + (st2.war() if T % 4 == 0 else []),
                    inc="sDV")
                if T % 4 == 0:
                    st2.set(tt2)
                sxp.read(tt2)
                s5.read(tt2)
                st2.set(tt2)
            if T % 4 == 3:
                t = t512
                t2tok = []
                for c in range(NCHUNK):
                    st2 = self.S_(f"t2_{c}")
                    tok = st2.ready
                    t2tok.append(tok)
                    xn_ap = self.xr[0][c * 128:(c + 1) * 128,
                                       t * 512:(t + 1) * 512]
                    tsp = P.dma("act", xn_ap, self.t2[:, c, :], waits=[tok])
                    st2.read(tsp)
                    tk = self.tile_stats(c, self.t2[:, c, :], tok, bank=6)
                    st2.read(tk)
                tmu, trb2, _ = self.ln_rowmath(6, 5)
                for c in range(NCHUNK):
                    st2 = self.S_(f"t2_{c}")
                    sd = self.S_("d1_t")
                    ta = P.emit("dve", lambda e, c=c: e.tensor_sub(
                        self.d1_t[:], self.t2[:, c, :], self.mu_t[:]),
                        waits=[st2.ready, tmu] + sd.war(), inc="sDV")
                    sd.set(ta)
                    st2.read(ta)
                    self.S_("mu_t").read(ta)
                    sh = self.S_("hs_t")
                    tb = P.emit("dve", lambda e: e.tensor_mul(
                        self.hs_t[:], self.d1_t[:], self.rb_t[:]),
                        waits=[ta, trb2] + sh.war(), inc="sDV")
                    sh.set(tb)
                    sd.read(tb)
                    self.S_("rb_t").read(tb)
                    self.stream_split(c, t, tb)

    def finalize(self):
        from contextlib import ExitStack
        nc = self.nc
        with ExitStack() as ctx:
            for s in list(self.P.counts):
                self.P.handles[s] = ctx.enter_context(nc.semaphore(s))
            block = ctx.enter_context(nc.Block())
            block.tensor(lambda e: self.P.run_stream("pe", e))
            block.vector(lambda e: self.P.run_stream("dve", e))
            block.scalar(lambda e: self.P.run_stream("act", e))
            block.gpsimd(lambda e: self.P.run_stream("pool", e))
            block.sync(lambda e: self.P.run_stream("sync", e))
        return nc


    def ffn_attn_tsub(self, seq, l, ts, wq_toks, trot, mid_cb=None):
        P = self.P
        z = self.zb
        t512 = ts // 4
        part_tok = []
        for part in range(3):
            for nh in range(2):
                bank = part * 2 + nh
                ps, pslot = self.ps[bank], self.ps_slot[bank]
                mmtok = None
                for f in range(NCHUNK):
                    first = f == 0
                    lhs = self.xb_ap(f, ts * 128, 128, shifted=(f >= 4))
                    w = [wq_toks[f], self.sXB[f][t512].ready]
                    if f >= 4 and ts % 4 == 0 and t512 > 0:
                        w.append(self.sXB[f][t512 - 1].ready)
                    if first:
                        w += pslot.war()
                    mmtok = P.emit("pe", lambda e, f=f, lhs=lhs, part=part, nh=nh,
                                   ps=ps: e.matmul(
                        ps[:, 0:512], lhs,
                        z["wqkv"][:, f, part * 1024 + nh * 512:
                                  part * 1024 + (nh + 1) * 512],
                        start=f == 0, stop=f == NCHUNK - 1),
                        waits=w, inc="sPE" if f == NCHUNK - 1 else None)
                    if first:
                        pslot.set(mmtok)
                pslot.set(mmtok)
                for f in range(NCHUNK):
                    self.sXB[f][t512].read(mmtok)
                    if f >= 4 and ts % 4 == 0 and t512 > 0:
                        self.sXB[f][t512 - 1].read(mmtok)
                part_tok.append(mmtok)
        if mid_cb is not None:
            mid_cb()
        ssq = self.S_("b_sq")
        snm = self.S_("b_norms")
        tnm = None
        for part in range(2):
            sq_toks = []
            for nh in range(2):
                bank = part * 2 + nh
                tq = P.emit("act", lambda e, bank=bank, nh=nh: e.activation(
                    z["sq"][:, nh * 512:(nh + 1) * 512], self.ps[bank][:, 0:512],
                    AF.Square, bias=0.0, scale=1.0),
                    waits=[part_tok[part * 2 + nh]] + (ssq.war() if nh == 0 else []),
                    inc="sAC")
                sq_toks.append(tq)
                self.ps_slot[bank].read(tq)
            ssq.set(sq_toks[-1])
            tnm = P.emit("dve", lambda e, part=part: e.tensor_reduce(
                z["norms"][:, part * 16:(part + 1) * 16],
                bass.AP(z["sq"], 0, [[1024, 128], [64, 16], [1, 64]]),
                AX.X, ALU.add),
                waits=sq_toks + (snm.war() if part == 0 else []), inc="sDV")
            ssq.read(tnm)
        snm.set(tnm)
        srs = self.S_("b_rstd")
        t1 = P.emit("dve", lambda e: e.tensor_scalar_max(
            z["rstd"][:], z["norms"][:], 1e-24),
            waits=[tnm] + srs.war(), inc="sDV")
        snm.read(t1)
        t2 = P.emit("act", lambda e: e.activation(
            z["rstd"][:], z["rstd"][:], AF.Ln, bias=0.0, scale=1.0),
            waits=[t1], inc="sAC")
        t3 = P.emit("act", lambda e: e.activation(
            z["rstd"][:], z["rstd"][:], AF.Exp, bias=0.0, scale=-0.5),
            waits=[t2], inc="sAC")
        srs.set(t3)
        sqh = self.S_("b_qhat")
        tqh = None
        for part in range(2):
            for nh in range(2):
                bank = part * 2 + nh
                rs = bass.AP(z["rstd"], part * 16 + nh * 8,
                             [[32, 128], [1, 8], [0, 64]])
                tqh = P.emit("dve", lambda e, bank=bank, part=part, nh=nh, rs=rs:
                             e.tensor_tensor(
                    z["qhat"][:, part * 1024 + nh * 512:
                              part * 1024 + (nh + 1) * 512],
                    self.ps[bank][:, 0:512], rs, ALU.mult),
                    waits=[t3, part_tok[part * 2 + nh]] +
                          (sqh.war() if (part == 0 and nh == 0) else []),
                    inc="sDV")
                self.ps_slot[bank].read(tqh)
        sqh.set(tqh)
        srs.read(tqh)
        if seq == 0 and l == 0 and ts == 0:
            self.dump_tile("qpre0", z["qhat"][:], [tqh],
                           slots=("b_qhat", "b_rout"))
        sr1 = self.S_("b_rsc1x")
        sr2 = self.S_("b_sq")
        sro = self.S_("b_rout")
        tro = None
        for part in range(2):
            c1 = bass.AP(z["rot"], (part * 2) * 1024 + ts * 64,
                         [[4 * 1024, 128], [0, 16], [1, 64]])
            c2 = bass.AP(z["rot"], (part * 2 + 1) * 1024 + ts * 64,
                         [[4 * 1024, 128], [0, 16], [1, 64]])
            ta = P.emit("dve", lambda e, part=part, c1=c1: e.tensor_tensor(
                self.zb["rsc1"][:, 0:1024], z["qhat"][:, part * 1024:(part + 1) * 1024],
                c1, ALU.mult),
                waits=[tqh, trot] + sr1.war(), inc="sDV")
            sr1.set(ta)
            sqh.read(ta)
            tb_ = None
            for j0, joff in ((0, 32), (32, -32)):
                qsh = bass.AP(z["qhat"], part * 1024 + j0 + joff,
                              [[2048, 128], [64, 16], [1, 32]])
                c2s = bass.AP(z["rot"], (part * 2 + 1) * 1024 + ts * 64 + j0,
                              [[4 * 1024, 128], [0, 16], [1, 32]])
                tb_ = P.emit("dve", lambda e, qsh=qsh, c2s=c2s, j0=j0: e.tensor_tensor(
                    bass.AP(self.zb["sq"], j0, [[1024, 128], [64, 16], [1, 32]]),
                    qsh, c2s, ALU.mult),
                    waits=[tqh, trot] + (sr2.war() if j0 == 0 else []), inc="sDV")
            sr2.set(tb_)
            sqh.read(tb_)
            tro = P.emit("pool", lambda e, part=part: e.tensor_add(
                z["qhat"][:, part * 1024:(part + 1) * 1024],
                self.zb["rsc1"][:, 0:1024], self.zb["sq"][:, 0:1024]),
                waits=[ta, tb_] + (sro.war() if part == 0 else []), inc="sPL")
            sr1.read(tro)
            sr2.read(tro)
        sro.set(tro)
        if seq == 0 and l == 0 and ts == 0:
            self.dump_tile("qh0", z["qhat"][:], [tro],
                           slots=("b_qhat", "b_rout"))
        psbf, sbf = self.psbf, self.ps_slot[7]
        sqc = self.S_("b_qcur")
        skr = self.S_(f"b_kroll{ts % 3}")
        tq_end = tk_end = None
        for hp in range(NCHUNK):
            slot = hp % 8
            ttq = P.emit("pe", lambda e, hp=hp, slot=slot: e.transpose(
                psbf[:, slot * 128:(slot + 1) * 128],
                z["qhat"][:, hp * 128:(hp + 1) * 128], self.cs["c_ident"][:]),
                waits=[tro] + sbf.war(), inc="sPE")
            sbf.set(ttq)
            sro.read(ttq)
            tq_end = P.emit("dve", lambda e, hp=hp, slot=slot: e.tensor_copy(
                z["qcur"][:, hp, :], psbf[:, slot * 128:(slot + 1) * 128]),
                waits=[ttq] + (sqc.war() if hp == 0 else []), inc="sDV")
            sbf.read(tq_end)
        sqc.set(tq_end)
        for hp in range(NCHUNK):
            slot = hp % 8
            ttk = P.emit("pe", lambda e, hp=hp, slot=slot: e.transpose(
                psbf[:, slot * 128:(slot + 1) * 128],
                z["qhat"][:, 1024 + hp * 128:1024 + (hp + 1) * 128],
                self.cs["c_ident"][:]),
                waits=[tro] + sbf.war(), inc="sPE")
            sbf.set(ttk)
            sro.read(ttk)
            tk_end = P.emit("dve", lambda e, hp=hp, slot=slot, ts=ts: e.tensor_copy(
                z["kroll"][:, hp, ts % 3, :], psbf[:, slot * 128:(slot + 1) * 128]),
                waits=[ttk] + (skr.war() if hp == 0 else []), inc="sDV")
            sbf.read(tk_end)
        skr.set(tk_end)
        svr = self.S_(f"b_vroll{ts % 3}")
        svra = self.S_("b_vroll")
        tv_end = None
        for nh in range(2):
            bank = 4 + nh
            tv_end = P.emit("dve", lambda e, nh=nh, ts=ts, bank=bank: e.tensor_copy(
                z["vroll"][:, ts % 3, nh * 8:(nh + 1) * 8, 0:64],
                self.ps[bank][:, 0:512].rearrange("p (h d) -> p h d", h=8)),
                waits=[part_tok[4 + nh], svra.ready] +
                      (svr.war() if nh == 0 else []), inc="sDV")
            self.ps_slot[bank].read(tv_end)
        svr.set(tv_end)
        T = ts
        nkb = min(T + 1, 3)
        kslots = [(T - i) % 3 for i in range(nkb - 1, -1, -1)]
        ktoks = [self.S_(f"b_kroll{s}").ready for s in kslots]
        vtoks = [self.S_(f"b_vroll{s}").ready for s in kslots]
        mask_c0 = (3 - nkb) * 128
        sat = self.S_("b_attout")
        for hp in range(NCHUNK):
            s_tok = [None, None]
            for hh in range(2):
                bank = hh
                ps, pslot = self.ps[bank], self.ps_slot[bank]
                mmtok = None
                for kb in range(nkb):
                    kr = z["kroll"][64 * hh:64 * hh + 64, hp, kslots[kb], :]
                    w = [tq_end, ktoks[kb]]
                    if kb == 0:
                        w += pslot.war()
                    mmtok = P.emit("pe", lambda e, kr=kr, hh=hh, kb=kb, hp=hp,
                                   ps=ps: e.matmul(
                        ps[:, kb * 128:(kb + 1) * 128],
                        z["qcur"][64 * hh:64 * hh + 64, hp, :], kr,
                        start=True, stop=True,
                        tile_position=(64 * hh, 0)),
                        waits=w, inc="sPE" if kb == nkb - 1 else None)
                    if kb == 0:
                        pslot.set(mmtok)
                pslot.set(mmtok)
                sqc.read(mmtok)
                s_tok[hh] = mmtok
            ssm = self.S_("b_sm")
            tm_ = [None, None]
            for hh in range(2):
                tm_[hh] = P.emit("dve", lambda e, hh=hh, nkb=nkb, mask_c0=mask_c0:
                                 e.scalar_tensor_tensor(
                    z["sm"][:, hh * 384:hh * 384 + nkb * 128],
                    self.ps[hh][:, 0:nkb * 128], 1.0,
                    self.cs["c_negmask_local"][:, mask_c0:mask_c0 + nkb * 128],
                    ALU.mult, ALU.add),
                    waits=[s_tok[hh]] + (ssm.war() if hh == 0 else []), inc="sDV")
                self.ps_slot[hh].read(tm_[hh])
            ssm.set(tm_[1])
            if seq == 0 and l == 0 and ts == 0 and hp == 0:
                self.dump_tile("sm0", z["sm"][:], [tm_[1]], slots=("b_sm",))
            smc = self.S_("b_mcol")
            tmc = P.emit("dve", lambda e, nkb=nkb: e.tensor_reduce(
                z["mcolf"][:], bass.AP(z["sm"], 0, [[768, 128], [384, 2], [1, nkb * 128]]),
                AX.X, ALU.max),
                waits=tm_ + smc.war(), inc="sDV")
            ssm.read(tmc)
            tmc2 = P.emit("pool", lambda e: e.tensor_copy(z["mcol"][:], z["mcolf"][:]),
                          waits=[tmc], inc="sPL")
            smc.set(tmc2)
            if seq == 0 and l == 0 and ts == 0 and hp == 0:
                self.dump_tile("mc0", z["mcolf"][:], [tmc], slots=("b_mcol",))
            ps2, s2 = self.ps[2], self.ps_slot[2]
            tr1 = P.emit("pe", lambda e: e.matmul(
                ps2[0:1, 0:128], z["mcol"][:, 0:1], self.cs["c_negident"][:],
                start=True, stop=True), waits=[tmc2] + s2.war(), inc=None)
            tr2 = P.emit("pe", lambda e: e.matmul(
                ps2[0:1, 128:256], z["mcol"][:, 1:2], self.cs["c_negident"][:],
                start=True, stop=True), waits=[], inc="sPE")
            s2.set(tr2)
            smc.read(tr2)
            smr = self.S_("x2b")
            tmr = P.emit("dve", lambda e: e.tensor_copy(
                self.x2b[0:1, 256:512], ps2[0:1, 0:256]),
                waits=[tr2] + smr.war(), inc="sDV")
            smr.set(tmr)
            s2.read(tmr)
            spt = self.S_("b_pt")
            ex_toks = []
            for hh in range(2):
                bank = 2 + hh
                ps, pslot = self.ps[bank], self.ps_slot[bank]
                stt_end = None
                for kb in range(nkb):
                    kr = z["kroll"][64 * hh:64 * hh + 64, hp, kslots[kb], :]
                    w = [tq_end, ktoks[kb], tmr]
                    if kb == 0:
                        w += pslot.war()
                    P.emit("pe", lambda e, kr=kr, hh=hh, kb=kb, ps=ps,
                           hp=hp: e.matmul(
                        ps[:, kb * 128:(kb + 1) * 128], kr,
                        z["qcur"][64 * hh:64 * hh + 64, hp, :],
                        start=True, stop=False,
                        tile_position=(64 * hh, 0)),
                        waits=w, inc=None)
                    stt_end = P.emit("pe", lambda e, hh=hh, kb=kb, ps=ps: e.matmul(
                        ps[:, kb * 128:(kb + 1) * 128], self.cs["c_ones_1x128"][:],
                        self.x2b[0:1, 256 + hh * 128:256 + (hh + 1) * 128],
                        start=False, stop=True),
                        waits=[], inc="sPE" if kb == nkb - 1 else None)
                pslot.set(stt_end)
                smr.read(stt_end)
                sqc.read(stt_end)
                tmask = stt_end
                mask_ops = []
                if True:
                    mask_ops.append((nkb - 1, 0))
                if nkb == 3:
                    mask_ops.append((0, 128))
                for kb, mc in mask_ops:
                    tmask = P.emit("dve", lambda e, kb=kb, mc=mc, ps=ps: e.scalar_tensor_tensor(
                        ps[:, kb * 128:(kb + 1) * 128], ps[:, kb * 128:(kb + 1) * 128],
                        1.0, self.cs["c_negmaskT_local01"][:, mc:mc + 128],
                        ALU.mult, ALU.add),
                        waits=[stt_end], inc="sDV")
                tex = P.emit("act", lambda e, hh=hh, nkb=nkb, ps=ps: e.activation(
                    z["pt"][:, hh, 0:nkb * 128], ps[:, 0:nkb * 128], AF.Exp,
                    bias=0.0, scale=1.0),
                    waits=[tmask] + (spt.war() if hh == 0 else []), inc="sAC")
                pslot.read(tex)
                ex_toks.append(tex)
            spt.set(ex_toks[-1])
            if seq == 0 and l == 0 and ts == 0 and hp == 0:
                self.dump_tile("pt0", z["pt"][:], [ex_toks[-1]], slots=("b_pt",))
            ps4, s4 = self.ps[4], self.ps_slot[4]
            av_end = None
            for hh in range(2):
                for kb in range(nkb):
                    w = [ex_toks[hh], vtoks[kb]]
                    if hh == 0 and kb == 0:
                        w += s4.war()
                    av_end = P.emit("pe", lambda e, hh=hh, kb=kb, hp=hp,
                                    ks=kslots[kb], nkb=nkb: e.matmul(
                        ps4[0:65, hh * 128:(hh + 1) * 128],
                        z["vroll"][:, ks, 2 * hp + hh, :],
                        z["pt"][:, hh, kb * 128:(kb + 1) * 128],
                        start=kb == 0, stop=kb == nkb - 1),
                        waits=w, inc="sPE" if (hh == 1 and kb == nkb - 1) else None)
                    if hh == 0 and kb == 0:
                        s4.set(av_end)
            s4.set(av_end)
            spt.read(av_end)
            for kb in range(nkb):
                self.S_(f"b_vroll{kslots[kb]}").read(av_end)
            sden = self.S_("sq_t")
            td = P.emit("dve", lambda e: e.tensor_copy(
                self.sq_t[0:1, 0:256], ps4[64:65, 0:256]),
                waits=[av_end] + sden.war(), inc="sDV")
            sden.set(td)
            s4.read(td)
            trc0 = P.emit("dve", lambda e: e.reciprocal(
                self.sq_t[0:1, 256:512], self.sq_t[0:1, 0:256]),
                waits=[td], inc="sDV")
            sden.set(trc0)
            srec = self.S_("x2b")
            trc = P.emit("dve", lambda e: e.tensor_copy(
                self.x2b[0:1, 0:256], self.sq_t[0:1, 256:512]),
                waits=[trc0] + srec.war(), inc="sDV")
            srec.set(trc)
            sden.read(trc)
            ps5, s5 = self.ps[5], self.ps_slot[5]
            trb = P.emit("pe", lambda e: e.matmul(
                ps5[0:64, 0:256], self.cs["c_ones_1x128"][0:1, 0:64],
                self.x2b[0:1, 0:256], start=True, stop=True),
                waits=[trc] + s5.war(), inc="sPE")
            s5.set(trb)
            srec.read(trb)
            srb = self.S_("b_sq")
            trbe = P.emit("dve", lambda e: e.tensor_copy(
                z["sq"][0:64, 0:256], ps5[0:64, 0:256]),
                waits=[trb] + srb.war(), inc="sDV")
            srb.set(trbe)
            s5.read(trbe)
            for hh in range(2):
                tf = P.emit("dve", lambda e, hh=hh, hp=hp, T=T: e.tensor_mul(
                    z["attout"][64 * hh:64 * hh + 64, hp, (T % 4) * 128:
                                (T % 4 + 1) * 128],
                    ps4[0:64, hh * 128:(hh + 1) * 128],
                    z["sq"][0:64, hh * 128:(hh + 1) * 128]),
                    waits=[trbe] + (sat.war() if (hp == 0 and hh == 0 and T % 4 == 0)
                                    else []), inc="sDV")
                s4.read(tf)
                srb.read(tf)
            sat.set(tf)
        if seq == 0 and l == 0 and ts == 3:
            self.dump_tile("att0", z["attout"][:],
                           [self.S_("b_attout").ready], slots=("b_attout",))

    def ffn_wo_resid(self, seq, l, t):
        P = self.P
        z = self.zb
        sat = self.S_("b_attout")
        ps5, s5 = self.ps[6], self.ps_slot[6]
        for ncx in range(NCHUNK):
            sl = ncx % 2
            slab = z["woslab"][:, sl, :, :]
            ssl = self.S_(f"b_woslab{sl}")
            tw = P.dma("pool", slab, self.W["fWo"][l, ncx], waits=ssl.war())
            ssl.set(tw)
            mmtok = None
            for f in range(NCHUNK):
                w = [tw, sat.ready]
                if f == 0:
                    w += s5.war()
                mmtok = P.emit("pe", lambda e, f=f, slab=slab: e.matmul(
                    ps5[:, 0:512], slab[:, f, :], z["attout"][:, f, :],
                    start=f == 0, stop=f == NCHUNK - 1),
                    waits=w, inc="sPE" if f == NCHUNK - 1 else None)
                if f == 0:
                    s5.set(mmtok)
            s5.set(mmtok)
            ssl.read(mmtok)
            sat.read(mmtok)
            sd = self.S_("d1_t")
            xb = self.xb_ap(ncx, t * 512, 512)
            xl = self.xb_ap(ncx, t * 512, 512, lo=True)
            p1 = P.emit("dve", lambda e, xb=xb: e.tensor_add(
                self.d1_t[:], ps5[:, 0:512], xb),
                waits=[mmtok, self.sXB[ncx][t].ready] + sd.war(), inc="sDV")
            sd.set(p1)
            s5.read(p1)
            self.sXB[ncx][t].read(p1)
            sh = self.S_("hs_t")
            p2 = P.emit("pool", lambda e, xl=xl: e.tensor_add(
                self.hs_t[:], self.d1_t[:], xl),
                waits=[p1, self.sXL[ncx][t].ready] + sh.war(), inc="sPL")
            sh.set(p2)
            sd.read(p2)
            self.sXL[ncx][t].read(p2)
            self.stream_split(ncx, t, p2)

    def ffn_geglu_tile(self, seq, l, t):
        P = self.P
        z = self.zb
        for c in range(NCHUNK):
            xbs = self.xb_ap(c, t * 512, 512, shifted=(c >= 4))
            prev = [self.sXB[c][t - 1].ready] if (c >= 4 and t > 0) else []
            s2 = self.S_("x2b")
            tq = P.emit("act", lambda e, xbs=xbs: e.activation(
                self.x2b[:], xbs, AF.Square, bias=0.0, scale=1.0),
                waits=[self.sXB[c][t].ready] + prev + s2.war(), inc="sAC")
            s2.set(tq)
            tk = self.stats_mms(6, c, xbs, self.x2b[:],
                                [tq, self.sXB[c][t].ready] + prev)
            s2.read(tk)
            self.sXB[c][t].read(tk)
            if c >= 4 and t > 0:
                self.sXB[c][t - 1].read(tk)
        tmu, trb, tnm = self.ln_rowmath(6, 5)
        ps4, s4 = self.ps[4], self.ps_slot[4]
        sxg = self.S_("b_xg")
        txg = None
        for c in range(NCHUNK):
            tb = P.emit("pe", lambda e, c=c: e.matmul(
                ps4[:, 0:512], self.prow[0:1, c * 128:(c + 1) * 128],
                self.negmu_row[:], start=True, stop=True),
                waits=[tnm, self.ptoks[-1]] + s4.war(), inc="sPE")
            s4.set(tb)
            self.S_("negmu_row").read(tb)
            xbs = self.xb_ap(c, t * 512, 512, shifted=(c >= 4))
            prev = [self.sXB[c][t - 1].ready] if (c >= 4 and t > 0) else []
            txg = P.emit("dve", lambda e, c=c, xbs=xbs: e.scalar_tensor_tensor(
                z["xg"][:, c, :], xbs, self.pcol[:, 52 + c:53 + c], ps4[:, 0:512],
                ALU.mult, ALU.add),
                waits=[tb, self.sXB[c][t].ready] + prev + self.ptoks +
                      (sxg.war() if c == 0 else []), inc="sDV")
            s4.read(txg)
            self.sXB[c][t].read(txg)
            if c >= 4 and t > 0:
                self.sXB[c][t - 1].read(txg)
        sxg.set(txg)
        first_t = t == NT512 - 1
        scb = self.S_("b_cb")
        for g in range(22):
            sl = g % 2
            slab = z["w1slab"][:, sl, :, :]
            ssl = self.S_(f"b_w1slab{sl}")
            tw = P.dma("pool", slab, self.W["fW1"][l, g], waits=ssl.war())
            ssl.set(tw)
            atok = gtok = None
            for half in range(2):
                bank = half * 2 + g % 2
                ps, pslot = self.ps[bank], self.ps_slot[bank]
                mmtok = None
                for f in range(NCHUNK):
                    w = [tw, txg]
                    if f == 0:
                        w += pslot.war()
                    mmtok = P.emit("pe", lambda e, f=f, slab=slab, half=half,
                                   ps=ps: e.matmul(
                        ps[:, 0:512], slab[:, f, half * 128:(half + 1) * 128],
                        z["xg"][:, f, :],
                        start=f == 0, stop=f == NCHUNK - 1),
                        waits=w, inc="sPE" if f == NCHUNK - 1 else None)
                    if f == 0:
                        pslot.set(mmtok)
                pslot.set(mmtok)
                sxg.read(mmtok)
                ssl.read(mmtok)
                if half == 0:
                    atok = mmtok
                else:
                    gtok = mmtok
            if first_t:
                cbtok = None
                for half in range(2):
                    for f in range(NCHUNK):
                        w = [tw] + self.ptoks
                        if half == 0 and f == 0:
                            w += s4.war()
                        cbtok = P.emit("pe", lambda e, f=f, half=half, slab=slab:
                                       e.matmul(
                            ps4[:, half:half + 1],
                            slab[:, f, half * 128:(half + 1) * 128],
                            self.t2b[:, f:f + 1],
                            start=f == 0, stop=f == NCHUNK - 1),
                            waits=w, inc="sPE" if (half == 1 and f == NCHUNK - 1)
                            else None)
                        if half == 0 and f == 0:
                            s4.set(cbtok)
                s4.set(cbtok)
                self.S_("t2b").read(cbtok)
                ssl.read(cbtok)
                tcb = P.emit("dve", lambda e, g=g: e.tensor_copy(
                    bass.AP(z["cb2"], g, [[44, 128], [22, 2], [1, 1]]),
                    ps4[:, 0:2]),
                    waits=[cbtok] + (scb.war() if g == 0 else []), inc="sDV")
                s4.read(tcb)
                if g == 21:
                    tcb = P.emit("pool", lambda e: e.tensor_add(
                        z["cb2"][:], z["cb2"][:], self.pcol[:, 0:44]),
                        waits=[tcb] + self.ptoks, inc="sPL")
                scb.set(tcb)
            sd = self.S_("d1_t")
            gbank = 2 + g % 2
            t1 = P.emit("dve", lambda e, gbank=gbank: e.tensor_mul(
                self.d1_t[:], self.ps[gbank][:, 0:512], self.rb_t[:]),
                waits=[gtok, trb, scb.ready] + sd.war(), inc="sDV")
            sd.set(t1)
            self.ps_slot[gbank].read(t1)
            self.S_("rb_t").read(t1)
            sh = self.S_("hs_t")
            t2 = P.emit("act", lambda e, g=g: e.activation(
                self.hs_t[:], self.d1_t[:], AF.Gelu,
                bias=z["cb2"][:, 22 + g:23 + g], scale=1.0),
                waits=[t1, scb.ready] + sh.war(), inc="sAC")
            sh.set(t2)
            sd.read(t2)
            scb.read(t2)
            abank = g % 2
            t3 = P.emit("dve", lambda e, abank=abank: e.tensor_mul(
                self.d1_t[:], self.ps[abank][:, 0:512], self.rb_t[:]),
                waits=[atok, trb, t2] + sd.war(), inc="sDV")
            sd.set(t3)
            self.ps_slot[abank].read(t3)
            self.S_("rb_t").read(t3)
            sgg = self.S_("b_gg")
            t4 = P.emit("dve", lambda e, g=g: e.scalar_tensor_tensor(
                z["gg"][:, g, :], self.d1_t[:], z["cb2"][:, g:g + 1],
                self.hs_t[:], ALU.add, ALU.mult),
                waits=[t3, t2, scb.ready] + (sgg.war() if g == 0 else []),
                inc="sDV")
            sd.read(t4)
            sh.read(t4)
            scb.read(t4)
        sgg = self.S_("b_gg")
        sgg.set(t4)
        for ncx in range(NCHUNK):
            sl = ncx % 2
            slab = z["w2slab"][:, sl, :, :]
            ssl = self.S_(f"b_w2slab{sl}")
            tw = P.dma("pool", slab, self.W["fW2"][l, ncx], waits=ssl.war())
            ssl.set(tw)
            bank = 4 + ncx % 2
            ps, pslot = self.ps[bank], self.ps_slot[bank]
            mmtok = None
            for gc in range(22):
                w = [tw, sgg.ready]
                if gc == 0:
                    w += pslot.war()
                mmtok = P.emit("pe", lambda e, gc=gc, slab=slab, ps=ps: e.matmul(
                    ps[:, 0:512], slab[:, gc, :], z["gg"][:, gc, :],
                    start=gc == 0, stop=gc == 21),
                    waits=w, inc="sPE" if gc == 21 else None)
                if gc == 0:
                    pslot.set(mmtok)
            pslot.set(mmtok)
            ssl.read(mmtok)
            sgg.read(mmtok)
            sd = self.S_("d1_t")
            xb = self.xb_ap(ncx, t * 512, 512)
            xl = self.xb_ap(ncx, t * 512, 512, lo=True)
            p1 = P.emit("dve", lambda e, ncx=ncx, xb=xb, ps=ps: e.scalar_tensor_tensor(
                self.d1_t[:], ps[:, 0:512], self.pcol[:, 44 + ncx:45 + ncx],
                xb, ALU.add, ALU.add),
                waits=[mmtok, self.sXB[ncx][t].ready] + self.ptoks + sd.war(),
                inc="sDV")
            sd.set(p1)
            pslot.read(p1)
            self.sXB[ncx][t].read(p1)
            sh = self.S_("hs_t")
            p2 = P.emit("pool", lambda e, xl=xl: e.tensor_add(
                self.hs_t[:], self.d1_t[:], xl),
                waits=[p1, self.sXL[ncx][t].ready] + sh.war(), inc="sPL")
            sh.set(p2)
            sd.read(p2)
            self.sXL[ncx][t].read(p2)
            self.stream_split(ncx, t, p2)


def build_kernel(nseq=SEQ_PER_CORE, phases="ABCD", debug=()):
    mk = MK(nseq=nseq, phases=phases, debug=debug)
    mk.build()
    nc = mk.finalize()
    return mk, nc


def kernel(**inputs):
    I = {k: np.asarray(v) for k, v in inputs.items()}
    H = prep_inputs(I)
    x = I["L0_attn_a_x"]
    mk, nc = build_kernel()
    in_maps = []
    for core in range(NCORES):
        sh = dict(H)
        xs = x[core * SEQ_PER_CORE:(core + 1) * SEQ_PER_CORE]
        sh["xT"] = np.ascontiguousarray(xs.transpose(0, 2, 1))
        in_maps.append(sh)
    res = run_bass_kernel_spmd(nc, in_maps, core_ids=list(range(NCORES)))
    outs = []
    for core in range(NCORES):
        oT = np.asarray(res.results[core]["outT"])
        outs.append(oT.transpose(0, 2, 1))
    return np.ascontiguousarray(np.concatenate(outs, axis=0)).astype(np.float32)


# version 03
# speedup vs baseline: 1.1283x; 1.1283x over previous
import numpy as np
import ml_dtypes
import concourse.bass as bass
import concourse.mybir as mybir
from concourse.bass_utils import run_bass_kernel_spmd

F32 = mybir.dt.float32
BF16 = mybir.dt.bfloat16
AF = mybir.ActivationFunctionType
ALU = mybir.AluOpType
AX = mybir.AxisListType

NCORES = 8
SEQ_PER_CORE = 4
S = 2048
D = 1024
NCHUNK = 8
NT512 = S // 512
NTSUB = S // 128
LN_EPS = 1e-5

BF = ml_dtypes.bfloat16



def repack_slab_a(W):
    *lead, K, N = W.shape
    A = W.reshape(*lead, K // 128, 128, N // 128, 128)
    nl = len(lead)
    perm = list(range(nl)) + [nl + 2, nl + 1, nl + 0, nl + 3]
    return np.ascontiguousarray(A.transpose(*perm))


def repack_resident_b(W):
    *lead, K, N = W.shape
    A = W.reshape(*lead, K // 128, 128, N)
    nl = len(lead)
    perm = list(range(nl)) + [nl + 1, nl + 0, nl + 2]
    return np.ascontiguousarray(A.transpose(*perm))


def repack_cols(v):
    *lead, N = v.shape
    A = v.reshape(*lead, N // 128, 128)
    nl = len(lead)
    perm = list(range(nl)) + [nl + 1, nl + 0]
    return np.ascontiguousarray(A.transpose(*perm))


def build_host_consts():
    C = {}
    ident = np.eye(128, dtype=np.float32)
    C["c_ident"] = ident.astype(BF)
    C["c_negident"] = (-ident).astype(BF)
    k = np.arange(128)[:, None]
    q = np.arange(128)[None, :]
    C["c_maskT_diag"] = (k <= q).astype(np.float32).astype(BF)
    il = np.arange(128)[:, None]
    jl = np.arange(128)[None, :]
    blk0 = np.where(il <= jl, 0.0, -1e9)
    blk1 = np.zeros((128, 128))
    blk2 = np.where(jl <= il, 0.0, -1e9)
    C["c_negmask_local"] = np.concatenate([blk0, blk1, blk2], axis=1).astype(np.float32).astype(BF)
    C["c_negmask_diag"] = blk2.astype(np.float32).astype(BF)
    C["c_negmaskT_local01"] = np.concatenate(
        [np.where(k <= q, 0.0, -1e9), np.where(q <= k, 0.0, -1e9)], axis=1
    ).astype(np.float32).astype(BF)
    pat = np.zeros((128, 33), np.float32)
    pat[0:64, 0] = -0.0625
    pat[64:128, 32] = -0.0625
    C["c_favor_pat"] = pat.astype(BF)
    indA = np.zeros((1, 128), np.float32); indA[0, 0:64] = 1.0
    indB = np.zeros((1, 128), np.float32); indB[0, 64:128] = 1.0
    C["c_indA"] = indA.astype(BF)
    C["c_indB"] = indB.astype(BF)
    C["c_ones_1x128"] = np.ones((1, 128), np.float32).astype(BF)
    C["c_ones_1x512"] = np.ones((1, 512), np.float32).astype(BF)
    C["c_ones_128x1"] = np.ones((128, 1), np.float32).astype(BF)
    st0 = np.zeros((128, 36), np.float32); st0[:, 0] = 1.0
    st32 = np.zeros((128, 36), np.float32); st32[:, 32] = 1.0
    C["c_stI0"] = st0.astype(BF)
    C["c_stI32"] = st32.astype(BF)
    DIM_HEAD, WINDOW = 64, 256
    half = np.arange(0, DIM_HEAD, 2, dtype=np.float32)
    inv_freq = (1.0 / 10000.0 ** (half / DIM_HEAD)).astype(np.float32)
    t = np.arange(S, dtype=np.float32)
    freqs = (t[:, None] * inv_freq[None, :]).astype(np.float32)
    freqs = np.concatenate([freqs, freqs], -1)
    base_scale = ((half + 0.4 * DIM_HEAD) / (1.4 * DIM_HEAD)).astype(np.float32)
    power = ((t - S // 2) / (WINDOW // 2)).astype(np.float32)
    scale = (base_scale[None, :] ** power[:, None]).astype(np.float32)
    scale = np.concatenate([scale, scale], -1)
    cos = np.cos(freqs).astype(np.float32)
    sin = np.sin(freqs).astype(np.float32)
    inv = (1.0 / scale).astype(np.float32)

    def lay(tab):
        return np.ascontiguousarray(
            tab.reshape(16, 128, 64).transpose(1, 0, 2)).astype(np.float32)
    sign = np.concatenate([-np.ones(32, np.float32), np.ones(32, np.float32)])
    QK_SCALE = np.float32(8.0)
    C["c_rot_cs_q"] = lay(cos * scale * QK_SCALE)
    C["c_rot_sn_q"] = lay(sin * scale * sign * QK_SCALE)
    C["c_rot_cs_k"] = lay(cos * inv)
    C["c_rot_sn_k"] = lay(sin * inv * sign)
    return C


def prep_inputs(I):
    H = {}
    H["aWq"] = repack_slab_a(I["L0_attn_a_Wq"])
    H["aWk"] = repack_slab_a(I["L0_attn_a_Wk"])
    H["aWv"] = repack_resident_b(I["L0_attn_a_Wv"])
    H["aWo"] = repack_resident_b(I["L0_attn_a_Wo"])
    H["aW1"] = repack_slab_a(I["L0_attn_a_W1"])
    H["aW2"] = repack_slab_a(I["L0_attn_a_W2"])
    H["abq"] = repack_cols(I["L0_attn_a_bq"])
    H["abk"] = repack_cols(I["L0_attn_a_bk"])
    H["abv_row"] = np.ascontiguousarray(I["L0_attn_a_bv"])[:, None, :]
    H["abo"] = repack_cols(I["L0_attn_a_bo"])
    H["abf1"] = repack_cols(I["L0_attn_a_bf1"])
    H["abf2"] = repack_cols(I["L0_attn_a_bf2"])
    H["aln1g"] = repack_cols(I["L0_attn_a_ln1_g"])
    H["aln1b"] = repack_cols(I["L0_attn_a_ln1_b"])
    H["aln2g"] = repack_cols(I["L0_attn_a_ln2_g"])
    H["aln2b"] = repack_cols(I["L0_attn_a_ln2_b"])
    H["aomega"] = np.ascontiguousarray(I["L0_attn_a_omega"])
    H["fWqkv"] = repack_resident_b(I["L0_ffn_Wqkv"])
    H["fWo"] = repack_slab_a(I["L0_ffn_Wo"])
    W1 = I["L0_ffn_W1"]
    a_part = W1[:, :, :2730]
    g_part = W1[:, :, 2730:]
    pad = np.zeros((6, 1024, 86), np.float32)
    W1p = np.concatenate([a_part, pad, g_part, pad], axis=2)
    w1s = repack_slab_a(W1p)
    H["fW1"] = np.ascontiguousarray(
        np.concatenate([w1s[:, :22], w1s[:, 22:]], axis=4))
    b1 = I["L0_ffn_b1"]
    b1p = np.concatenate([b1[:, :2730], np.zeros((6, 86), np.float32),
                          b1[:, 2730:], np.zeros((6, 86), np.float32)], axis=1)
    H["fb1"] = repack_cols(b1p)
    W2 = I["L0_ffn_W2"]
    W2p = np.concatenate([W2, np.zeros((6, 86, 1024), np.float32)], axis=1)
    H["fW2"] = repack_slab_a(W2p)
    H["fb2"] = repack_cols(I["L0_ffn_b2"])
    H["flng_col"] = repack_cols(I["L0_ffn_ln_g"])
    H["flng_row"] = np.ascontiguousarray(
        I["L0_ffn_ln_g"].reshape(6, 1, 1024))
    H["flnb_col"] = repack_cols(I["L0_ffn_ln_b"])
    H["fqs_row"] = np.ascontiguousarray(I["L0_ffn_q_scale"])[:, None, :]
    H["fks_row"] = np.ascontiguousarray(I["L0_ffn_k_scale"])[:, None, :]
    wq = I["L0_attn_b_w_q"]
    wqT = np.ascontiguousarray(wq.transpose(2, 0, 1).reshape(1024, 1024))
    wk = I["L0_attn_b_w_k"]
    wkT = np.ascontiguousarray(wk.transpose(2, 0, 1).reshape(1024, 1024))
    wv = I["L0_attn_b_w_v"]
    wvT = np.ascontiguousarray(wv.transpose(2, 0, 1).reshape(1024, 1024))
    H["bWq"] = repack_resident_b(wqT)
    H["bWk"] = repack_resident_b(wkT)
    H["bWv"] = repack_resident_b(wvT)
    wo = I["L0_attn_b_w_o"]
    H["bWo"] = repack_resident_b(np.ascontiguousarray(wo.T))
    for k in BF16_W:
        H[k] = H[k].astype(BF)
    H.update(build_host_consts())
    return H



ENGINES = ("pe", "dve", "act", "pool", "sync")
ENG_SEM = {"pe": "sPE", "dve": "sDV", "act": "sAC", "pool": "sPL", "sync": "sSY"}


class Prog:
    def __init__(self):
        self.ops = {e: [] for e in ENGINES}
        self.counts = {}
        self.handles = {}
        self.last_wait = {}
        self.dma_n = {"pool": 0, "sync": 0, "act": 0}
        sems = ["sPE", "sDV", "sAC", "sPL", "sSY"]
        for q in ("qW", "qX", "qS"):
            sems += [f"{q}{i}" for i in range(4)]
        for s in sems:
            self.counts[s] = 0

    DMA_SEM = {"pool": "qW", "sync": "qX", "act": "qS"}

    def wait(self, eng, tok):
        if tok is None:
            return
        sem, val = tok
        if val <= 0:
            return
        key = (eng, sem)
        if self.last_wait.get(key, 0) >= val:
            return
        self.last_wait[key] = val
        self.ops[eng].append(("w", sem, val))

    def emit(self, eng, fn, waits=(), inc=None, inc_n=1):
        for t in waits:
            self.wait(eng, t)
        tok = None
        if inc is not None:
            self.counts[inc] += inc_n
            tok = (inc, self.counts[inc])
            self.ops[eng].append(("i", fn, inc, inc_n))
        else:
            self.ops[eng].append(("n", fn))
        return tok

    def dma(self, eng, out_ap, in_ap, waits=()):
        fam = self.DMA_SEM[eng]
        idx = self.dma_n[eng] % 4
        self.dma_n[eng] += 1
        sem = f"{fam}{idx}"
        waits = list(waits) + [(sem, self.counts[sem])]
        return self.emit(eng, lambda e, o=out_ap, i=in_ap: e.dma_start(out=o, in_=i),
                         waits=waits, inc=sem, inc_n=16)

    def run_stream(self, eng_name, engobj):
        for op in self.ops[eng_name]:
            kind = op[0]
            if kind == "w":
                engobj.wait_ge(self.handles[op[1]], op[2])
            elif kind == "i":
                inst = op[1](engobj)
                inst.then_inc(self.handles[op[2]], op[3])
            else:
                op[1](engobj)


class Slot:
    __slots__ = ("ready", "cons")

    def __init__(self):
        self.ready = None
        self.cons = []

    def war(self):
        out = list(self.cons)
        if self.ready is not None:
            out.append(self.ready)
        return out

    def set(self, tok):
        self.ready = tok
        self.cons = []

    def read(self, tok):
        if tok is not None:
            self.cons.append(tok)


class Arena:

    def __init__(self, nc):
        self.nc = nc
        self.off = (nc.sbuf_base + 63) // 64 * 64
        self.budget = nc.sbuf_top
        self.marks = {}

    def alloc(self, name, shape, dtype, align=64):
        per_part = int(np.prod(shape[1:])) * mybir.dt.size(dtype)
        self.off = (self.off + align - 1) // align * align
        t = self.nc.alloc_sbuf_tensor_at(name, list(shape), dtype, offset=self.off)
        self.off += per_part
        assert self.off <= self.budget, f"SBUF overflow at {name}: {self.off}"
        return t

    def mark(self, name):
        self.marks[name] = self.off

    def reset_to(self, name):
        self.off = self.marks[name]


def rep_mid(ap, nrep):
    s = [list(x) for x in ap.ap]
    assert len(s) == 2
    return bass.AP(ap.tensor, ap.offset, [s[0], [0, nrep], s[1]])


def rep_inner(ap, nrep):
    s = [list(x) for x in ap.ap]
    assert len(s) == 2
    return bass.AP(ap.tensor, ap.offset, [s[0], s[1], [0, nrep]])



CONST_SPECS = {
    "c_ident": ([128, 128], BF16), "c_negident": ([128, 128], BF16),
    "c_maskT_diag": ([128, 128], BF16),
    "c_negmask_local": ([128, 384], BF16), "c_negmask_diag": ([128, 128], BF16),
    "c_negmaskT_local01": ([128, 256], BF16),
    "c_favor_pat": ([128, 33], BF16),
    "c_indA": ([1, 128], BF16), "c_indB": ([1, 128], BF16),
    "c_ones_1x128": ([1, 128], BF16), "c_ones_1x512": ([1, 512], BF16),
    "c_ones_128x1": ([128, 1], BF16),
    "c_stI0": ([128, 36], BF16), "c_stI32": ([128, 36], BF16),
}

ROT_SPECS = {
    "c_rot_cs_q": ([128, 16, 64], F32), "c_rot_sn_q": ([128, 16, 64], F32),
    "c_rot_cs_k": ([128, 16, 64], F32), "c_rot_sn_k": ([128, 16, 64], F32),
}

BF16_W = {"aWq", "aWk", "aWv", "aWo", "aW1", "aW2",
          "fWqkv", "fWo", "fW1", "fW2", "bWq", "bWk", "bWv", "bWo"}

WEIGHT_SPECS = {
    "aWq": [3, 8, 128, 8, 128], "aWk": [3, 8, 128, 8, 128],
    "aWv": [3, 128, 8, 1024], "aWo": [3, 128, 8, 1024],
    "aW1": [3, 32, 128, 8, 128], "aW2": [3, 8, 128, 32, 128],
    "abq": [3, 128, 8], "abk": [3, 128, 8], "abv_row": [3, 1, 1024],
    "abo": [3, 128, 8], "abf1": [3, 128, 32], "abf2": [3, 128, 8],
    "aln1g": [3, 128, 8], "aln1b": [3, 128, 8],
    "aln2g": [3, 128, 8], "aln2b": [3, 128, 8],
    "aomega": [3, 64, 32],
    "fWqkv": [6, 128, 8, 3072], "fWo": [6, 8, 128, 8, 128],
    "fW1": [6, 22, 128, 8, 256], "fb1": [6, 128, 44],
    "fW2": [6, 8, 128, 22, 128], "fb2": [6, 128, 8],
    "flng_col": [6, 128, 8], "flng_row": [6, 1, 1024], "flnb_col": [6, 128, 8],
    "fqs_row": [6, 1, 64], "fks_row": [6, 1, 64],
    "bWq": [128, 8, 1024], "bWk": [128, 8, 1024], "bWv": [128, 8, 1024],
    "bWo": [128, 8, 1024],
}


class MK:
    def __init__(self, nseq=SEQ_PER_CORE, phases="ABCD", debug=()):
        self.nseq = nseq
        self.phases = phases
        self.debug = set(debug)
        nc = bass.Bass("TRN2", target_bir_lowering=False)
        self.nc = nc
        self.P = Prog()
        self.xT = nc.dram_tensor("xT", [nseq, D, S], F32, kind="ExternalInput").ap()
        self.outT = nc.dram_tensor("outT", [nseq, D, S], F32, kind="ExternalOutput").ap()
        self.xr = [nc.dram_tensor(f"xr{i}", [D, S], F32, kind="Internal").ap()
                   for i in range(2)]
        self.W = {k: nc.dram_tensor(k, list(v if isinstance(v, list) else v[0]),
                                    BF16 if k in BF16_W else F32,
                                    kind="ExternalInput").ap()
                  for k, v in WEIGHT_SPECS.items()}
        self.C = {k: nc.dram_tensor(k, list(sh), dt, kind="ExternalInput").ap()
                  for k, (sh, dt) in {**CONST_SPECS, **ROT_SPECS}.items()}
        self.dbg = {}
        for name in self.debug:
            if isinstance(name, tuple):
                nm, shape = name
                self.dbg[nm] = nc.dram_tensor(
                    "dbg_" + nm, list(shape), F32, kind="ExternalOutput").ap()
            else:
                self.dbg[name] = nc.dram_tensor(
                    "dbg_" + name, [D, S], F32, kind="ExternalOutput").ap()

        A = Arena(nc)
        self.arena = A
        self.XB = A.alloc("XB", [128, 8 * 2052], BF16)
        self.XL = A.alloc("XL", [128, 8 * 2052], BF16)
        self.cs = {k: A.alloc("s_" + k, list(sh), dt)
                   for k, (sh, dt) in CONST_SPECS.items()}
        self.ccols = A.alloc("ccols", [128, 8], F32)
        self.pcol = A.alloc("pcol", [128, 96], F32)
        self.prow = A.alloc("prow", [1, 1024], BF16)
        self.t2 = A.alloc("t2", [128, 8, 512], F32)
        self.t2b = A.alloc("t2b", [128, 512], BF16)
        self.x2b = A.alloc("x2b", [128, 512], BF16)
        self.srow = A.alloc("srow", [1, 512], BF16)
        self.s2row = A.alloc("s2row", [1, 512], BF16)
        self.negmu_row = A.alloc("negmu_row", [1, 512], BF16)
        self.mu_t = A.alloc("mu_t", [128, 512], F32)
        self.sq_t = A.alloc("sq_t", [128, 512], F32)
        self.rb_t = A.alloc("rb_t", [128, 512], F32)
        self.d1_t = A.alloc("d1_t", [128, 512], F32)
        self.hs_t = A.alloc("hs_t", [128, 512], F32)
        self.xprev = A.alloc("xprev", [128, 512], F32)
        A.mark("phase")

        self.ps = [nc.alloc_psum_tensor(f"psb{i}", [128, 512], F32) for i in range(7)]
        self.psbf = nc.alloc_psum_tensor("psbf", [128, 1024], BF16)
        self.ps_slot = [Slot() for _ in range(8)]

        self.sXB = [[Slot() for _ in range(NT512)] for _ in range(NCHUNK)]
        self.sXL = [[Slot() for _ in range(NT512)] for _ in range(NCHUNK)]
        self.slot = {}
        if "A" in self.phases:
            self.alloc_zone_a()
        if "B" in self.phases or "D" in self.phases:
            self.alloc_zone_b()
        if "C" in self.phases:
            self.alloc_zone_c()

    def S_(self, name):
        if name not in self.slot:
            self.slot[name] = Slot()
        return self.slot[name]

    def xb_ap(self, c, t0, n, shifted=False, lo=False):
        t = self.XL if lo else self.XB
        tile, within = divmod(t0, 512)
        assert within + n <= 512
        off = c * 2052 + tile * 513 + (0 if shifted else 1) + within
        return t[:, off: off + n]

    def prologue(self):
        P = self.P
        toks = []
        for k in self.cs:
            toks.append(P.dma("sync", self.cs[k][:], self.C[k]))
        P.emit("pool", lambda e: e.memset(self.ccols[:, 0:1], LN_EPS))
        P.emit("pool", lambda e: e.memset(self.ccols[:, 1:2], -2.0794415416798357))
        t = P.emit("pool", lambda e: e.memset(self.ccols[:, 2:8], 0.0), inc="sPL")
        P.emit("pool", lambda e: e.memset(
            bass.AP(self.XB, 0, [[8 * 2052, 128], [2052, 8], [513, 4], [1, 1]]), 0.0))
        P.emit("pool", lambda e: e.memset(
            bass.AP(self.XL, 0, [[8 * 2052, 128], [2052, 8], [513, 4], [1, 1]]), 0.0))
        tz = P.emit("pool", lambda e: e.engine_nop(), inc="sPL")
        self.ccols_tok = t
        self.pad_tok = tz
        ctoks = [(f"qX{i}", P.counts[f"qX{i}"]) for i in range(4)]
        for e in ENGINES:
            for ct in ctoks:
                P.wait(e, ct)
            P.wait(e, tz)

    def ln_rowmath(self, stats_bank, bcast_bank, eps_col=None):
        P = self.P
        ps = self.ps[stats_bank]
        pslot = self.ps_slot[stats_bank]
        bb = self.ps[bcast_bank]
        bslot = self.ps_slot[bcast_bank]
        if eps_col is None:
            eps_col = self.ccols[:, 0:1]
        tsr = P.emit("dve", lambda e: e.tensor_copy(self.srow[:], ps[0:1, :]),
                     waits=self.S_("srow").war() + [pslot.ready], inc="sDV")
        self.S_("srow").set(tsr)
        ts2 = P.emit("dve", lambda e: e.tensor_copy(self.s2row[:], ps[32:33, :]),
                     waits=self.S_("s2row").war(), inc="sDV")
        self.S_("s2row").set(ts2)
        pslot.read(ts2)
        tnm = P.emit("dve", lambda e: e.tensor_scalar_mul(
            self.negmu_row[:], self.srow[:], -1.0 / 1024.0),
            waits=self.S_("negmu_row").war() + [tsr], inc="sDV")
        self.S_("negmu_row").set(tnm)
        tba = P.emit("pe", lambda e: e.matmul(
            bb[:, :], self.cs["c_ones_1x128"][:], self.srow[:],
            start=True, stop=True),
            waits=[tsr] + bslot.war(), inc="sPE")
        bslot.set(tba)
        tmu = P.emit("act", lambda e: e.activation(
            self.mu_t[:], bb[:, :], AF.Copy, bias=0.0, scale=1.0 / 1024.0),
            waits=[tba] + self.S_("mu_t").war(), inc="sAC")
        self.S_("mu_t").set(tmu)
        tsq = P.emit("act", lambda e: e.activation(
            self.sq_t[:], bb[:, :], AF.Square, bias=0.0, scale=1.0 / 1024.0),
            waits=[tba] + self.S_("sq_t").war(), inc="sAC")
        self.S_("sq_t").set(tsq)
        bslot.read(tmu)
        bslot.read(tsq)
        tbb = P.emit("pe", lambda e: e.matmul(
            bb[:, :], self.cs["c_ones_1x128"][:], self.s2row[:],
            start=True, stop=True),
            waits=[ts2] + bslot.war(), inc="sPE")
        bslot.set(tbb)
        tv = P.emit("dve", lambda e: e.scalar_tensor_tensor(
            self.sq_t[:], bb[:, :], 1.0 / 1024.0, self.sq_t[:],
            ALU.mult, ALU.subtract),
            waits=[tbb, tsq], inc="sDV")
        self.S_("sq_t").set(tv)
        bslot.read(tv)
        tl = P.emit("act", lambda e: e.activation(
            self.sq_t[:], self.sq_t[:], AF.Ln, bias=eps_col, scale=1.0),
            waits=[tv], inc="sAC")
        self.S_("sq_t").set(tl)
        tr = P.emit("act", lambda e: e.activation(
            self.rb_t[:], self.sq_t[:], AF.Exp, bias=0.0, scale=-0.5),
            waits=[tl] + self.S_("rb_t").war(), inc="sAC")
        self.S_("sq_t").read(tr)
        self.S_("rb_t").set(tr)
        return tmu, tr, tnm

    def stats_mms(self, bank, nc_idx, src_b_ap, src_sq_ap, toks):
        P = self.P
        ps = self.ps[bank]
        first = nc_idx == 0
        t1 = P.emit("pe", lambda e: e.matmul(
            ps[0:36, :], self.cs["c_stI0"][:], src_b_ap,
            start=first, stop=False),
            waits=toks + (self.ps_slot[bank].war() if first else []), inc="sPE")
        if first:
            self.ps_slot[bank].set(t1)
        t2 = P.emit("pe", lambda e: e.matmul(
            ps[0:36, :], self.cs["c_stI32"][:], src_sq_ap,
            start=False, stop=(nc_idx == NCHUNK - 1)),
            waits=toks, inc="sPE")
        self.ps_slot[bank].set(t2)
        return t2


    def stream_split(self, c, t, src_tok):
        P = self.P
        xb = self.xb_ap(c, t * 512, 512)
        xl = self.xb_ap(c, t * 512, 512, lo=True)
        sb, sl = self.sXB[c][t], self.sXL[c][t]
        tb = P.emit("act", lambda e, xb=xb: e.activation(
            xb, self.hs_t[:], AF.Copy, bias=0.0, scale=1.0),
            waits=[src_tok] + sb.war(), inc="sAC")
        if t < NT512 - 1:
            pad = self.XB[:, c * 2052 + (t + 1) * 513: c * 2052 + (t + 1) * 513 + 1]
            tb = P.emit("act", lambda e, pad=pad: e.activation(
                pad, self.hs_t[:, 511:512], AF.Copy, bias=0.0, scale=1.0),
                waits=[src_tok] + self.sXB[c][t + 1].war(), inc="sAC")
            self.sXB[c][t + 1].read(tb)
        sb.set(tb)
        tl = P.emit("pool", lambda e, xb=xb, xl=xl: e.tensor_sub(
            xl, self.hs_t[:], xb),
            waits=[src_tok, tb] + sl.war(), inc="sPL")
        sl.set(tl)
        self.S_("hs_t").read(tl)
        return tb, tl

    def tile_stats(self, c, t2c_ap, t2tok, bank=4):
        P = self.P
        s1 = self.S_("t2b")
        tc = P.emit("act", lambda e, a=t2c_ap: e.activation(
            self.t2b[:], a, AF.Copy, bias=0.0, scale=1.0),
            waits=[t2tok] + s1.war(), inc="sAC")
        s1.set(tc)
        s2 = self.S_("x2b")
        tq = P.emit("act", lambda e, a=t2c_ap: e.activation(
            self.x2b[:], a, AF.Square, bias=0.0, scale=1.0),
            waits=[t2tok] + s2.war(), inc="sAC")
        s2.set(tq)
        tk = self.stats_mms(bank, c, self.t2b[:], self.x2b[:], [tc, tq])
        s1.read(tk)
        s2.read(tk)
        return tk

    def emit_boundary(self, seq, x_prev, x_new, mode):
        P = self.P
        last_out = None
        for t in range(NT512):
            t2tok = [None] * NCHUNK
            for c in range(NCHUNK):
                t2c = self.t2[:, c, :]
                st2 = self.S_(f"t2_{c}")
                xp_ap = x_prev[c * 128:(c + 1) * 128, t * 512:(t + 1) * 512]
                if mode == "init":
                    tok = P.dma("sync", t2c, xp_ap, waits=st2.war())
                else:
                    sxp = self.S_("xprev")
                    tld = P.dma("sync", self.xprev[:], xp_ap, waits=sxp.war())
                    sxp.set(tld)
                    xb = self.xb_ap(c, t * 512, 512)
                    xl = self.xb_ap(c, t * 512, 512, lo=True)
                    sd = self.S_("d1_t")
                    ta = P.emit("dve", lambda e, t2c=t2c, xb=xb, c=c: e.tensor_add(
                        self.d1_t[:], self.xprev[:], xb),
                        waits=[tld, self.sXB[c][t].ready] + sd.war(), inc="sDV")
                    sd.set(ta)
                    sxp.read(ta)
                    self.sXB[c][t].read(ta)
                    tok = P.emit("pool", lambda e, t2c=t2c, xl=xl: e.tensor_add(
                        t2c, self.d1_t[:], xl),
                        waits=[ta, self.sXL[c][t].ready] + st2.war(), inc="sPL")
                    sd.read(tok)
                    self.sXL[c][t].read(tok)
                st2.set(tok)
                if mode != "init":
                    xn_ap = x_new[c * 128:(c + 1) * 128, t * 512:(t + 1) * 512]
                    last_out = P.dma("act", xn_ap, t2c, waits=[tok])
                    st2.read(last_out)
                t2tok[c] = tok
                if mode != "final":
                    tk = self.tile_stats(c, t2c, tok)
                    st2.read(tk)
            if mode == "final":
                continue
            tmu, trb, _ = self.ln_rowmath(4, 5)
            for c in range(NCHUNK):
                t2c = self.t2[:, c, :]
                st2 = self.S_(f"t2_{c}")
                sd = self.S_("d1_t")
                t1 = P.emit("dve", lambda e, t2c=t2c: e.tensor_sub(
                    self.d1_t[:], t2c, self.mu_t[:]),
                    waits=[t2tok[c], tmu] + sd.war(), inc="sDV")
                sd.set(t1)
                st2.read(t1)
                self.S_("mu_t").read(t1)
                sh = self.S_("hs_t")
                t2_ = P.emit("dve", lambda e: e.tensor_mul(
                    self.hs_t[:], self.d1_t[:], self.rb_t[:]),
                    waits=[t1, trb] + sh.war(), inc="sDV")
                sh.set(t2_)
                sd.read(t2_)
                self.S_("rb_t").read(t2_)
                self.stream_split(c, t, t2_)
        return last_out

    def dump_stream(self, name):
        if name not in self.dbg:
            return
        P = self.P
        dst = self.dbg[name]
        for c in range(NCHUNK):
            for t in range(NT512):
                sd = self.S_("d1_t")
                xb = self.xb_ap(c, t * 512, 512)
                xl = self.xb_ap(c, t * 512, 512, lo=True)
                tk = P.emit("dve", lambda e, xb=xb, xl=xl: e.tensor_add(
                    self.d1_t[:], xb, xl),
                    waits=[self.sXB[c][t].ready, self.sXL[c][t].ready] + sd.war(),
                    inc="sDV")
                sd.set(tk)
                self.sXB[c][t].read(tk)
                self.sXL[c][t].read(tk)
                td = P.dma("act", dst[c * 128:(c + 1) * 128, t * 512:(t + 1) * 512],
                           self.d1_t[:], waits=[tk])
                sd.read(td)
                self.last_dbg = td

    def dump_tile(self, name, src_ap, waits, slots=()):
        if name not in self.dbg:
            return
        td = self.P.dma("pool", self.dbg[name][:], src_ap, waits=list(waits))
        for s in slots:
            self.S_(s).read(td)
        self.last_dbg = td


    def build(self):
        P = self.P
        self.last_dbg = None
        self.prologue()
        final_toks = []
        for seq in range(self.nseq):
            xin = self.xT[seq]
            self.emit_boundary(seq, xin, None, "init")
            if seq == 0:
                self.dump_stream("a_in")
            if "A" in self.phases:
                self.phase_attn_a(seq)
                if seq == 0:
                    self.dump_stream("a_out")
            self.emit_boundary(seq, xin, self.xr[0], "stream")
            if seq == 0:
                self.dump_stream("b_in")
            if "B" in self.phases:
                self.phase_ffn(seq, first=True)
            self.emit_boundary(seq, self.xr[0], self.xr[1], "stream")
            if seq == 0:
                self.dump_stream("c_in")
            if "C" in self.phases:
                self.phase_attn_b(seq)
            else:
                self.emit_boundary(seq, self.xr[1], self.xr[0], "stream")
            if seq == 0:
                self.dump_stream("d_in")
            if "D" in self.phases:
                self.phase_ffn(seq, first=False)
            tk = self.emit_boundary(seq, self.xr[0], self.outT[seq], "final")
            final_toks.append(tk)
        P.wait("act", final_toks[-1])
        if self.last_dbg is not None:
            P.wait("act", self.last_dbg)

    def barrier(self):
        toks = [(s, v) for s, v in self.P.counts.items()]
        for e in ENGINES:
            for t in toks:
                self.P.wait(e, t)

    def alloc_zone_a(self):
        A = self.arena
        A.reset_to("phase")
        z = {}
        z["wv"] = A.alloc("za_wv", [128, 8, 1024], BF16)
        z["wo"] = A.alloc("za_wo", [128, 8, 1024], BF16)
        z["slab"] = A.alloc("za_slab", [128, 3, 8, 128], BF16)
        z["bdiag"] = A.alloc("za_bdiag", [128, 128], BF16)
        z["omega"] = A.alloc("za_omega", [64, 32], F32)
        z["qfm"] = A.alloc("za_qfm", [128, 2, 1024], BF16)
        z["kfm"] = A.alloc("za_kfm", [128, 2, 1024], BF16)
        z["Qf"] = A.alloc("za_Qf", [128, 8, 1024], BF16)
        z["Kf"] = A.alloc("za_Kf", [128, 8, 1024], BF16)
        z["vaug"] = A.alloc("za_vaug", [128, 2, 16, 65], BF16)
        z["abuf"] = A.alloc("za_abuf", [128, 8, 1024], BF16)
        z["negh"] = A.alloc("za_negh", [1, 512], BF16)
        z["ktm"] = A.alloc("za_ktm", [128, 128], BF16)
        z["state"] = A.alloc("za_state", [128, 65], BF16)
        z["state_sb"] = A.alloc("za_state_sb", [128, 8, 65], F32)
        z["rec"] = A.alloc("za_rec", [1, 256], BF16)
        z["rb"] = A.alloc("za_rb", [64, 256], BF16)
        z["atsb"] = A.alloc("za_atsb", [128, 256], BF16)
        attn_end = A.off
        A.reset_to("phase")
        z["fslab"] = A.alloc("zaf_slab", [128, 4, 8, 128], BF16)
        z["fslab2"] = A.alloc("zaf_slab2", [128, 4, 32, 128], BF16)
        z["hgr"] = A.alloc("zaf_hgr", [128, 32, 512], BF16)
        A.off = max(A.off, attn_end)
        self.za = z

    def alloc_zone_b(self):
        A = self.arena
        A.reset_to("phase")
        z = {}
        z["wqkv"] = A.alloc("zb_wqkv", [128, 8, 3072], BF16)
        A.mark("bpool")
        z["rot"] = A.alloc("zb_rot", [128, 4, 16, 64], F32)
        z["kroll"] = A.alloc("zb_kroll", [128, 8, 3, 128], BF16)
        z["vroll"] = A.alloc("zb_vroll", [128, 3, 16, 65], BF16)
        z["qcur"] = A.alloc("zb_qcur", [128, 8, 128], BF16)
        z["qhat"] = A.alloc("zb_qhat", [128, 2048], BF16)
        z["sq"] = A.alloc("zb_sq", [128, 1024], BF16)
        z["rsc1"] = A.alloc("zb_rsc1", [128, 1024], F32)
        z["attout"] = A.alloc("zb_attout", [128, 8, 512], BF16)
        z["pt"] = A.alloc("zb_pt", [128, 2, 384], BF16)
        z["sm"] = A.alloc("zb_sm", [128, 768], BF16)
        z["norms"] = A.alloc("zb_norms", [128, 32], F32)
        z["rstd"] = A.alloc("zb_rstd", [128, 32], F32)
        z["mcolf"] = A.alloc("zb_mcolf", [128, 2], F32)
        z["mcol"] = A.alloc("zb_mcol", [128, 2], BF16)
        z["qstile"] = A.alloc("zb_qstile", [128, 128], BF16)
        z["woslab"] = A.alloc("zb_woslab", [128, 2, 8, 128], BF16)
        attn_end = A.off
        A.reset_to("bpool")
        z["gg"] = A.alloc("zbf_gg", [128, 22, 512], BF16)
        z["w1slab"] = A.alloc("zbf_w1slab", [128, 2, 8, 256], BF16)
        z["w2slab"] = A.alloc("zbf_w2slab", [128, 2, 22, 128], BF16)
        z["xg"] = A.alloc("zbf_xg", [128, 8, 512], BF16)
        z["cb2"] = A.alloc("zbf_cb", [128, 44], F32)
        A.off = max(A.off, attn_end)
        self.zb = z

    def alloc_zone_c(self):
        A = self.arena
        A.reset_to("phase")
        z = {}
        z["w1"] = A.alloc("zc_w1", [128, 8, 1024], BF16)
        off_w2 = A.off
        z["w2"] = A.alloc("zc_w2", [128, 8, 1024], BF16)
        z["K"] = A.alloc("zc_K", [128, 8, 2048], BF16)
        z["V"] = A.alloc("zc_V", [128, 16, 16, 65], BF16)
        z["Q"] = A.alloc("zc_Q", [128, 8, 128], BF16)
        z["ao"] = A.alloc("zc_ao", [128, 8, 128], BF16)
        z["pt"] = A.alloc("zc_pt", [128, 2, 128], BF16)
        z["m"] = A.alloc("zc_m", [128, 4], F32)
        z["mb"] = A.alloc("zc_mb", [128, 2], BF16)
        z["xp"] = A.alloc("zc_xp", [128, 128], F32)
        end = A.off
        A.off = off_w2
        z["woslab"] = A.alloc("zc_woslab", [128, 2, 8, 128], BF16)
        A.off = max(A.off, end)
        self.zc = z


    def phase_attn_a(self, seq):
        for l in range(3):
            self.perf_layer(seq, l)

    def perf_layer(self, seq, l):
        P = self.P
        z = self.za
        self.barrier()
        twv = P.dma("pool", z["wv"][:], self.W["aWv"][l])
        two = P.dma("sync", z["wo"][:], self.W["aWo"][l])
        tbv = P.dma("pool", self.prow[:], self.W["abv_row"][l])
        tom = P.dma("sync", z["omega"][:], self.W["aomega"][l])
        pc = []
        for i, nm in enumerate(("abq", "abk", "abo", "aln1g", "aln1b",
                                "aln2g", "aln2b", "abf2")):
            pc.append(P.dma("sync", self.pcol[:, 8 * i: 8 * i + 8], self.W[nm][l]))
        tbf1 = P.dma("sync", self.pcol[:, 64:96], self.W["abf1"][l])
        tpc = pc[-1]
        self.ptoks = pc + [tbf1]
        sc = 0.3535533905932738
        tm0 = P.emit("pool", lambda e: e.memset(z["bdiag"][:], 0.0),
                     waits=[], inc="sPL")
        tb1 = P.emit("act", lambda e: e.activation(
            z["bdiag"][0:64, 0:32], z["omega"][:], AF.Copy, bias=0.0, scale=sc),
            waits=[tom, tm0], inc="sAC")
        tb2 = P.emit("act", lambda e: e.activation(
            z["bdiag"][0:64, 32:64], z["omega"][:], AF.Copy, bias=0.0, scale=-sc),
            waits=[tom, tm0], inc="sAC")
        tb3 = P.emit("act", lambda e: e.activation(
            z["bdiag"][64:128, 64:96], z["omega"][:], AF.Copy, bias=0.0, scale=sc),
            waits=[tom, tm0], inc="sAC")
        tb4 = P.emit("act", lambda e: e.activation(
            z["bdiag"][64:128, 96:128], z["omega"][:], AF.Copy, bias=0.0, scale=-sc),
            waits=[tom, tm0], inc="sAC")
        tbd = tb4
        tvo = P.emit("pool", lambda e: e.memset(
            bass.AP(z["vaug"], 64, [[2080, 128], [65, 2 * 16], [1, 1]]), 1.0),
            inc="sPL")
        tsz = P.emit("pool", lambda e: e.memset(z["state_sb"][:], 0.0),
                     waits=self.S_("a_state_sb").war(), inc="sPL")
        self.S_("a_state_sb").set(tsz)

        for half in range(2):
            hb = half * 1024
            for proj, wname, bcol0, fm, Ff in (
                    ("q", "aWq", 0, z["qfm"], z["Qf"]),
                    ("k", "aWk", 8, z["kfm"], z["Kf"])):
                for ncx in range(NCHUNK):
                    sl = ncx % 3
                    slab = z["slab"][:, sl, :, :]
                    ssl = self.S_(f"a_slab{sl}")
                    tw = P.dma("pool" if ncx % 2 == 0 else "sync",
                               slab, self.W[wname][l, ncx], waits=ssl.war())
                    ssl.set(tw)
                    for t2 in range(2):
                        bank = 4 + t2
                        ps = self.ps[bank]
                        pslot = self.ps_slot[bank]
                        mmtok = None
                        for f in range(NCHUNK):
                            first, last = f == 0, f == NCHUNK - 1
                            rhs = self.xb_ap(f, hb + t2 * 512, 512)
                            w = [tw, self.sXB[f][2 * half + t2].ready]
                            if first:
                                w += pslot.war()
                            mmtok = P.emit("pe", lambda e, ps=ps, slab=slab, f=f,
                                           rhs=rhs, first=first, last=last: e.matmul(
                                ps[:, :], slab[:, f, :], rhs,
                                start=first, stop=last),
                                waits=w, inc="sPE" if last else None)
                            if first:
                                pslot.set(mmtok)
                        pslot.set(mmtok)
                        ssl.read(mmtok)
                        for f in range(NCHUNK):
                            self.sXB[f][2 * half + t2].read(mmtok)
                        sfm = self.S_(f"a_fm_{proj}{ncx % 2}")
                        dst = fm[:, ncx % 2, t2 * 512:(t2 + 1) * 512]
                        te = P.emit("act", lambda e, dst=dst, ps=ps, ncx=ncx,
                                    bcol0=bcol0: e.activation(
                            dst, ps[:, :], AF.Identity,
                            bias=self.pcol[:, bcol0 + ncx: bcol0 + ncx + 1],
                            scale=1.0),
                            waits=[mmtok, tpc] + sfm.war(), inc="sAC")
                        sfm.set(te)
                        pslot.read(te)
                        sq2 = self.S_("x2b")
                        tq2 = P.emit("act", lambda e, dst=dst: e.activation(
                            self.x2b[:], dst, AF.Square, bias=0.0, scale=1.0),
                            waits=[te] + sq2.war(), inc="sAC")
                        sq2.set(tq2)
                        ps6 = self.ps[6]
                        ps6s = self.ps_slot[6]
                        th = P.emit("pe", lambda e: e.matmul(
                            ps6[0:33, 0:512], self.cs["c_favor_pat"][:],
                            self.x2b[:], start=True, stop=True),
                            waits=[tq2] + ps6s.war(), inc="sPE")
                        ps6s.set(th)
                        sq2.read(th)
                        snh = self.S_("a_negh")
                        tnA = P.emit("dve", lambda e: e.tensor_copy(
                            self.za["negh"][:], ps6[0:1, 0:512]),
                            waits=[th] + snh.war(), inc="sDV")
                        snh.set(tnA)
                        tu0 = P.emit("pe", lambda e, ps=ps, dst=dst: e.matmul(
                            ps[:, :], self.za["bdiag"][:], dst,
                            start=True, stop=False),
                            waits=[te, tbd] + pslot.war(), inc=None)
                        tu1 = P.emit("pe", lambda e, ps=ps: e.matmul(
                            ps[:, :], self.cs["c_indA"][:],
                            self.za["negh"][:], start=False, stop=False),
                            waits=[tnA], inc="sPE")
                        snh.read(tu1)
                        tnB = P.emit("dve", lambda e: e.tensor_copy(
                            self.za["negh"][:], ps6[32:33, 0:512]),
                            waits=[th, tu1] + snh.war(), inc="sDV")
                        snh.set(tnB)
                        ps6s.read(tnB)
                        tu2 = P.emit("pe", lambda e, ps=ps: e.matmul(
                            ps[:, :], self.cs["c_indB"][:],
                            self.za["negh"][:], start=False, stop=True),
                            waits=[tnB], inc="sPE")
                        pslot.set(tu2)
                        snh.read(tu2)
                        sF = self.S_(f"a_F_{proj}{ncx}")
                        tF = P.emit("act", lambda e, ps=ps, Ff=Ff, ncx=ncx,
                                    t2=t2: e.activation(
                            Ff[:, ncx, t2 * 512:(t2 + 1) * 512], ps[:, :], AF.Exp,
                            bias=self.ccols[:, 1:2], scale=1.0),
                            waits=[tu2, self.ccols_tok] + sF.war(), inc="sAC")
                        sF.set(tF)
                        pslot.read(tF)
                        sfm.read(tq2)
                        sfm.read(tu2)
            self.perf_scan_half(seq, l, half, twv, tbv, tvo)
            for t2 in range(2):
                self.perf_wo_ln1(seq, l, half, t2, two)
        self.barrier()
        for t in range(NT512):
            self.perf_ffn_tile(seq, l, t)

    def perf_scan_half(self, seq, l, half, twv, tbv, tvo):
        P = self.P
        z = self.za
        hb = half * 1024
        ps4, ps5, ps6 = self.ps[4], self.ps[5], self.ps[6]
        s4, s5, s6 = self.ps_slot[4], self.ps_slot[5], self.ps_slot[6]
        psbf, sbf = self.psbf, self.ps_slot[7]
        for c in range(8):
            ts_abs = 8 * half + c
            vslot = c % 2
            sv = self.S_(f"a_vaug{vslot}")
            for nh in range(2):
                mmtok = None
                for f in range(NCHUNK):
                    first = f == 0
                    lhs = self.xb_ap(f, hb + c * 128, 128)
                    w = [twv, self.sXB[f][ts_abs // 4].ready]
                    if first:
                        w += s6.war()
                    mmtok = P.emit("pe", lambda e, f=f, lhs=lhs, nh=nh: e.matmul(
                        ps6[:, 0:512], lhs, z["wv"][:, f, nh * 512:(nh + 1) * 512],
                        start=f == 0, stop=False), waits=w, inc=None)
                    if first:
                        s6.set(mmtok)
                    self.sXB[f][ts_abs // 4].read(mmtok)
                tb = P.emit("pe", lambda e, nh=nh: e.matmul(
                    ps6[:, 0:512], self.cs["c_ones_1x128"][:],
                    self.prow[0:1, nh * 512:(nh + 1) * 512],
                    start=False, stop=True), waits=[tbv], inc="sPE")
                s6.set(tb)
                for f in range(NCHUNK):
                    self.sXB[f][ts_abs // 4].read(tb)
                te = P.emit("dve", lambda e, nh=nh, vslot=vslot: e.tensor_copy(
                    z["vaug"][:, vslot, nh * 8:(nh + 1) * 8, 0:64],
                    ps6[:, 0:512].rearrange("p (h d) -> p h d", h=8)),
                    waits=[tb, tvo] + (sv.war() if nh == 0 else []), inc="sDV")
                if nh == 0:
                    sv.set(te)
                s6.read(te)
            sv.set(te)
            for hp in range(8):
                sQ = self.S_(f"a_F_q{hp}")
                sK = self.S_(f"a_F_k{hp}")
                qA = z["Qf"][0:64, hp, c * 128:(c + 1) * 128]
                qB = z["Qf"][64:128, hp, c * 128:(c + 1) * 128]
                kA = z["Kf"][0:64, hp, c * 128:(c + 1) * 128]
                kB = z["Kf"][64:128, hp, c * 128:(c + 1) * 128]
                first_chunk = (half == 0 and c == 0)
                tt_ = P.emit("pe", lambda e, hp=hp, c=c: e.transpose(
                    psbf[:, hp * 128:(hp + 1) * 128],
                    z["Kf"][:, hp, c * 128:(c + 1) * 128], self.cs["c_ident"][:]),
                    waits=[sK.ready] + sbf.war(), inc="sPE")
                sbf.set(tt_)
                sK.read(tt_)
                sktm = self.S_("a_ktm")
                tk = P.emit("dve", lambda e, hp=hp: e.tensor_copy(
                    z["ktm"][:], psbf[:, hp * 128:(hp + 1) * 128]),
                    waits=[tt_] + sktm.war(), inc="sDV")
                sktm.set(tk)
                sbf.read(tk)
                sst = self.S_("a_state")
                ssb = self.S_("a_state_sb")
                if not first_chunk:
                    tse = P.emit("dve", lambda e, hp=hp: e.tensor_copy(
                        z["state"][:], z["state_sb"][:, hp, :]),
                        waits=[ssb.ready] + sst.war(), inc="sDV")
                    sst.set(tse)
                    ssb.read(tse)
                su1 = P.emit("pe", lambda e, vslot=vslot, hp=hp: e.matmul(
                    ps5[0:64, 256:321], z["ktm"][:, 0:64],
                    z["vaug"][:, vslot, 2 * hp, :],
                    start=True, stop=True),
                    waits=[tk, sv.ready] + s5.war(), inc=None)
                su2 = P.emit("pe", lambda e, vslot=vslot, hp=hp: e.matmul(
                    ps5[0:64, 321:386], z["ktm"][:, 64:128],
                    z["vaug"][:, vslot, 2 * hp + 1, :],
                    start=True, stop=True),
                    waits=[], inc="sPE")
                s5.set(su2)
                sktm.read(su2)
                sv.read(su2)
                tua = P.emit("dve", lambda e, hp=hp: e.tensor_add(
                    z["state_sb"][0:64, hp, :], z["state_sb"][0:64, hp, :],
                    ps5[0:64, 256:321]),
                    waits=[su2] + ([sst.ready] if not first_chunk else []) +
                          ssb.war(), inc="sDV")
                tub = P.emit("dve", lambda e, hp=hp: e.tensor_add(
                    z["state_sb"][64:128, hp, :], z["state_sb"][64:128, hp, :],
                    ps5[0:64, 321:386]),
                    waits=[su2], inc="sDV")
                ssb.set(tub)
                s5.read(tub)
                ta1 = P.emit("pe", lambda e, kA=kA, qA=qA: e.matmul(
                    ps4[:, 0:128], kA, qA, start=True, stop=True,
                    tile_position=(0, 0)),
                    waits=[sK.ready, sQ.ready] + s4.war(), inc="sPE")
                s4.set(ta1)
                ta2 = P.emit("pe", lambda e, kB=kB, qB=qB: e.matmul(
                    ps5[:, 0:128], kB, qB, start=True, stop=True,
                    tile_position=(64, 0)),
                    waits=[sK.ready, sQ.ready] + s5.war(), inc="sPE")
                s5.set(ta2)
                sK.read(ta1); sK.read(ta2); sQ.read(ta1); sQ.read(ta2)
                sat = self.S_("a_atsb")
                tm1 = P.emit("dve", lambda e: e.tensor_mul(
                    z["atsb"][:, 0:128], ps4[:, 0:128], self.cs["c_maskT_diag"][:]),
                    waits=[ta1] + sat.war(), inc="sDV")
                sat.set(tm1)
                s4.read(tm1)
                tm2 = P.emit("dve", lambda e: e.tensor_mul(
                    z["atsb"][:, 128:256], ps5[:, 0:128], self.cs["c_maskT_diag"][:]),
                    waits=[ta2], inc="sDV")
                sat.set(tm2)
                s5.read(tm2)
                tfin = None
                for hh in range(2):
                    c0 = 256 + 128 * hh
                    vh = 2 * hp + hh
                    at0 = 128 * hh
                    qh = qA if hh == 0 else qB
                    last = (hh == 1) and first_chunk
                    tfin = P.emit("pe", lambda e, c0=c0, vh=vh, at0=at0,
                                  vslot=vslot, fc=first_chunk: e.matmul(
                        ps4[0:65, c0:c0 + 128], z["vaug"][:, vslot, vh, :],
                        z["atsb"][:, at0:at0 + 128],
                        start=True, stop=fc),
                        waits=[tm1, tm2], inc="sPE" if last else None)
                    if not first_chunk:
                        last = (hh == 1)
                        tfin = P.emit("pe", lambda e, c0=c0, hh=hh, qh=qh: e.matmul(
                            ps4[0:65, c0:c0 + 128],
                            z["state"][64 * hh:64 * hh + 64, :],
                            qh, start=False, stop=True),
                            waits=[sst.ready], inc="sPE" if last else None)
                sv.read(tfin)
                sat.read(tfin)
                sQ.read(tfin)
                if not first_chunk:
                    sst.read(tfin)
                s4.set(tfin)
                sden = self.S_("sq_t")
                td = P.emit("dve", lambda e: e.tensor_scalar_add(
                    self.sq_t[0:1, 0:256], ps4[64:65, 256:512], 1e-6),
                    waits=[tfin] + sden.war(), inc="sDV")
                sden.set(td)
                s4.read(td)
                trc0 = P.emit("dve", lambda e: e.reciprocal(
                    self.sq_t[0:1, 256:512], self.sq_t[0:1, 0:256]),
                    waits=[td], inc="sDV")
                sden.set(trc0)
                srec = self.S_("a_rec")
                trc = P.emit("dve", lambda e: e.tensor_copy(
                    z["rec"][:], self.sq_t[0:1, 256:512]),
                    waits=[trc0] + srec.war(), inc="sDV")
                srec.set(trc)
                sden.read(trc)
                trb = P.emit("pe", lambda e: e.matmul(
                    ps5[0:64, 128:384], self.cs["c_ones_1x128"][0:1, 0:64],
                    z["rec"][:], start=True, stop=True),
                    waits=[trc] + s5.war(), inc="sPE")
                s5.set(trb)
                srec.read(trb)
                srb = self.S_("a_rb")
                trbe = P.emit("dve", lambda e: e.tensor_copy(
                    z["rb"][:], ps5[0:64, 128:384]),
                    waits=[trb] + srb.war(), inc="sDV")
                srb.set(trbe)
                s5.read(trbe)
                sab = self.S_(f"a_abuf{hp}")
                tf1 = P.emit("dve", lambda e, hp=hp, c=c: e.tensor_mul(
                    z["abuf"][0:64, hp, c * 128:(c + 1) * 128],
                    ps4[0:64, 256:384], z["rb"][0:64, 0:128]),
                    waits=[trbe] + sab.war(), inc="sDV")
                s4.read(tf1)
                tf2 = P.emit("dve", lambda e, hp=hp, c=c: e.tensor_mul(
                    z["abuf"][64:128, hp, c * 128:(c + 1) * 128],
                    ps4[0:64, 384:512], z["rb"][0:64, 128:256]),
                    waits=[trbe], inc="sDV")
                s4.read(tf2)
                srb.read(tf2)
                sab.set(tf2)

    def perf_wo_ln1(self, seq, l, half, t2, two):
        P = self.P
        z = self.za
        t_abs = 2 * half + t2
        ps4, s4 = self.ps[4], self.ps_slot[4]
        for ncx in range(NCHUNK):
            mmtok = None
            for f in range(NCHUNK):
                first, last = f == 0, f == NCHUNK - 1
                w = [two, self.S_(f"a_abuf{f}").ready]
                if first:
                    w += s4.war()
                mmtok = P.emit("pe", lambda e, f=f, ncx=ncx, t2=t2: e.matmul(
                    ps4[:, 0:512], z["wo"][:, f, ncx * 128:(ncx + 1) * 128],
                    z["abuf"][:, f, t2 * 512:(t2 + 1) * 512],
                    start=f == 0, stop=f == NCHUNK - 1),
                    waits=w, inc="sPE" if last else None)
                if first:
                    s4.set(mmtok)
                self.S_(f"a_abuf{f}").read(mmtok)
            s4.set(mmtok)
            sd = self.S_("d1_t")
            xb = self.xb_ap(ncx, t_abs * 512, 512)
            xl = self.xb_ap(ncx, t_abs * 512, 512, lo=True)
            p1 = P.emit("dve", lambda e, ncx=ncx, xb=xb: e.scalar_tensor_tensor(
                self.d1_t[:], ps4[:, 0:512], self.pcol[:, 16 + ncx:17 + ncx],
                xb, ALU.add, ALU.add),
                waits=[mmtok, self.sXB[ncx][t_abs].ready] + self.ptoks + sd.war(),
                inc="sDV")
            sd.set(p1)
            s4.read(p1)
            self.sXB[ncx][t_abs].read(p1)
            st2 = self.S_(f"t2_{ncx}")
            p2 = P.emit("pool", lambda e, ncx=ncx, xl=xl: e.tensor_add(
                self.t2[:, ncx, :], self.d1_t[:], xl),
                waits=[p1, self.sXL[ncx][t_abs].ready] + st2.war(), inc="sPL")
            st2.set(p2)
            sd.read(p2)
            self.sXL[ncx][t_abs].read(p2)
            tk = self.tile_stats(ncx, self.t2[:, ncx, :], p2, bank=6)
            st2.read(tk)
        tmu, trb, _ = self.ln_rowmath(6, 5)
        for ncx in range(NCHUNK):
            st2 = self.S_(f"t2_{ncx}")
            sd = self.S_("d1_t")
            ta = P.emit("dve", lambda e, ncx=ncx: e.tensor_sub(
                self.d1_t[:], self.t2[:, ncx, :], self.mu_t[:]),
                waits=[st2.ready, tmu] + sd.war(), inc="sDV")
            sd.set(ta)
            st2.read(ta)
            self.S_("mu_t").read(ta)
            sh = self.S_("hs_t")
            tb = P.emit("dve", lambda e: e.tensor_mul(
                self.hs_t[:], self.d1_t[:], self.rb_t[:]),
                waits=[ta, trb] + sh.war(), inc="sDV")
            sd.read(tb)
            self.S_("rb_t").read(tb)
            tc = P.emit("pool", lambda e, ncx=ncx: e.tensor_scalar(
                self.hs_t[:], self.hs_t[:], self.pcol[:, 24 + ncx:25 + ncx],
                self.pcol[:, 32 + ncx:33 + ncx], ALU.mult, ALU.add),
                waits=[tb] + self.ptoks, inc="sPL")
            sh.set(tc)
            self.stream_split(ncx, t_abs, tc)

    def perf_ffn_tile(self, seq, l, t):
        P = self.P
        z = self.za
        for g in range(32):
            sl = g % 4
            slab = z["fslab"][:, sl, :, :]
            ssl = self.S_(f"af1_slab{sl}")
            tw = P.dma("pool" if g % 2 == 0 else "sync",
                       slab, self.W["aW1"][l, g], waits=ssl.war())
            ssl.set(tw)
            bank = g % 2
            ps, pslot = self.ps[bank], self.ps_slot[bank]
            mmtok = None
            for f in range(NCHUNK):
                first, last = f == 0, f == NCHUNK - 1
                rhs = self.xb_ap(f, t * 512, 512)
                w = [tw, self.sXB[f][t].ready]
                if first:
                    w += pslot.war()
                mmtok = P.emit("pe", lambda e, slab=slab, f=f, rhs=rhs,
                               ps=ps: e.matmul(
                    ps[:, 0:512], slab[:, f, :], rhs,
                    start=f == 0, stop=f == NCHUNK - 1),
                    waits=w, inc="sPE" if last else None)
                if first:
                    pslot.set(mmtok)
                self.sXB[f][t].read(mmtok)
            pslot.set(mmtok)
            ssl.read(mmtok)
            sh = self.S_(f"a_hgr{g}")
            te = P.emit("act", lambda e, g=g, ps=ps: e.activation(
                z["hgr"][:, g, :], ps[:, 0:512], AF.Relu,
                bias=self.pcol[:, 64 + g:65 + g], scale=1.0),
                waits=[mmtok] + self.ptoks + sh.war(), inc="sAC")
            sh.set(te)
            pslot.read(te)
        for ncx in range(NCHUNK):
            sl = ncx % 4
            slab2 = z["fslab2"][:, sl, :, :]
            ssl = self.S_(f"af2_slab{sl}")
            tw = P.dma("pool" if ncx % 2 == 0 else "sync",
                       slab2, self.W["aW2"][l, ncx], waits=ssl.war())
            ssl.set(tw)
            bank = 2 + ncx % 2
            ps, pslot = self.ps[bank], self.ps_slot[bank]
            mmtok = None
            for g in range(32):
                first, last = g == 0, g == 31
                w = [tw, self.S_(f"a_hgr{g}").ready]
                if first:
                    w += pslot.war()
                mmtok = P.emit("pe", lambda e, slab2=slab2, g=g, ps=ps: e.matmul(
                    ps[:, 0:512], slab2[:, g, :], z["hgr"][:, g, :],
                    start=g == 0, stop=g == 31),
                    waits=w, inc="sPE" if last else None)
                if first:
                    pslot.set(mmtok)
                self.S_(f"a_hgr{g}").read(mmtok)
            pslot.set(mmtok)
            ssl.read(mmtok)
            sd = self.S_("d1_t")
            xb = self.xb_ap(ncx, t * 512, 512)
            xl = self.xb_ap(ncx, t * 512, 512, lo=True)
            p1 = P.emit("dve", lambda e, ncx=ncx, xb=xb, ps=ps: e.scalar_tensor_tensor(
                self.d1_t[:], ps[:, 0:512], self.pcol[:, 56 + ncx:57 + ncx],
                xb, ALU.add, ALU.add),
                waits=[mmtok, self.sXB[ncx][t].ready] + self.ptoks + sd.war(),
                inc="sDV")
            sd.set(p1)
            pslot.read(p1)
            self.sXB[ncx][t].read(p1)
            st2 = self.S_(f"t2_{ncx}")
            p2 = P.emit("pool", lambda e, ncx=ncx, xl=xl: e.tensor_add(
                self.t2[:, ncx, :], self.d1_t[:], xl),
                waits=[p1, self.sXL[ncx][t].ready] + st2.war(), inc="sPL")
            st2.set(p2)
            sd.read(p2)
            self.sXL[ncx][t].read(p2)
            tk = self.tile_stats(ncx, self.t2[:, ncx, :], p2, bank=6)
            st2.read(tk)
        tmu, trb, _ = self.ln_rowmath(6, 5)
        for ncx in range(NCHUNK):
            st2 = self.S_(f"t2_{ncx}")
            sd = self.S_("d1_t")
            ta = P.emit("dve", lambda e, ncx=ncx: e.tensor_sub(
                self.d1_t[:], self.t2[:, ncx, :], self.mu_t[:]),
                waits=[st2.ready, tmu] + sd.war(), inc="sDV")
            sd.set(ta)
            st2.read(ta)
            self.S_("mu_t").read(ta)
            sh = self.S_("hs_t")
            tb = P.emit("dve", lambda e: e.tensor_mul(
                self.hs_t[:], self.d1_t[:], self.rb_t[:]),
                waits=[ta, trb] + sh.war(), inc="sDV")
            sd.read(tb)
            self.S_("rb_t").read(tb)
            tc = P.emit("pool", lambda e, ncx=ncx: e.tensor_scalar(
                self.hs_t[:], self.hs_t[:], self.pcol[:, 40 + ncx:41 + ncx],
                self.pcol[:, 48 + ncx:49 + ncx], ALU.mult, ALU.add),
                waits=[tb] + self.ptoks, inc="sPL")
            sh.set(tc)
            self.stream_split(ncx, t, tc)

    def phase_ffn(self, seq, first):
        for l in range(6):
            self.ffn_layer(seq, l)

    def ffn_layer(self, seq, l):
        P = self.P
        z = self.zb
        self.barrier()
        wq_toks = []
        for f in range(NCHUNK):
            wq_toks.append(P.dma("pool" if f % 2 == 0 else "sync",
                                 z["wqkv"][:, f, :],
                                 self.W["fWqkv"][l, :, f, :]))
        rot_toks = [P.dma("sync", z["rot"][:, i, :, :], self.C[nm])
                    for i, nm in enumerate(("c_rot_cs_q", "c_rot_sn_q",
                                            "c_rot_cs_k", "c_rot_sn_k"))]
        pc = [P.dma("sync", self.pcol[:, 0:44], self.W["fb1"][l]),
              P.dma("sync", self.pcol[:, 44:52], self.W["fb2"][l]),
              P.dma("sync", self.pcol[:, 52:60], self.W["flng_col"][l]),
              P.dma("sync", self.pcol[:, 60:68], self.W["flnb_col"][l])]
        tgr = P.dma("pool", self.prow[:, 0:1024], self.W["flng_row"][l])
        tlb = P.dma("pool", self.t2b[:, 0:8], self.W["flnb_col"][l])
        self.S_("t2b").set(tlb)
        self.ptoks = pc + [tgr, tlb]
        tq1 = P.dma("pool", self.x2b[0:1, 0:64], self.W["fqs_row"][l])
        tq2 = P.dma("pool", self.x2b[0:1, 64:128], self.W["fks_row"][l])
        ps6, s6 = self.ps[6], self.ps_slot[6]
        tb1 = P.emit("pe", lambda e: e.matmul(
            ps6[:, 0:128], self.cs["c_ones_1x128"][:], self.x2b[0:1, 0:128],
            start=True, stop=True), waits=[tq1, tq2] + s6.war(), inc="sPE")
        s6.set(tb1)
        sqs = self.S_("b_qstile")
        te1 = P.emit("dve", lambda e: e.tensor_copy(z["qstile"][:], ps6[:, 0:128]),
                     waits=[tb1] + sqs.war(), inc="sDV")
        sqs.set(te1)
        s6.read(te1)
        srs = self.S_("b_rsc1")
        tsh = None
        for half0 in (0, 1):
            for j0 in (0, 32):
                src_c = half0 * 64 + ((j0 + 32) % 64)
                dst_c = half0 * 64 + j0
                tsh = P.emit("pool", lambda e, src_c=src_c, dst_c=dst_c: e.tensor_copy(
                    self.zb["rsc1"][:, dst_c:dst_c + 32],
                    z["qstile"][:, src_c:src_c + 32]),
                    waits=[te1] + (srs.war() if (half0 == 0 and j0 == 0) else []),
                    inc="sPL")
        srs.set(tsh)
        sqs.read(tsh)
        srot = self.S_("b_rot")
        trot = None
        for i, (col0, from_sh) in enumerate(((0, False), (0, True),
                                             (64, False), (64, True))):
            srctile = self.zb["rsc1"] if from_sh else z["qstile"]
            trot = P.emit("pool", lambda e, i=i, col0=col0, srctile=srctile: e.tensor_tensor(
                z["rot"][:, i, :, :], z["rot"][:, i, :, :],
                bass.AP(srctile, col0, [[srctile.shape[1], 128], [0, 16], [1, 64]]),
                ALU.mult),
                waits=[rot_toks[i], te1, tsh] + (srot.war() if i == 0 else []),
                inc="sPL")
        srot.set(trot)
        srs.read(trot)
        sqs.read(trot)
        tvo = P.emit("pool", lambda e: e.memset(
            bass.AP(z["vroll"], 64, [[3120, 128], [65, 3 * 16], [1, 1]]), 1.0),
            waits=self.S_("b_vroll").war(), inc="sPL")
        self.S_("b_vroll").set(tvo)
        tsm0 = P.emit("pool", lambda e: e.memset(z["sm"][:], 0.0),
                      waits=self.S_("b_sm").war(), inc="sPL")
        self.S_("b_sm").set(tsm0)
        tpt0 = P.emit("pool", lambda e: e.memset(z["pt"][:], 0.0),
                      waits=self.S_("b_pt").war(), inc="sPL")
        self.S_("b_pt").set(tpt0)

        for ts in range(NTSUB):
            cb = None
            if ts % 4 == 0 and ts > 0:
                cb = (lambda t=ts // 4 - 1: self.ffn_wo_resid(seq, l, t))
            self.ffn_attn_tsub(seq, l, ts, wq_toks, trot, mid_cb=cb)
        self.ffn_wo_resid(seq, l, 3)
        if seq == 0 and l == 0:
            self.dump_stream("b_l0a")
        self.barrier()
        for t in range(NT512 - 1, -1, -1):
            self.ffn_geglu_tile(seq, l, t)
        if seq == 0 and l == 0:
            self.dump_stream("b_l0")

    def phase_attn_b(self, seq):
        P = self.P
        z = self.zc
        self.barrier()
        wk_toks = [P.dma("pool" if f % 2 == 0 else "sync",
                         z["w1"][:, f, :], self.W["bWk"][:, f, :])
                   for f in range(NCHUNK)]
        wv_toks = [P.dma("sync" if f % 2 == 0 else "pool",
                         z["w2"][:, f, :], self.W["bWv"][:, f, :])
                   for f in range(NCHUNK)]
        sK = self.S_("c_K")
        for nc in range(NCHUNK):
            for t in range(NT512):
                bank = (nc * NT512 + t) % 2
                ps, pslot = self.ps[bank], self.ps_slot[bank]
                mmtok = None
                for f in range(NCHUNK):
                    w = [wk_toks[f], self.sXB[f][t].ready]
                    if f == 0:
                        w += pslot.war()
                    mmtok = P.emit("pe", lambda e, f=f, nc=nc, t=t, ps=ps: e.matmul(
                        ps[:, 0:512], z["w1"][:, f, nc * 128:(nc + 1) * 128],
                        self.xb_ap(f, t * 512, 512),
                        start=f == 0, stop=f == NCHUNK - 1),
                        waits=w, inc="sPE" if f == NCHUNK - 1 else None)
                    if f == 0:
                        pslot.set(mmtok)
                pslot.set(mmtok)
                for f in range(NCHUNK):
                    self.sXB[f][t].read(mmtok)
                te = P.emit("act", lambda e, nc=nc, t=t, ps=ps: e.activation(
                    z["K"][:, nc, t * 512:(t + 1) * 512], ps[:, 0:512],
                    AF.Identity, bias=0.0, scale=1.0),
                    waits=[mmtok] + (sK.war() if (nc == 0 and t == 0) else []),
                    inc="sAC")
                pslot.read(te)
        sK.set(te)
        sV = self.S_("c_V")
        tvo = P.emit("pool", lambda e: e.memset(
            bass.AP(z["V"], 64, [[16640, 128], [65, 16 * 16], [1, 1]]), 1.0),
            waits=sV.war(), inc="sPL")
        sV.set(tvo)
        for ts in range(NTSUB):
            for nh in range(2):
                bank = 2 + (ts * 2 + nh) % 2
                ps, pslot = self.ps[bank], self.ps_slot[bank]
                mmtok = None
                for f in range(NCHUNK):
                    w = [wv_toks[f], self.sXB[f][ts // 4].ready]
                    if f == 0:
                        w += pslot.war()
                    mmtok = P.emit("pe", lambda e, f=f, ts=ts, nh=nh, ps=ps: e.matmul(
                        ps[:, 0:512], self.xb_ap(f, ts * 128, 128),
                        z["w2"][:, f, nh * 512:(nh + 1) * 512],
                        start=f == 0, stop=f == NCHUNK - 1),
                        waits=w, inc="sPE" if f == NCHUNK - 1 else None)
                    if f == 0:
                        pslot.set(mmtok)
                pslot.set(mmtok)
                for f in range(NCHUNK):
                    self.sXB[f][ts // 4].read(mmtok)
                tev = P.emit("dve", lambda e, ts=ts, nh=nh, ps=ps: e.tensor_copy(
                    z["V"][:, ts, nh * 8:(nh + 1) * 8, 0:64],
                    ps[:, 0:512].rearrange("p (h d) -> p h d", h=8)),
                    waits=[mmtok, tvo], inc="sDV")
                pslot.read(tev)
                sV.read(tev)
        sV.set(tev)
        self.barrier()
        wq_toks = [P.dma("pool" if f % 2 == 0 else "sync",
                         z["w1"][:, f, :], self.W["bWq"][:, f, :])
                   for f in range(NCHUNK)]
        for T in range(NTSUB):
            t512 = T // 4
            sQ = self.S_("c_Q")
            for nc in range(NCHUNK):
                bank = nc % 2
                ps, pslot = self.ps[bank], self.ps_slot[bank]
                mmtok = None
                for f in range(NCHUNK):
                    w = [wq_toks[f], self.sXB[f][t512].ready]
                    if f == 0:
                        w += pslot.war()
                    mmtok = P.emit("pe", lambda e, f=f, nc=nc, T=T, ps=ps: e.matmul(
                        ps[:, 0:128], z["w1"][:, f, nc * 128:(nc + 1) * 128],
                        self.xb_ap(f, T * 128, 128),
                        start=f == 0, stop=f == NCHUNK - 1),
                        waits=w, inc="sPE" if f == NCHUNK - 1 else None)
                    if f == 0:
                        pslot.set(mmtok)
                pslot.set(mmtok)
                for f in range(NCHUNK):
                    self.sXB[f][t512].read(mmtok)
                tqe = P.emit("act", lambda e, nc=nc, ps=ps: e.activation(
                    z["Q"][:, nc, :], ps[:, 0:128], AF.Identity,
                    bias=0.0, scale=0.125),
                    waits=[mmtok] + (sQ.war() if nc == 0 else []), inc="sAC")
                pslot.read(tqe)
            sQ.set(tqe)
            sao = self.S_("c_ao")
            for hp in range(NCHUNK):
                sm_col = self.S_("c_m")
                ngrp = (T + 4) // 4
                tred = None
                for g in range(ngrp):
                    kb0 = g * 4
                    nk = min(4, T + 1 - kb0)
                    smm = [None, None]
                    for hh in range(2):
                        bank = hh
                        ps, pslot = self.ps[bank], self.ps_slot[bank]
                        mmtok = None
                        for i in range(nk):
                            kb = kb0 + i
                            w = [tqe, sK.ready]
                            if i == 0:
                                w += pslot.war()
                            mmtok = P.emit("pe", lambda e, hh=hh, hp=hp, kb=kb,
                                           i=i, ps=ps: e.matmul(
                                ps[:, i * 128:(i + 1) * 128],
                                z["Q"][64 * hh:64 * hh + 64, hp, :],
                                z["K"][64 * hh:64 * hh + 64, hp,
                                       kb * 128:(kb + 1) * 128],
                                start=True, stop=True,
                                tile_position=(64 * hh, 0)),
                                waits=w, inc="sPE" if i == nk - 1 else None)
                            if i == 0:
                                pslot.set(mmtok)
                        pslot.set(mmtok)
                        sQ.read(mmtok)
                        sK.read(mmtok)
                        smm[hh] = mmtok
                    if kb0 + nk - 1 == T:
                        for hh in range(2):
                            ps, pslot = self.ps[hh], self.ps_slot[hh]
                            tmsk = P.emit("dve", lambda e, hh=hh, nk=nk, ps=ps:
                                          e.scalar_tensor_tensor(
                                ps[:, (nk - 1) * 128:nk * 128],
                                ps[:, (nk - 1) * 128:nk * 128], 1.0,
                                self.cs["c_negmask_diag"][:],
                                ALU.mult, ALU.add),
                                waits=[smm[hh]], inc="sDV")
                            self.ps_slot[hh].set(tmsk)
                            smm[hh] = tmsk
                    for hh in range(2):
                        ps = self.ps[hh]
                        dst = z["m"][:, 2 + hh:3 + hh] if g > 0 else z["m"][:, hh:hh + 1]
                        tred = P.emit("dve", lambda e, hh=hh, nk=nk, g=g,
                                      dst=dst, ps=ps: e.tensor_reduce(
                            dst, ps[:, 0:nk * 128], AX.X, ALU.max),
                            waits=[smm[hh]] + (sm_col.war() if (g == 0 and hh == 0)
                                               else []), inc="sDV")
                        self.ps_slot[hh].read(tred)
                        if g > 0:
                            tred = P.emit("dve", lambda e, hh=hh: e.tensor_max(
                                z["m"][:, hh:hh + 1], z["m"][:, hh:hh + 1],
                                z["m"][:, 2 + hh:3 + hh]),
                                waits=[tred], inc="sDV")
                    sm_col.set(tred)
                tcb = P.emit("pool", lambda e: e.tensor_copy(z["mb"][:], z["m"][:, 0:2]),
                             waits=[tred] + self.S_("c_mb").war(), inc="sPL")
                self.S_("c_mb").set(tcb)
                sm_col.read(tcb)
                ps2, s2 = self.ps[2], self.ps_slot[2]
                tr1 = P.emit("pe", lambda e: e.matmul(
                    ps2[0:1, 0:128], z["mb"][:, 0:1], self.cs["c_negident"][:],
                    start=True, stop=True), waits=[tcb] + s2.war(), inc=None)
                tr2 = P.emit("pe", lambda e: e.matmul(
                    ps2[0:1, 128:256], z["mb"][:, 1:2], self.cs["c_negident"][:],
                    start=True, stop=True), waits=[], inc="sPE")
                s2.set(tr2)
                self.S_("c_mb").read(tr2)
                smr = self.S_("x2b")
                tmr = P.emit("dve", lambda e: e.tensor_copy(
                    self.x2b[0:1, 256:512], ps2[0:1, 0:256]),
                    waits=[tr2] + smr.war(), inc="sDV")
                smr.set(tmr)
                s2.read(tmr)
                ps4, s4 = self.ps[4], self.ps_slot[4]
                ps6b, s6b = self.ps[6], self.ps_slot[6]
                avps = (ps4, ps6b)
                avsl = (s4, s6b)
                spt = self.S_("c_pt")
                av_end = None
                first_av = [True, True]
                for kb in range(T + 1):
                    ex_toks = []
                    for hh in range(2):
                        bank = 2 + hh
                        ps, pslot = self.ps[bank], self.ps_slot[bank]
                        t1 = P.emit("pe", lambda e, hh=hh, hp=hp, kb=kb,
                                    ps=ps: e.matmul(
                            ps[:, 0:128],
                            z["K"][64 * hh:64 * hh + 64, hp,
                                   kb * 128:(kb + 1) * 128],
                            z["Q"][64 * hh:64 * hh + 64, hp, :],
                            start=True, stop=False,
                            tile_position=(64 * hh, 0)),
                            waits=[tqe, sK.ready, tmr] + pslot.war(), inc=None)
                        t2 = P.emit("pe", lambda e, hh=hh, ps=ps: e.matmul(
                            ps[:, 0:128], self.cs["c_ones_1x128"][:],
                            self.x2b[0:1, 256 + hh * 128:384 + hh * 128],
                            start=False, stop=True),
                            waits=[], inc="sPE")
                        pslot.set(t2)
                        sQ.read(t2)
                        sK.read(t2)
                        smr.read(t2)
                        tmask = t2
                        if kb == T:
                            tmask = P.emit("dve", lambda e, ps=ps: e.scalar_tensor_tensor(
                                ps[:, 0:128], ps[:, 0:128], 1.0,
                                self.cs["c_negmaskT_local01"][:, 0:128],
                                ALU.mult, ALU.add),
                                waits=[t2], inc="sDV")
                            pslot.set(tmask)
                        tex = P.emit("act", lambda e, hh=hh, ps=ps: e.activation(
                            z["pt"][:, hh, :], ps[:, 0:128], AF.Exp,
                            bias=0.0, scale=1.0),
                            waits=[tmask] + (spt.war() if hh == 0 else []),
                            inc="sAC")
                        pslot.read(tex)
                        ex_toks.append(tex)
                    spt.set(ex_toks[-1])
                    for hh in range(2):
                        w = [ex_toks[hh], sV.ready]
                        if first_av[hh]:
                            w += avsl[hh].war()
                        av_end = P.emit("pe", lambda e, hh=hh, hp=hp, kb=kb,
                                        T=T, pp=avps[hh]: e.matmul(
                            pp[0:65, 0:128],
                            z["V"][:, kb, 2 * hp + hh, :],
                            z["pt"][:, hh, :],
                            start=kb == 0, stop=kb == T),
                            waits=w, inc="sPE" if (kb == T and hh == 1) else None)
                        if first_av[hh]:
                            avsl[hh].set(av_end)
                            first_av[hh] = False
                    spt.read(av_end)
                    sV.read(av_end)
                sden = self.S_("sq_t")
                tdA = P.emit("dve", lambda e: e.tensor_copy(
                    self.sq_t[0:1, 0:128], ps4[64:65, 0:128]),
                    waits=[av_end] + sden.war(), inc="sDV")
                s4.read(tdA)
                td = P.emit("dve", lambda e: e.tensor_copy(
                    self.sq_t[0:1, 128:256], ps6b[64:65, 0:128]),
                    waits=[av_end], inc="sDV")
                sden.set(td)
                s6b.read(td)
                trc0 = P.emit("dve", lambda e: e.reciprocal(
                    self.sq_t[0:1, 256:512], self.sq_t[0:1, 0:256]),
                    waits=[td], inc="sDV")
                sden.set(trc0)
                srec = self.S_("t2b")
                trc = P.emit("dve", lambda e: e.tensor_copy(
                    self.t2b[0:1, 0:256], self.sq_t[0:1, 256:512]),
                    waits=[trc0] + srec.war(), inc="sDV")
                srec.set(trc)
                sden.read(trc)
                ps5, s5 = self.ps[5], self.ps_slot[5]
                trb = P.emit("pe", lambda e: e.matmul(
                    ps5[0:64, 0:256], self.cs["c_ones_1x128"][0:1, 0:64],
                    self.t2b[0:1, 0:256], start=True, stop=True),
                    waits=[trc] + s5.war(), inc="sPE")
                s5.set(trb)
                srec.read(trb)
                srb = self.S_("t2b")
                trbe = P.emit("dve", lambda e: e.tensor_copy(
                    self.t2b[0:64, 256:512], ps5[0:64, 0:256]),
                    waits=[trb] + srb.war(), inc="sDV")
                srb.set(trbe)
                s5.read(trbe)
                for hh in range(2):
                    tf = P.emit("dve", lambda e, hh=hh, hp=hp, pp=avps[hh]:
                                e.tensor_mul(
                        z["ao"][64 * hh:64 * hh + 64, hp, :],
                        pp[0:64, 0:128],
                        self.t2b[0:64, 256 + hh * 128:384 + hh * 128]),
                        waits=[trbe] + (sao.war() if (hp == 0 and hh == 0) else []),
                        inc="sDV")
                    avsl[hh].read(tf)
                    srb.read(tf)
            sao.set(tf)
            for nc in range(NCHUNK):
                sl = nc % 2
                slab = z["woslab"][:, sl, :, :]
                ssl = self.S_(f"c_woslab{sl}")
                tw = P.dma("pool" if nc % 2 == 0 else "sync", slab,
                           self.W["bWo"][:, :, nc * 128:(nc + 1) * 128],
                           waits=ssl.war())
                ssl.set(tw)
                ps5, s5 = self.ps[5], self.ps_slot[5]
                mmtok = None
                for f in range(NCHUNK):
                    w = [tw, sao.ready]
                    if f == 0:
                        w += s5.war()
                    mmtok = P.emit("pe", lambda e, f=f, slab=slab: e.matmul(
                        ps5[:, 0:128], slab[:, f, :], z["ao"][:, f, :],
                        start=f == 0, stop=f == NCHUNK - 1),
                        waits=w, inc="sPE" if f == NCHUNK - 1 else None)
                    if f == 0:
                        s5.set(mmtok)
                s5.set(mmtok)
                ssl.read(mmtok)
                sao.read(mmtok)
                sxp = self.S_("c_xp")
                tld = P.dma("sync", z["xp"][:],
                            self.xr[1][nc * 128:(nc + 1) * 128,
                                       T * 128:(T + 1) * 128], waits=sxp.war())
                sxp.set(tld)
                st2 = self.S_(f"t2_{nc}")
                tt2 = P.emit("dve", lambda e, nc=nc, T=T: e.tensor_add(
                    self.t2[:, nc, (T % 4) * 128:(T % 4 + 1) * 128],
                    z["xp"][:], ps5[:, 0:128]),
                    waits=[mmtok, tld] + (st2.war() if T % 4 == 0 else []),
                    inc="sDV")
                if T % 4 == 0:
                    st2.set(tt2)
                sxp.read(tt2)
                s5.read(tt2)
                st2.set(tt2)
            if T % 4 == 3:
                t = t512
                t2tok = []
                for c in range(NCHUNK):
                    st2 = self.S_(f"t2_{c}")
                    tok = st2.ready
                    t2tok.append(tok)
                    xn_ap = self.xr[0][c * 128:(c + 1) * 128,
                                       t * 512:(t + 1) * 512]
                    tsp = P.dma("act", xn_ap, self.t2[:, c, :], waits=[tok])
                    st2.read(tsp)
                    tk = self.tile_stats(c, self.t2[:, c, :], tok, bank=6)
                    st2.read(tk)
                tmu, trb2, _ = self.ln_rowmath(6, 5)
                for c in range(NCHUNK):
                    st2 = self.S_(f"t2_{c}")
                    sd = self.S_("d1_t")
                    ta = P.emit("dve", lambda e, c=c: e.tensor_sub(
                        self.d1_t[:], self.t2[:, c, :], self.mu_t[:]),
                        waits=[st2.ready, tmu] + sd.war(), inc="sDV")
                    sd.set(ta)
                    st2.read(ta)
                    self.S_("mu_t").read(ta)
                    sh = self.S_("hs_t")
                    tb = P.emit("dve", lambda e: e.tensor_mul(
                        self.hs_t[:], self.d1_t[:], self.rb_t[:]),
                        waits=[ta, trb2] + sh.war(), inc="sDV")
                    sh.set(tb)
                    sd.read(tb)
                    self.S_("rb_t").read(tb)
                    self.stream_split(c, t, tb)

    def finalize(self):
        from contextlib import ExitStack
        nc = self.nc
        with ExitStack() as ctx:
            for s in list(self.P.counts):
                self.P.handles[s] = ctx.enter_context(nc.semaphore(s))
            block = ctx.enter_context(nc.Block())
            block.tensor(lambda e: self.P.run_stream("pe", e))
            block.vector(lambda e: self.P.run_stream("dve", e))
            block.scalar(lambda e: self.P.run_stream("act", e))
            block.gpsimd(lambda e: self.P.run_stream("pool", e))
            block.sync(lambda e: self.P.run_stream("sync", e))
        return nc


    def ffn_attn_tsub(self, seq, l, ts, wq_toks, trot, mid_cb=None):
        P = self.P
        z = self.zb
        t512 = ts // 4
        part_tok = []
        for part in range(3):
            for nh in range(2):
                bank = part * 2 + nh
                ps, pslot = self.ps[bank], self.ps_slot[bank]
                mmtok = None
                for f in range(NCHUNK):
                    first = f == 0
                    lhs = self.xb_ap(f, ts * 128, 128, shifted=(f >= 4))
                    w = [wq_toks[f], self.sXB[f][t512].ready]
                    if f >= 4 and ts % 4 == 0 and t512 > 0:
                        w.append(self.sXB[f][t512 - 1].ready)
                    if first:
                        w += pslot.war()
                    mmtok = P.emit("pe", lambda e, f=f, lhs=lhs, part=part, nh=nh,
                                   ps=ps: e.matmul(
                        ps[:, 0:512], lhs,
                        z["wqkv"][:, f, part * 1024 + nh * 512:
                                  part * 1024 + (nh + 1) * 512],
                        start=f == 0, stop=f == NCHUNK - 1),
                        waits=w, inc="sPE" if f == NCHUNK - 1 else None)
                    if first:
                        pslot.set(mmtok)
                pslot.set(mmtok)
                for f in range(NCHUNK):
                    self.sXB[f][t512].read(mmtok)
                    if f >= 4 and ts % 4 == 0 and t512 > 0:
                        self.sXB[f][t512 - 1].read(mmtok)
                part_tok.append(mmtok)
        if mid_cb is not None:
            mid_cb()
        ssq = self.S_("b_sq")
        snm = self.S_("b_norms")
        tnm = None
        for part in range(2):
            sq_toks = []
            for nh in range(2):
                bank = part * 2 + nh
                tq = P.emit("act", lambda e, bank=bank, nh=nh: e.activation(
                    z["sq"][:, nh * 512:(nh + 1) * 512], self.ps[bank][:, 0:512],
                    AF.Square, bias=0.0, scale=1.0),
                    waits=[part_tok[part * 2 + nh]] + (ssq.war() if nh == 0 else []),
                    inc="sAC")
                sq_toks.append(tq)
                self.ps_slot[bank].read(tq)
            ssq.set(sq_toks[-1])
            tnm = P.emit("dve", lambda e, part=part: e.tensor_reduce(
                z["norms"][:, part * 16:(part + 1) * 16],
                bass.AP(z["sq"], 0, [[1024, 128], [64, 16], [1, 64]]),
                AX.X, ALU.add),
                waits=sq_toks + (snm.war() if part == 0 else []), inc="sDV")
            ssq.read(tnm)
        snm.set(tnm)
        srs = self.S_("b_rstd")
        t1 = P.emit("dve", lambda e: e.tensor_scalar_max(
            z["rstd"][:], z["norms"][:], 1e-24),
            waits=[tnm] + srs.war(), inc="sDV")
        snm.read(t1)
        t2 = P.emit("act", lambda e: e.activation(
            z["rstd"][:], z["rstd"][:], AF.Ln, bias=0.0, scale=1.0),
            waits=[t1], inc="sAC")
        t3 = P.emit("act", lambda e: e.activation(
            z["rstd"][:], z["rstd"][:], AF.Exp, bias=0.0, scale=-0.5),
            waits=[t2], inc="sAC")
        srs.set(t3)
        sqh = self.S_("b_qhat")
        tqh = None
        for part in range(2):
            for nh in range(2):
                bank = part * 2 + nh
                rs = bass.AP(z["rstd"], part * 16 + nh * 8,
                             [[32, 128], [1, 8], [0, 64]])
                tqh = P.emit("dve", lambda e, bank=bank, part=part, nh=nh, rs=rs:
                             e.tensor_tensor(
                    z["qhat"][:, part * 1024 + nh * 512:
                              part * 1024 + (nh + 1) * 512],
                    self.ps[bank][:, 0:512], rs, ALU.mult),
                    waits=[t3, part_tok[part * 2 + nh]] +
                          (sqh.war() if (part == 0 and nh == 0) else []),
                    inc="sDV")
                self.ps_slot[bank].read(tqh)
        sqh.set(tqh)
        srs.read(tqh)
        if seq == 0 and l == 0 and ts == 0:
            self.dump_tile("qpre0", z["qhat"][:], [tqh],
                           slots=("b_qhat", "b_rout"))
        sr1 = self.S_("b_rsc1x")
        sr2 = self.S_("b_sq")
        sro = self.S_("b_rout")
        tro = None
        for part in range(2):
            c1 = bass.AP(z["rot"], (part * 2) * 1024 + ts * 64,
                         [[4 * 1024, 128], [0, 16], [1, 64]])
            c2 = bass.AP(z["rot"], (part * 2 + 1) * 1024 + ts * 64,
                         [[4 * 1024, 128], [0, 16], [1, 64]])
            ta = P.emit("dve", lambda e, part=part, c1=c1: e.tensor_tensor(
                self.zb["rsc1"][:, 0:1024], z["qhat"][:, part * 1024:(part + 1) * 1024],
                c1, ALU.mult),
                waits=[tqh, trot] + sr1.war(), inc="sDV")
            sr1.set(ta)
            sqh.read(ta)
            tb_ = None
            for j0, joff in ((0, 32), (32, -32)):
                qsh = bass.AP(z["qhat"], part * 1024 + j0 + joff,
                              [[2048, 128], [64, 16], [1, 32]])
                c2s = bass.AP(z["rot"], (part * 2 + 1) * 1024 + ts * 64 + j0,
                              [[4 * 1024, 128], [0, 16], [1, 32]])
                tb_ = P.emit("dve", lambda e, qsh=qsh, c2s=c2s, j0=j0: e.tensor_tensor(
                    bass.AP(self.zb["sq"], j0, [[1024, 128], [64, 16], [1, 32]]),
                    qsh, c2s, ALU.mult),
                    waits=[tqh, trot] + (sr2.war() if j0 == 0 else []), inc="sDV")
            sr2.set(tb_)
            sqh.read(tb_)
            tro = P.emit("pool", lambda e, part=part: e.tensor_add(
                z["qhat"][:, part * 1024:(part + 1) * 1024],
                self.zb["rsc1"][:, 0:1024], self.zb["sq"][:, 0:1024]),
                waits=[ta, tb_] + (sro.war() if part == 0 else []), inc="sPL")
            sr1.read(tro)
            sr2.read(tro)
        sro.set(tro)
        if seq == 0 and l == 0 and ts == 0:
            self.dump_tile("qh0", z["qhat"][:], [tro],
                           slots=("b_qhat", "b_rout"))
        psbf, sbf = self.psbf, self.ps_slot[7]
        sqc = self.S_("b_qcur")
        skr = self.S_(f"b_kroll{ts % 3}")
        tq_end = tk_end = None
        for hp in range(NCHUNK):
            slot = hp % 8
            ttq = P.emit("pe", lambda e, hp=hp, slot=slot: e.transpose(
                psbf[:, slot * 128:(slot + 1) * 128],
                z["qhat"][:, hp * 128:(hp + 1) * 128], self.cs["c_ident"][:]),
                waits=[tro] + sbf.war(), inc="sPE")
            sbf.set(ttq)
            sro.read(ttq)
            tq_end = P.emit("dve", lambda e, hp=hp, slot=slot: e.tensor_copy(
                z["qcur"][:, hp, :], psbf[:, slot * 128:(slot + 1) * 128]),
                waits=[ttq] + (sqc.war() if hp == 0 else []), inc="sDV")
            sbf.read(tq_end)
        sqc.set(tq_end)
        for hp in range(NCHUNK):
            slot = hp % 8
            ttk = P.emit("pe", lambda e, hp=hp, slot=slot: e.transpose(
                psbf[:, slot * 128:(slot + 1) * 128],
                z["qhat"][:, 1024 + hp * 128:1024 + (hp + 1) * 128],
                self.cs["c_ident"][:]),
                waits=[tro] + sbf.war(), inc="sPE")
            sbf.set(ttk)
            sro.read(ttk)
            tk_end = P.emit("dve", lambda e, hp=hp, slot=slot, ts=ts: e.tensor_copy(
                z["kroll"][:, hp, ts % 3, :], psbf[:, slot * 128:(slot + 1) * 128]),
                waits=[ttk] + (skr.war() if hp == 0 else []), inc="sDV")
            sbf.read(tk_end)
        skr.set(tk_end)
        svr = self.S_(f"b_vroll{ts % 3}")
        svra = self.S_("b_vroll")
        tv_end = None
        for nh in range(2):
            bank = 4 + nh
            tv_end = P.emit("dve", lambda e, nh=nh, ts=ts, bank=bank: e.tensor_copy(
                z["vroll"][:, ts % 3, nh * 8:(nh + 1) * 8, 0:64],
                self.ps[bank][:, 0:512].rearrange("p (h d) -> p h d", h=8)),
                waits=[part_tok[4 + nh], svra.ready] +
                      (svr.war() if nh == 0 else []), inc="sDV")
            self.ps_slot[bank].read(tv_end)
        svr.set(tv_end)
        T = ts
        nkb = min(T + 1, 3)
        kslots = [(T - i) % 3 for i in range(nkb - 1, -1, -1)]
        ktoks = [self.S_(f"b_kroll{s}").ready for s in kslots]
        vtoks = [self.S_(f"b_vroll{s}").ready for s in kslots]
        mask_c0 = (3 - nkb) * 128
        sat = self.S_("b_attout")
        for hp in range(NCHUNK):
            s_tok = [None, None]
            for hh in range(2):
                bank = hh
                ps, pslot = self.ps[bank], self.ps_slot[bank]
                mmtok = None
                for kb in range(nkb):
                    kr = z["kroll"][64 * hh:64 * hh + 64, hp, kslots[kb], :]
                    w = [tq_end, ktoks[kb]]
                    if kb == 0:
                        w += pslot.war()
                    mmtok = P.emit("pe", lambda e, kr=kr, hh=hh, kb=kb, hp=hp,
                                   ps=ps: e.matmul(
                        ps[:, kb * 128:(kb + 1) * 128],
                        z["qcur"][64 * hh:64 * hh + 64, hp, :], kr,
                        start=True, stop=True,
                        tile_position=(64 * hh, 0)),
                        waits=w, inc="sPE" if kb == nkb - 1 else None)
                    if kb == 0:
                        pslot.set(mmtok)
                pslot.set(mmtok)
                sqc.read(mmtok)
                s_tok[hh] = mmtok
            ssm = self.S_("b_sm")
            tm_ = [None, None]
            for hh in range(2):
                tm_[hh] = P.emit("dve", lambda e, hh=hh, nkb=nkb, mask_c0=mask_c0:
                                 e.scalar_tensor_tensor(
                    z["sm"][:, hh * 384:hh * 384 + nkb * 128],
                    self.ps[hh][:, 0:nkb * 128], 1.0,
                    self.cs["c_negmask_local"][:, mask_c0:mask_c0 + nkb * 128],
                    ALU.mult, ALU.add),
                    waits=[s_tok[hh]] + (ssm.war() if hh == 0 else []), inc="sDV")
                self.ps_slot[hh].read(tm_[hh])
            ssm.set(tm_[1])
            if seq == 0 and l == 0 and ts == 0 and hp == 0:
                self.dump_tile("sm0", z["sm"][:], [tm_[1]], slots=("b_sm",))
            smc = self.S_("b_mcol")
            tmc = P.emit("dve", lambda e, nkb=nkb: e.tensor_reduce(
                z["mcolf"][:], bass.AP(z["sm"], 0, [[768, 128], [384, 2], [1, nkb * 128]]),
                AX.X, ALU.max),
                waits=tm_ + smc.war(), inc="sDV")
            ssm.read(tmc)
            tmc2 = P.emit("pool", lambda e: e.tensor_copy(z["mcol"][:], z["mcolf"][:]),
                          waits=[tmc], inc="sPL")
            smc.set(tmc2)
            if seq == 0 and l == 0 and ts == 0 and hp == 0:
                self.dump_tile("mc0", z["mcolf"][:], [tmc], slots=("b_mcol",))
            ps2, s2 = self.ps[2], self.ps_slot[2]
            tr1 = P.emit("pe", lambda e: e.matmul(
                ps2[0:1, 0:128], z["mcol"][:, 0:1], self.cs["c_negident"][:],
                start=True, stop=True), waits=[tmc2] + s2.war(), inc=None)
            tr2 = P.emit("pe", lambda e: e.matmul(
                ps2[0:1, 128:256], z["mcol"][:, 1:2], self.cs["c_negident"][:],
                start=True, stop=True), waits=[], inc="sPE")
            s2.set(tr2)
            smc.read(tr2)
            smr = self.S_("x2b")
            tmr = P.emit("dve", lambda e: e.tensor_copy(
                self.x2b[0:1, 256:512], ps2[0:1, 0:256]),
                waits=[tr2] + smr.war(), inc="sDV")
            smr.set(tmr)
            s2.read(tmr)
            spt = self.S_("b_pt")
            ex_toks = []
            for hh in range(2):
                bank = 2 + hh
                ps, pslot = self.ps[bank], self.ps_slot[bank]
                stt_end = None
                for kb in range(nkb):
                    kr = z["kroll"][64 * hh:64 * hh + 64, hp, kslots[kb], :]
                    w = [tq_end, ktoks[kb], tmr]
                    if kb == 0:
                        w += pslot.war()
                    P.emit("pe", lambda e, kr=kr, hh=hh, kb=kb, ps=ps,
                           hp=hp: e.matmul(
                        ps[:, kb * 128:(kb + 1) * 128], kr,
                        z["qcur"][64 * hh:64 * hh + 64, hp, :],
                        start=True, stop=False,
                        tile_position=(64 * hh, 0)),
                        waits=w, inc=None)
                    stt_end = P.emit("pe", lambda e, hh=hh, kb=kb, ps=ps: e.matmul(
                        ps[:, kb * 128:(kb + 1) * 128], self.cs["c_ones_1x128"][:],
                        self.x2b[0:1, 256 + hh * 128:256 + (hh + 1) * 128],
                        start=False, stop=True),
                        waits=[], inc="sPE" if kb == nkb - 1 else None)
                pslot.set(stt_end)
                smr.read(stt_end)
                sqc.read(stt_end)
                tmask = stt_end
                mask_ops = []
                if True:
                    mask_ops.append((nkb - 1, 0))
                if nkb == 3:
                    mask_ops.append((0, 128))
                for kb, mc in mask_ops:
                    tmask = P.emit("dve", lambda e, kb=kb, mc=mc, ps=ps: e.scalar_tensor_tensor(
                        ps[:, kb * 128:(kb + 1) * 128], ps[:, kb * 128:(kb + 1) * 128],
                        1.0, self.cs["c_negmaskT_local01"][:, mc:mc + 128],
                        ALU.mult, ALU.add),
                        waits=[stt_end], inc="sDV")
                tex = P.emit("act", lambda e, hh=hh, nkb=nkb, ps=ps: e.activation(
                    z["pt"][:, hh, 0:nkb * 128], ps[:, 0:nkb * 128], AF.Exp,
                    bias=0.0, scale=1.0),
                    waits=[tmask] + (spt.war() if hh == 0 else []), inc="sAC")
                pslot.read(tex)
                ex_toks.append(tex)
            spt.set(ex_toks[-1])
            if seq == 0 and l == 0 and ts == 0 and hp == 0:
                self.dump_tile("pt0", z["pt"][:], [ex_toks[-1]], slots=("b_pt",))
            ps4, s4 = self.ps[4], self.ps_slot[4]
            av_end = None
            for hh in range(2):
                for kb in range(nkb):
                    w = [ex_toks[hh], vtoks[kb]]
                    if hh == 0 and kb == 0:
                        w += s4.war()
                    av_end = P.emit("pe", lambda e, hh=hh, kb=kb, hp=hp,
                                    ks=kslots[kb], nkb=nkb: e.matmul(
                        ps4[0:65, hh * 128:(hh + 1) * 128],
                        z["vroll"][:, ks, 2 * hp + hh, :],
                        z["pt"][:, hh, kb * 128:(kb + 1) * 128],
                        start=kb == 0, stop=kb == nkb - 1),
                        waits=w, inc="sPE" if (hh == 1 and kb == nkb - 1) else None)
                    if hh == 0 and kb == 0:
                        s4.set(av_end)
            s4.set(av_end)
            spt.read(av_end)
            for kb in range(nkb):
                self.S_(f"b_vroll{kslots[kb]}").read(av_end)
            sden = self.S_("sq_t")
            td = P.emit("dve", lambda e: e.tensor_copy(
                self.sq_t[0:1, 0:256], ps4[64:65, 0:256]),
                waits=[av_end] + sden.war(), inc="sDV")
            sden.set(td)
            s4.read(td)
            trc0 = P.emit("dve", lambda e: e.reciprocal(
                self.sq_t[0:1, 256:512], self.sq_t[0:1, 0:256]),
                waits=[td], inc="sDV")
            sden.set(trc0)
            srec = self.S_("x2b")
            trc = P.emit("dve", lambda e: e.tensor_copy(
                self.x2b[0:1, 0:256], self.sq_t[0:1, 256:512]),
                waits=[trc0] + srec.war(), inc="sDV")
            srec.set(trc)
            sden.read(trc)
            ps5, s5 = self.ps[5], self.ps_slot[5]
            trb = P.emit("pe", lambda e: e.matmul(
                ps5[0:64, 0:256], self.cs["c_ones_1x128"][0:1, 0:64],
                self.x2b[0:1, 0:256], start=True, stop=True),
                waits=[trc] + s5.war(), inc="sPE")
            s5.set(trb)
            srec.read(trb)
            srb = self.S_("b_sq")
            trbe = P.emit("dve", lambda e: e.tensor_copy(
                z["sq"][0:64, 0:256], ps5[0:64, 0:256]),
                waits=[trb] + srb.war(), inc="sDV")
            srb.set(trbe)
            s5.read(trbe)
            for hh in range(2):
                tf = P.emit("dve", lambda e, hh=hh, hp=hp, T=T: e.tensor_mul(
                    z["attout"][64 * hh:64 * hh + 64, hp, (T % 4) * 128:
                                (T % 4 + 1) * 128],
                    ps4[0:64, hh * 128:(hh + 1) * 128],
                    z["sq"][0:64, hh * 128:(hh + 1) * 128]),
                    waits=[trbe] + (sat.war() if (hp == 0 and hh == 0 and T % 4 == 0)
                                    else []), inc="sDV")
                s4.read(tf)
                srb.read(tf)
            sat.set(tf)
        if seq == 0 and l == 0 and ts == 3:
            self.dump_tile("att0", z["attout"][:],
                           [self.S_("b_attout").ready], slots=("b_attout",))

    def ffn_wo_resid(self, seq, l, t):
        P = self.P
        z = self.zb
        sat = self.S_("b_attout")
        ps5, s5 = self.ps[6], self.ps_slot[6]
        for ncx in range(NCHUNK):
            sl = ncx % 2
            slab = z["woslab"][:, sl, :, :]
            ssl = self.S_(f"b_woslab{sl}")
            tw = P.dma("pool" if ncx % 2 == 0 else "sync",
                       slab, self.W["fWo"][l, ncx], waits=ssl.war())
            ssl.set(tw)
            mmtok = None
            for f in range(NCHUNK):
                w = [tw, sat.ready]
                if f == 0:
                    w += s5.war()
                mmtok = P.emit("pe", lambda e, f=f, slab=slab: e.matmul(
                    ps5[:, 0:512], slab[:, f, :], z["attout"][:, f, :],
                    start=f == 0, stop=f == NCHUNK - 1),
                    waits=w, inc="sPE" if f == NCHUNK - 1 else None)
                if f == 0:
                    s5.set(mmtok)
            s5.set(mmtok)
            ssl.read(mmtok)
            sat.read(mmtok)
            sd = self.S_("d1_t")
            xb = self.xb_ap(ncx, t * 512, 512)
            xl = self.xb_ap(ncx, t * 512, 512, lo=True)
            p1 = P.emit("dve", lambda e, xb=xb: e.tensor_add(
                self.d1_t[:], ps5[:, 0:512], xb),
                waits=[mmtok, self.sXB[ncx][t].ready] + sd.war(), inc="sDV")
            sd.set(p1)
            s5.read(p1)
            self.sXB[ncx][t].read(p1)
            sh = self.S_("hs_t")
            p2 = P.emit("pool", lambda e, xl=xl: e.tensor_add(
                self.hs_t[:], self.d1_t[:], xl),
                waits=[p1, self.sXL[ncx][t].ready] + sh.war(), inc="sPL")
            sh.set(p2)
            sd.read(p2)
            self.sXL[ncx][t].read(p2)
            self.stream_split(ncx, t, p2)

    def ffn_geglu_tile(self, seq, l, t):
        P = self.P
        z = self.zb
        for c in range(NCHUNK):
            xbs = self.xb_ap(c, t * 512, 512, shifted=(c >= 4))
            prev = [self.sXB[c][t - 1].ready] if (c >= 4 and t > 0) else []
            s2 = self.S_("x2b")
            tq = P.emit("act", lambda e, xbs=xbs: e.activation(
                self.x2b[:], xbs, AF.Square, bias=0.0, scale=1.0),
                waits=[self.sXB[c][t].ready] + prev + s2.war(), inc="sAC")
            s2.set(tq)
            tk = self.stats_mms(6, c, xbs, self.x2b[:],
                                [tq, self.sXB[c][t].ready] + prev)
            s2.read(tk)
            self.sXB[c][t].read(tk)
            if c >= 4 and t > 0:
                self.sXB[c][t - 1].read(tk)
        tmu, trb, tnm = self.ln_rowmath(6, 5)
        ps4, s4 = self.ps[4], self.ps_slot[4]
        sxg = self.S_("b_xg")
        txg = None
        for c in range(NCHUNK):
            tb = P.emit("pe", lambda e, c=c: e.matmul(
                ps4[:, 0:512], self.prow[0:1, c * 128:(c + 1) * 128],
                self.negmu_row[:], start=True, stop=True),
                waits=[tnm, self.ptoks[-1]] + s4.war(), inc="sPE")
            s4.set(tb)
            self.S_("negmu_row").read(tb)
            xbs = self.xb_ap(c, t * 512, 512, shifted=(c >= 4))
            prev = [self.sXB[c][t - 1].ready] if (c >= 4 and t > 0) else []
            txg = P.emit("dve", lambda e, c=c, xbs=xbs: e.scalar_tensor_tensor(
                z["xg"][:, c, :], xbs, self.pcol[:, 52 + c:53 + c], ps4[:, 0:512],
                ALU.mult, ALU.add),
                waits=[tb, self.sXB[c][t].ready] + prev + self.ptoks +
                      (sxg.war() if c == 0 else []), inc="sDV")
            s4.read(txg)
            self.sXB[c][t].read(txg)
            if c >= 4 and t > 0:
                self.sXB[c][t - 1].read(txg)
        sxg.set(txg)
        first_t = t == NT512 - 1
        scb = self.S_("b_cb")
        for g in range(22):
            sl = g % 2
            slab = z["w1slab"][:, sl, :, :]
            ssl = self.S_(f"b_w1slab{sl}")
            tw = P.dma("pool" if g % 2 == 0 else "sync",
                       slab, self.W["fW1"][l, g], waits=ssl.war())
            ssl.set(tw)
            atok = gtok = None
            for half in range(2):
                bank = half * 2 + g % 2
                ps, pslot = self.ps[bank], self.ps_slot[bank]
                mmtok = None
                for f in range(NCHUNK):
                    w = [tw, txg]
                    if f == 0:
                        w += pslot.war()
                    mmtok = P.emit("pe", lambda e, f=f, slab=slab, half=half,
                                   ps=ps: e.matmul(
                        ps[:, 0:512], slab[:, f, half * 128:(half + 1) * 128],
                        z["xg"][:, f, :],
                        start=f == 0, stop=f == NCHUNK - 1),
                        waits=w, inc="sPE" if f == NCHUNK - 1 else None)
                    if f == 0:
                        pslot.set(mmtok)
                pslot.set(mmtok)
                sxg.read(mmtok)
                ssl.read(mmtok)
                if half == 0:
                    atok = mmtok
                else:
                    gtok = mmtok
            if first_t:
                cbtok = None
                for half in range(2):
                    for f in range(NCHUNK):
                        w = [tw] + self.ptoks
                        if half == 0 and f == 0:
                            w += s4.war()
                        cbtok = P.emit("pe", lambda e, f=f, half=half, slab=slab:
                                       e.matmul(
                            ps4[:, half:half + 1],
                            slab[:, f, half * 128:(half + 1) * 128],
                            self.t2b[:, f:f + 1],
                            start=f == 0, stop=f == NCHUNK - 1),
                            waits=w, inc="sPE" if (half == 1 and f == NCHUNK - 1)
                            else None)
                        if half == 0 and f == 0:
                            s4.set(cbtok)
                s4.set(cbtok)
                self.S_("t2b").read(cbtok)
                ssl.read(cbtok)
                tcb = P.emit("dve", lambda e, g=g: e.tensor_copy(
                    bass.AP(z["cb2"], g, [[44, 128], [22, 2], [1, 1]]),
                    ps4[:, 0:2]),
                    waits=[cbtok] + (scb.war() if g == 0 else []), inc="sDV")
                s4.read(tcb)
                if g == 21:
                    tcb = P.emit("pool", lambda e: e.tensor_add(
                        z["cb2"][:], z["cb2"][:], self.pcol[:, 0:44]),
                        waits=[tcb] + self.ptoks, inc="sPL")
                scb.set(tcb)
            sd = self.S_("d1_t")
            gbank = 2 + g % 2
            t1 = P.emit("dve", lambda e, gbank=gbank: e.tensor_mul(
                self.d1_t[:], self.ps[gbank][:, 0:512], self.rb_t[:]),
                waits=[gtok, trb, scb.ready] + sd.war(), inc="sDV")
            sd.set(t1)
            self.ps_slot[gbank].read(t1)
            self.S_("rb_t").read(t1)
            sh = self.S_("hs_t")
            t2 = P.emit("act", lambda e, g=g: e.activation(
                self.hs_t[:], self.d1_t[:], AF.Gelu,
                bias=z["cb2"][:, 22 + g:23 + g], scale=1.0),
                waits=[t1, scb.ready] + sh.war(), inc="sAC")
            sh.set(t2)
            sd.read(t2)
            scb.read(t2)
            abank = g % 2
            t3 = P.emit("dve", lambda e, abank=abank: e.tensor_mul(
                self.d1_t[:], self.ps[abank][:, 0:512], self.rb_t[:]),
                waits=[atok, trb, t2] + sd.war(), inc="sDV")
            sd.set(t3)
            self.ps_slot[abank].read(t3)
            self.S_("rb_t").read(t3)
            sgg = self.S_("b_gg")
            t4 = P.emit("dve", lambda e, g=g: e.scalar_tensor_tensor(
                z["gg"][:, g, :], self.d1_t[:], z["cb2"][:, g:g + 1],
                self.hs_t[:], ALU.add, ALU.mult),
                waits=[t3, t2, scb.ready] + (sgg.war() if g == 0 else []),
                inc="sDV")
            sd.read(t4)
            sh.read(t4)
            scb.read(t4)
        sgg = self.S_("b_gg")
        sgg.set(t4)
        for ncx in range(NCHUNK):
            sl = ncx % 2
            slab = z["w2slab"][:, sl, :, :]
            ssl = self.S_(f"b_w2slab{sl}")
            tw = P.dma("pool" if ncx % 2 == 0 else "sync",
                       slab, self.W["fW2"][l, ncx], waits=ssl.war())
            ssl.set(tw)
            bank = 4 + ncx % 2
            ps, pslot = self.ps[bank], self.ps_slot[bank]
            mmtok = None
            for gc in range(22):
                w = [tw, sgg.ready]
                if gc == 0:
                    w += pslot.war()
                mmtok = P.emit("pe", lambda e, gc=gc, slab=slab, ps=ps: e.matmul(
                    ps[:, 0:512], slab[:, gc, :], z["gg"][:, gc, :],
                    start=gc == 0, stop=gc == 21),
                    waits=w, inc="sPE" if gc == 21 else None)
                if gc == 0:
                    pslot.set(mmtok)
            pslot.set(mmtok)
            ssl.read(mmtok)
            sgg.read(mmtok)
            sd = self.S_("d1_t")
            xb = self.xb_ap(ncx, t * 512, 512)
            xl = self.xb_ap(ncx, t * 512, 512, lo=True)
            p1 = P.emit("dve", lambda e, ncx=ncx, xb=xb, ps=ps: e.scalar_tensor_tensor(
                self.d1_t[:], ps[:, 0:512], self.pcol[:, 44 + ncx:45 + ncx],
                xb, ALU.add, ALU.add),
                waits=[mmtok, self.sXB[ncx][t].ready] + self.ptoks + sd.war(),
                inc="sDV")
            sd.set(p1)
            pslot.read(p1)
            self.sXB[ncx][t].read(p1)
            sh = self.S_("hs_t")
            p2 = P.emit("pool", lambda e, xl=xl: e.tensor_add(
                self.hs_t[:], self.d1_t[:], xl),
                waits=[p1, self.sXL[ncx][t].ready] + sh.war(), inc="sPL")
            sh.set(p2)
            sd.read(p2)
            self.sXL[ncx][t].read(p2)
            self.stream_split(ncx, t, p2)


def build_kernel(nseq=SEQ_PER_CORE, phases="ABCD", debug=()):
    mk = MK(nseq=nseq, phases=phases, debug=debug)
    mk.build()
    nc = mk.finalize()
    return mk, nc


def kernel(**inputs):
    I = {k: np.asarray(v) for k, v in inputs.items()}
    H = prep_inputs(I)
    x = I["L0_attn_a_x"]
    mk, nc = build_kernel()
    in_maps = []
    for core in range(NCORES):
        sh = dict(H)
        xs = x[core * SEQ_PER_CORE:(core + 1) * SEQ_PER_CORE]
        sh["xT"] = np.ascontiguousarray(xs.transpose(0, 2, 1))
        in_maps.append(sh)
    res = run_bass_kernel_spmd(nc, in_maps, core_ids=list(range(NCORES)))
    outs = []
    for core in range(NCORES):
        oT = np.asarray(res.results[core]["outT"])
        outs.append(oT.transpose(0, 2, 1))
    return np.ascontiguousarray(np.concatenate(outs, axis=0)).astype(np.float32)


# version 06 (final)
# speedup vs baseline: 1.3886x; 1.2307x over previous
import numpy as np
import ml_dtypes
import concourse.bass as bass
import concourse.mybir as mybir
from concourse.bass_utils import run_bass_kernel_spmd

F32 = mybir.dt.float32
BF16 = mybir.dt.bfloat16
AF = mybir.ActivationFunctionType
ALU = mybir.AluOpType
AX = mybir.AxisListType

NCORES = 8
SEQ_PER_CORE = 4
S = 2048
D = 1024
NCHUNK = 8
NT512 = S // 512
NTSUB = S // 128
LN_EPS = 1e-5

BF = ml_dtypes.bfloat16



def repack_slab_a(W):
    *lead, K, N = W.shape
    A = W.reshape(*lead, K // 128, 128, N // 128, 128)
    nl = len(lead)
    perm = list(range(nl)) + [nl + 2, nl + 1, nl + 0, nl + 3]
    return np.ascontiguousarray(A.transpose(*perm))


def repack_resident_b(W):
    *lead, K, N = W.shape
    A = W.reshape(*lead, K // 128, 128, N)
    nl = len(lead)
    perm = list(range(nl)) + [nl + 1, nl + 0, nl + 2]
    return np.ascontiguousarray(A.transpose(*perm))


def repack_cols(v):
    *lead, N = v.shape
    A = v.reshape(*lead, N // 128, 128)
    nl = len(lead)
    perm = list(range(nl)) + [nl + 1, nl + 0]
    return np.ascontiguousarray(A.transpose(*perm))


def build_host_consts():
    C = {}
    ident = np.eye(128, dtype=np.float32)
    C["c_ident"] = ident.astype(BF)
    C["c_negident"] = (-ident).astype(BF)
    k = np.arange(128)[:, None]
    q = np.arange(128)[None, :]
    C["c_maskT_diag"] = (k <= q).astype(np.float32).astype(BF)
    il = np.arange(128)[:, None]
    jl = np.arange(128)[None, :]
    blk0 = np.where(il <= jl, 0.0, -1e9)
    blk1 = np.zeros((128, 128))
    blk2 = np.where(jl <= il, 0.0, -1e9)
    C["c_negmask_local"] = np.concatenate([blk0, blk1, blk2], axis=1).astype(np.float32).astype(BF)
    C["c_negmask_rot0"] = np.concatenate([blk2, blk0, blk1], axis=1).astype(np.float32).astype(BF)
    C["c_negmask_rot1"] = np.concatenate([blk1, blk2, blk0], axis=1).astype(np.float32).astype(BF)
    C["c_negmaskT_local01"] = np.where(k <= q, 0.0, -1e9).astype(
        np.float32).astype(BF)
    pat = np.zeros((128, 33), np.float32)
    pat[0:64, 0] = -0.0625
    pat[64:128, 32] = -0.0625
    C["c_favor_pat"] = pat.astype(BF)
    indA = np.zeros((1, 128), np.float32); indA[0, 0:64] = 1.0
    indB = np.zeros((1, 128), np.float32); indB[0, 64:128] = 1.0
    C["c_indA"] = indA.astype(BF)
    C["c_indB"] = indB.astype(BF)
    C["c_ones_1x128"] = np.ones((1, 128), np.float32).astype(BF)
    st0 = np.zeros((128, 36), np.float32); st0[:, 0] = 1.0
    st32 = np.zeros((128, 36), np.float32); st32[:, 32] = 1.0
    C["c_stI0"] = st0.astype(BF)
    C["c_stI32"] = st32.astype(BF)
    DIM_HEAD, WINDOW = 64, 256
    half = np.arange(0, DIM_HEAD, 2, dtype=np.float32)
    inv_freq = (1.0 / 10000.0 ** (half / DIM_HEAD)).astype(np.float32)
    t = np.arange(S, dtype=np.float32)
    freqs = (t[:, None] * inv_freq[None, :]).astype(np.float32)
    freqs = np.concatenate([freqs, freqs], -1)
    base_scale = ((half + 0.4 * DIM_HEAD) / (1.4 * DIM_HEAD)).astype(np.float32)
    power = ((t - S // 2) / (WINDOW // 2)).astype(np.float32)
    scale = (base_scale[None, :] ** power[:, None]).astype(np.float32)
    scale = np.concatenate([scale, scale], -1)
    cos = np.cos(freqs).astype(np.float32)
    sin = np.sin(freqs).astype(np.float32)
    inv = (1.0 / scale).astype(np.float32)

    def lay(tab):
        return np.ascontiguousarray(
            tab.reshape(16, 128, 64).transpose(1, 0, 2)).astype(np.float32)
    sign = np.concatenate([-np.ones(32, np.float32), np.ones(32, np.float32)])
    QK_SCALE = np.float32(8.0)
    C["c_rot_cs_q"] = lay(cos * scale * QK_SCALE)
    C["c_rot_sn_q"] = lay(sin * scale * sign * QK_SCALE)
    C["c_rot_cs_k"] = lay(cos * inv)
    C["c_rot_sn_k"] = lay(sin * inv * sign)
    return C


def prep_inputs(I):
    H = {}
    H["aWq"] = repack_slab_a(I["L0_attn_a_Wq"])
    H["aWk"] = repack_slab_a(I["L0_attn_a_Wk"])
    H["aWv"] = repack_resident_b(I["L0_attn_a_Wv"])
    H["aWo"] = repack_resident_b(I["L0_attn_a_Wo"])
    H["aW1"] = repack_slab_a(I["L0_attn_a_W1"])
    H["aW2"] = repack_slab_a(I["L0_attn_a_W2"])
    H["abq"] = repack_cols(I["L0_attn_a_bq"])
    H["abk"] = repack_cols(I["L0_attn_a_bk"])
    H["abv_row"] = np.ascontiguousarray(I["L0_attn_a_bv"])[:, None, :]
    H["abo"] = repack_cols(I["L0_attn_a_bo"])
    H["abf1"] = repack_cols(I["L0_attn_a_bf1"])
    H["abf2"] = repack_cols(I["L0_attn_a_bf2"])
    H["aln1g"] = repack_cols(I["L0_attn_a_ln1_g"])
    H["aln1b"] = repack_cols(I["L0_attn_a_ln1_b"])
    H["aln2g"] = repack_cols(I["L0_attn_a_ln2_g"])
    H["aln2b"] = repack_cols(I["L0_attn_a_ln2_b"])
    H["aomega"] = np.ascontiguousarray(I["L0_attn_a_omega"])
    H["fWqkv"] = repack_resident_b(I["L0_ffn_Wqkv"])
    H["fWo"] = repack_slab_a(I["L0_ffn_Wo"])
    W1 = I["L0_ffn_W1"]
    a_part = W1[:, :, :2730]
    g_part = W1[:, :, 2730:]
    pad = np.zeros((6, 1024, 86), np.float32)
    W1p = np.concatenate([a_part, pad, g_part, pad], axis=2)
    w1s = repack_slab_a(W1p)
    H["fW1"] = np.ascontiguousarray(
        np.concatenate([w1s[:, :22], w1s[:, 22:]], axis=4))
    b1 = I["L0_ffn_b1"]
    b1p = np.concatenate([b1[:, :2730], np.zeros((6, 86), np.float32),
                          b1[:, 2730:], np.zeros((6, 86), np.float32)], axis=1)
    H["fb1"] = repack_cols(b1p)
    W2 = I["L0_ffn_W2"]
    W2p = np.concatenate([W2, np.zeros((6, 86, 1024), np.float32)], axis=1)
    H["fW2"] = repack_slab_a(W2p)
    H["fb2"] = repack_cols(I["L0_ffn_b2"])
    H["flng_col"] = repack_cols(I["L0_ffn_ln_g"])
    H["flng_row"] = np.ascontiguousarray(
        I["L0_ffn_ln_g"].reshape(6, 1, 1024))
    H["flnb_col"] = repack_cols(I["L0_ffn_ln_b"])
    H["fqs_row"] = np.ascontiguousarray(I["L0_ffn_q_scale"])[:, None, :]
    H["fks_row"] = np.ascontiguousarray(I["L0_ffn_k_scale"])[:, None, :]
    wq = I["L0_attn_b_w_q"]
    wqT = np.ascontiguousarray(wq.transpose(2, 0, 1).reshape(1024, 1024))
    wk = I["L0_attn_b_w_k"]
    wkT = np.ascontiguousarray(wk.transpose(2, 0, 1).reshape(1024, 1024))
    wv = I["L0_attn_b_w_v"]
    wvT = np.ascontiguousarray(wv.transpose(2, 0, 1).reshape(1024, 1024))
    H["bWq"] = repack_resident_b(wqT)
    H["bWk"] = repack_resident_b(wkT)
    H["bWv"] = repack_resident_b(wvT)
    wo = I["L0_attn_b_w_o"]
    H["bWo"] = repack_resident_b(np.ascontiguousarray(wo.T))
    for k in BF16_W:
        H[k] = H[k].astype(BF)
    H.update(build_host_consts())
    return H



ENGINES = ("pe", "dve", "act", "pool", "sync")
ENG_SEM = {"pe": "sPE", "dve": "sDV", "act": "sAC", "pool": "sPL", "sync": "sSY"}


class Prog:
    def __init__(self):
        self.ops = {e: [] for e in ENGINES}
        self.counts = {}
        self.handles = {}
        self.last_wait = {}
        self.dma_n = {"pool": 0, "sync": 0, "act": 0}
        sems = ["sPE", "sDV", "sAC", "sPL", "sSY"]
        for q in ("qW", "qX", "qS"):
            sems += [f"{q}{i}" for i in range(4)]
        for s in sems:
            self.counts[s] = 0

    DMA_SEM = {"pool": "qW", "sync": "qX", "act": "qS"}

    def wait(self, eng, tok):
        if tok is None:
            return
        sem, val = tok
        if val <= 0:
            return
        key = (eng, sem)
        if self.last_wait.get(key, 0) >= val:
            return
        self.last_wait[key] = val
        self.ops[eng].append(("w", sem, val))

    def emit(self, eng, fn, waits=(), inc=None, inc_n=1):
        for t in waits:
            self.wait(eng, t)
        tok = None
        if inc is not None:
            self.counts[inc] += inc_n
            tok = (inc, self.counts[inc])
            self.ops[eng].append(("i", fn, inc, inc_n))
        else:
            self.ops[eng].append(("n", fn))
        return tok

    def dma(self, eng, out_ap, in_ap, waits=()):
        fam = self.DMA_SEM[eng]
        idx = self.dma_n[eng] % 4
        self.dma_n[eng] += 1
        sem = f"{fam}{idx}"
        waits = list(waits) + [(sem, self.counts[sem])]
        return self.emit(eng, lambda e, o=out_ap, i=in_ap: e.dma_start(out=o, in_=i),
                         waits=waits, inc=sem, inc_n=16)

    def run_stream(self, eng_name, engobj):
        for op in self.ops[eng_name]:
            kind = op[0]
            if kind == "w":
                engobj.wait_ge(self.handles[op[1]], op[2])
            elif kind == "i":
                inst = op[1](engobj)
                inst.then_inc(self.handles[op[2]], op[3])
            else:
                op[1](engobj)


class Slot:
    __slots__ = ("ready", "cons")

    def __init__(self):
        self.ready = None
        self.cons = []

    def war(self):
        out = list(self.cons)
        if self.ready is not None:
            out.append(self.ready)
        return out

    def set(self, tok):
        self.ready = tok
        self.cons = []

    def read(self, tok):
        if tok is not None:
            self.cons.append(tok)


class Arena:

    def __init__(self, nc):
        self.nc = nc
        self.off = (nc.sbuf_base + 63) // 64 * 64
        self.budget = nc.sbuf_top
        self.marks = {}

    def alloc(self, name, shape, dtype, align=64):
        per_part = int(np.prod(shape[1:])) * mybir.dt.size(dtype)
        self.off = (self.off + align - 1) // align * align
        t = self.nc.alloc_sbuf_tensor_at(name, list(shape), dtype, offset=self.off)
        self.off += per_part
        assert self.off <= self.budget, f"SBUF overflow at {name}: {self.off}"
        return t

    def mark(self, name):
        self.marks[name] = self.off

    def reset_to(self, name):
        self.off = self.marks[name]


def rep_mid(ap, nrep):
    s = [list(x) for x in ap.ap]
    assert len(s) == 2
    return bass.AP(ap.tensor, ap.offset, [s[0], [0, nrep], s[1]])


def rep_inner(ap, nrep):
    s = [list(x) for x in ap.ap]
    assert len(s) == 2
    return bass.AP(ap.tensor, ap.offset, [s[0], s[1], [0, nrep]])



CONST_SPECS = {
    "c_ident": ([128, 128], BF16), "c_negident": ([128, 128], BF16),
    "c_maskT_diag": ([128, 128], BF16),
    "c_negmask_local": ([128, 384], BF16),
    "c_negmask_rot0": ([128, 384], BF16), "c_negmask_rot1": ([128, 384], BF16),
    "c_negmaskT_local01": ([128, 128], BF16),
    "c_favor_pat": ([128, 33], BF16),
    "c_indA": ([1, 128], BF16), "c_indB": ([1, 128], BF16),
    "c_ones_1x128": ([1, 128], BF16),
    "c_stI0": ([128, 36], BF16), "c_stI32": ([128, 36], BF16),
}

ROT_SPECS = {
    "c_rot_cs_q": ([128, 16, 64], F32), "c_rot_sn_q": ([128, 16, 64], F32),
    "c_rot_cs_k": ([128, 16, 64], F32), "c_rot_sn_k": ([128, 16, 64], F32),
}

BF16_W = {"aWq", "aWk", "aWv", "aWo", "aW1", "aW2",
          "fWqkv", "fWo", "fW1", "fW2", "bWq", "bWk", "bWv", "bWo"}

WEIGHT_SPECS = {
    "aWq": [3, 8, 128, 8, 128], "aWk": [3, 8, 128, 8, 128],
    "aWv": [3, 128, 8, 1024], "aWo": [3, 128, 8, 1024],
    "aW1": [3, 32, 128, 8, 128], "aW2": [3, 8, 128, 32, 128],
    "abq": [3, 128, 8], "abk": [3, 128, 8], "abv_row": [3, 1, 1024],
    "abo": [3, 128, 8], "abf1": [3, 128, 32], "abf2": [3, 128, 8],
    "aln1g": [3, 128, 8], "aln1b": [3, 128, 8],
    "aln2g": [3, 128, 8], "aln2b": [3, 128, 8],
    "aomega": [3, 64, 32],
    "fWqkv": [6, 128, 8, 3072], "fWo": [6, 8, 128, 8, 128],
    "fW1": [6, 22, 128, 8, 256], "fb1": [6, 128, 44],
    "fW2": [6, 8, 128, 22, 128], "fb2": [6, 128, 8],
    "flng_col": [6, 128, 8], "flng_row": [6, 1, 1024], "flnb_col": [6, 128, 8],
    "fqs_row": [6, 1, 64], "fks_row": [6, 1, 64],
    "bWq": [128, 8, 1024], "bWk": [128, 8, 1024], "bWv": [128, 8, 1024],
    "bWo": [128, 8, 1024],
}


class MK:
    def __init__(self, nseq=SEQ_PER_CORE, phases="ABCD", debug=()):
        self.nseq = nseq
        self.phases = phases
        self.debug = set(debug)
        nc = bass.Bass("TRN2", target_bir_lowering=False)
        self.nc = nc
        self.P = Prog()
        self.xT = nc.dram_tensor("xT", [nseq, D, S], F32, kind="ExternalInput").ap()
        self.outT = nc.dram_tensor("outT", [nseq, D, S], F32, kind="ExternalOutput").ap()
        self.xr = [nc.dram_tensor(f"xr{i}", [D, S], F32, kind="Internal").ap()
                   for i in range(2)]
        self.W = {k: nc.dram_tensor(k, list(v if isinstance(v, list) else v[0]),
                                    BF16 if k in BF16_W else F32,
                                    kind="ExternalInput").ap()
                  for k, v in WEIGHT_SPECS.items()}
        self.C = {k: nc.dram_tensor(k, list(sh), dt, kind="ExternalInput").ap()
                  for k, (sh, dt) in {**CONST_SPECS, **ROT_SPECS}.items()}
        self.dbg = {}
        for name in self.debug:
            if isinstance(name, tuple):
                nm, shape = name
                self.dbg[nm] = nc.dram_tensor(
                    "dbg_" + nm, list(shape), F32, kind="ExternalOutput").ap()
            else:
                self.dbg[name] = nc.dram_tensor(
                    "dbg_" + name, [D, S], F32, kind="ExternalOutput").ap()

        A = Arena(nc)
        self.arena = A
        self.XB = A.alloc("XB", [128, 8 * 2052], BF16)
        self.XL = A.alloc("XL", [128, 8 * 2052], BF16)
        self.cs = {k: A.alloc("s_" + k, list(sh), dt)
                   for k, (sh, dt) in CONST_SPECS.items()}
        self.ccols = A.alloc("ccols", [128, 8], F32)
        self.pcol = A.alloc("pcol", [128, 96], F32)
        self.prow = A.alloc("prow", [1, 1024], BF16)
        self.t2 = A.alloc("t2", [128, 8, 512], F32)
        self.t2b = A.alloc("t2b", [128, 512], BF16)
        self.x2b = A.alloc("x2b", [128, 512], BF16)
        self.srow = A.alloc("srow", [1, 512], BF16)
        self.s2row = A.alloc("s2row", [1, 512], BF16)
        self.negmu_row = A.alloc("negmu_row", [1, 512], BF16)
        self.mu_t = A.alloc("mu_t", [128, 512], F32)
        self.sq_t = A.alloc("sq_t", [128, 512], F32)
        self.rb_t = A.alloc("rb_t", [128, 512], F32)
        self.d1_t = A.alloc("d1_t", [128, 512], F32)
        self.hs_t = A.alloc("hs_t", [128, 512], F32)
        self.xprev = A.alloc("xprev", [128, 512], F32)
        A.mark("phase")

        self.ps = [nc.alloc_psum_tensor(f"psb{i}", [128, 512], F32) for i in range(7)]
        self.psbf = nc.alloc_psum_tensor("psbf", [128, 1024], BF16)
        self.ps_slot = [Slot() for _ in range(8)]

        self.sXB = [[Slot() for _ in range(NT512)] for _ in range(NCHUNK)]
        self.sXL = [[Slot() for _ in range(NT512)] for _ in range(NCHUNK)]
        self.slot = {}
        if "A" in self.phases:
            self.alloc_zone_a()
        if "B" in self.phases or "D" in self.phases:
            self.alloc_zone_b()
        if "C" in self.phases:
            self.alloc_zone_c()

    def S_(self, name):
        if name not in self.slot:
            self.slot[name] = Slot()
        return self.slot[name]

    def xb_ap(self, c, t0, n, shifted=False, lo=False):
        t = self.XL if lo else self.XB
        tile, within = divmod(t0, 512)
        assert within + n <= 512
        off = c * 2052 + tile * 513 + (0 if shifted else 1) + within
        return t[:, off: off + n]

    def prologue(self):
        P = self.P
        toks = []
        for k in self.cs:
            toks.append(P.dma("sync", self.cs[k][:], self.C[k]))
        P.emit("pool", lambda e: e.memset(self.ccols[:, 0:1], LN_EPS))
        P.emit("pool", lambda e: e.memset(self.ccols[:, 1:2], -2.0794415416798357))
        P.emit("pool", lambda e: e.memset(self.ccols[:, 2:8], 0.0))
        t = P.emit("pool", lambda e: e.memset(self.ccols[:, 3:4], 1e-6), inc="sPL")
        P.emit("pool", lambda e: e.memset(
            bass.AP(self.XB, 0, [[8 * 2052, 128], [2052, 8], [513, 4], [1, 1]]), 0.0))
        P.emit("pool", lambda e: e.memset(
            bass.AP(self.XL, 0, [[8 * 2052, 128], [2052, 8], [513, 4], [1, 1]]), 0.0))
        tz = P.emit("pool", lambda e: e.engine_nop(), inc="sPL")
        self.ccols_tok = t
        self.pad_tok = tz
        ctoks = [(f"qX{i}", P.counts[f"qX{i}"]) for i in range(4)]
        for e in ENGINES:
            for ct in ctoks:
                P.wait(e, ct)
            P.wait(e, tz)

    def ln_rowmath(self, stats_bank, bcast_bank, eps_col=None):
        P = self.P
        ps = self.ps[stats_bank]
        pslot = self.ps_slot[stats_bank]
        bb = self.ps[bcast_bank]
        bslot = self.ps_slot[bcast_bank]
        if eps_col is None:
            eps_col = self.ccols[:, 0:1]
        tsr = P.emit("dve", lambda e: e.tensor_copy(self.srow[:], ps[0:1, :]),
                     waits=self.S_("srow").war() + [pslot.ready], inc="sDV")
        self.S_("srow").set(tsr)
        ts2 = P.emit("dve", lambda e: e.tensor_copy(self.s2row[:], ps[32:33, :]),
                     waits=self.S_("s2row").war(), inc="sDV")
        self.S_("s2row").set(ts2)
        pslot.read(ts2)
        tnm = P.emit("dve", lambda e: e.tensor_scalar_mul(
            self.negmu_row[:], self.srow[:], -1.0 / 1024.0),
            waits=self.S_("negmu_row").war() + [tsr], inc="sDV")
        self.S_("negmu_row").set(tnm)
        tba = P.emit("pe", lambda e: e.matmul(
            bb[:, :], self.cs["c_ones_1x128"][:], self.srow[:],
            start=True, stop=True),
            waits=[tsr] + bslot.war(), inc="sPE")
        bslot.set(tba)
        tmu = P.emit("act", lambda e: e.activation(
            self.mu_t[:], bb[:, :], AF.Copy, bias=0.0, scale=1.0 / 1024.0),
            waits=[tba] + self.S_("mu_t").war(), inc="sAC")
        self.S_("mu_t").set(tmu)
        tsq = P.emit("act", lambda e: e.activation(
            self.sq_t[:], bb[:, :], AF.Square, bias=0.0, scale=1.0 / 1024.0),
            waits=[tba] + self.S_("sq_t").war(), inc="sAC")
        self.S_("sq_t").set(tsq)
        bslot.read(tmu)
        bslot.read(tsq)
        tbb = P.emit("pe", lambda e: e.matmul(
            bb[:, :], self.cs["c_ones_1x128"][:], self.s2row[:],
            start=True, stop=True),
            waits=[ts2] + bslot.war(), inc="sPE")
        bslot.set(tbb)
        tv = P.emit("dve", lambda e: e.scalar_tensor_tensor(
            self.sq_t[:], bb[:, :], 1.0 / 1024.0, self.sq_t[:],
            ALU.mult, ALU.subtract),
            waits=[tbb, tsq], inc="sDV")
        self.S_("sq_t").set(tv)
        bslot.read(tv)
        tl = P.emit("act", lambda e: e.activation(
            self.sq_t[:], self.sq_t[:], AF.Ln, bias=eps_col, scale=1.0),
            waits=[tv], inc="sAC")
        self.S_("sq_t").set(tl)
        tr = P.emit("act", lambda e: e.activation(
            self.rb_t[:], self.sq_t[:], AF.Exp, bias=0.0, scale=-0.5),
            waits=[tl] + self.S_("rb_t").war(), inc="sAC")
        self.S_("sq_t").read(tr)
        self.S_("rb_t").set(tr)
        return tmu, tr, tnm

    def stats_mms(self, bank, nc_idx, src_b_ap, src_sq_ap, toks):
        P = self.P
        ps = self.ps[bank]
        first = nc_idx == 0
        t1 = P.emit("pe", lambda e: e.matmul(
            ps[0:36, :], self.cs["c_stI0"][:], src_b_ap,
            start=first, stop=False),
            waits=toks + (self.ps_slot[bank].war() if first else []), inc="sPE")
        if first:
            self.ps_slot[bank].set(t1)
        t2 = P.emit("pe", lambda e: e.matmul(
            ps[0:36, :], self.cs["c_stI32"][:], src_sq_ap,
            start=False, stop=(nc_idx == NCHUNK - 1)),
            waits=toks, inc="sPE")
        self.ps_slot[bank].set(t2)
        return t2


    def stream_split(self, c, t, src_tok):
        P = self.P
        xb = self.xb_ap(c, t * 512, 512)
        xl = self.xb_ap(c, t * 512, 512, lo=True)
        sb, sl = self.sXB[c][t], self.sXL[c][t]
        tb = P.emit("act", lambda e, xb=xb: e.activation(
            xb, self.hs_t[:], AF.Copy, bias=0.0, scale=1.0),
            waits=[src_tok] + sb.war(), inc="sAC")
        if t < NT512 - 1:
            pad = self.XB[:, c * 2052 + (t + 1) * 513: c * 2052 + (t + 1) * 513 + 1]
            tb = P.emit("act", lambda e, pad=pad: e.activation(
                pad, self.hs_t[:, 511:512], AF.Copy, bias=0.0, scale=1.0),
                waits=[src_tok] + self.sXB[c][t + 1].war(), inc="sAC")
            self.sXB[c][t + 1].read(tb)
        sb.set(tb)
        tl = P.emit("pool", lambda e, xb=xb, xl=xl: e.tensor_sub(
            xl, self.hs_t[:], xb),
            waits=[src_tok, tb] + sl.war(), inc="sPL")
        sl.set(tl)
        self.S_("hs_t").read(tl)
        return tb, tl

    def tile_stats(self, c, t2c_ap, t2tok, bank=4):
        P = self.P
        s1 = self.S_("t2b")
        tc = P.emit("act", lambda e, a=t2c_ap: e.activation(
            self.t2b[:], a, AF.Copy, bias=0.0, scale=1.0),
            waits=[t2tok] + s1.war(), inc="sAC")
        s1.set(tc)
        s2 = self.S_("x2b")
        tq = P.emit("act", lambda e, a=t2c_ap: e.activation(
            self.x2b[:], a, AF.Square, bias=0.0, scale=1.0),
            waits=[t2tok] + s2.war(), inc="sAC")
        s2.set(tq)
        tk = self.stats_mms(bank, c, self.t2b[:], self.x2b[:], [tc, tq])
        s1.read(tk)
        s2.read(tk)
        return tk

    def emit_boundary(self, seq, x_prev, x_new, mode):
        P = self.P
        last_out = None
        for t in range(NT512):
            t2tok = [None] * NCHUNK
            for c in range(NCHUNK):
                t2c = self.t2[:, c, :]
                st2 = self.S_(f"t2_{c}")
                xp_ap = x_prev[c * 128:(c + 1) * 128, t * 512:(t + 1) * 512]
                if mode == "init":
                    tok = P.dma("sync", t2c, xp_ap, waits=st2.war())
                else:
                    sxp = self.S_("xprev")
                    tld = P.dma("sync", self.xprev[:], xp_ap, waits=sxp.war())
                    sxp.set(tld)
                    xb = self.xb_ap(c, t * 512, 512)
                    xl = self.xb_ap(c, t * 512, 512, lo=True)
                    sd = self.S_("d1_t")
                    ta = P.emit("dve", lambda e, t2c=t2c, xb=xb, c=c: e.tensor_add(
                        self.d1_t[:], self.xprev[:], xb),
                        waits=[tld, self.sXB[c][t].ready] + sd.war(), inc="sDV")
                    sd.set(ta)
                    sxp.read(ta)
                    self.sXB[c][t].read(ta)
                    tok = P.emit("pool", lambda e, t2c=t2c, xl=xl: e.tensor_add(
                        t2c, self.d1_t[:], xl),
                        waits=[ta, self.sXL[c][t].ready] + st2.war(), inc="sPL")
                    sd.read(tok)
                    self.sXL[c][t].read(tok)
                st2.set(tok)
                if mode != "init":
                    xn_ap = x_new[c * 128:(c + 1) * 128, t * 512:(t + 1) * 512]
                    last_out = P.dma("act", xn_ap, t2c, waits=[tok])
                    st2.read(last_out)
                t2tok[c] = tok
                if mode != "final":
                    tk = self.tile_stats(c, t2c, tok)
                    st2.read(tk)
            if mode == "final":
                continue
            tmu, trb, _ = self.ln_rowmath(4, 5)
            for c in range(NCHUNK):
                t2c = self.t2[:, c, :]
                st2 = self.S_(f"t2_{c}")
                sd = self.S_("d1_t")
                t1 = P.emit("dve", lambda e, t2c=t2c: e.tensor_sub(
                    self.d1_t[:], t2c, self.mu_t[:]),
                    waits=[t2tok[c], tmu] + sd.war(), inc="sDV")
                sd.set(t1)
                st2.read(t1)
                self.S_("mu_t").read(t1)
                sh = self.S_("hs_t")
                t2_ = P.emit("dve", lambda e: e.tensor_mul(
                    self.hs_t[:], self.d1_t[:], self.rb_t[:]),
                    waits=[t1, trb] + sh.war(), inc="sDV")
                sh.set(t2_)
                sd.read(t2_)
                self.S_("rb_t").read(t2_)
                self.stream_split(c, t, t2_)
        return last_out

    def dump_stream(self, name):
        if name not in self.dbg:
            return
        P = self.P
        dst = self.dbg[name]
        for c in range(NCHUNK):
            for t in range(NT512):
                sd = self.S_("d1_t")
                xb = self.xb_ap(c, t * 512, 512)
                xl = self.xb_ap(c, t * 512, 512, lo=True)
                tk = P.emit("dve", lambda e, xb=xb, xl=xl: e.tensor_add(
                    self.d1_t[:], xb, xl),
                    waits=[self.sXB[c][t].ready, self.sXL[c][t].ready] + sd.war(),
                    inc="sDV")
                sd.set(tk)
                self.sXB[c][t].read(tk)
                self.sXL[c][t].read(tk)
                td = P.dma("act", dst[c * 128:(c + 1) * 128, t * 512:(t + 1) * 512],
                           self.d1_t[:], waits=[tk])
                sd.read(td)
                self.last_dbg = td

    def dump_tile(self, name, src_ap, waits, slots=()):
        if name not in self.dbg:
            return
        td = self.P.dma("pool", self.dbg[name][:], src_ap, waits=list(waits))
        for s in slots:
            self.S_(s).read(td)
        self.last_dbg = td


    def build(self):
        P = self.P
        self.last_dbg = None
        self.prologue()
        final_toks = []
        for seq in range(self.nseq):
            xin = self.xT[seq]
            self.emit_boundary(seq, xin, None, "init")
            if seq == 0:
                self.dump_stream("a_in")
            if "A" in self.phases:
                self.phase_attn_a(seq)
                if seq == 0:
                    self.dump_stream("a_out")
            self.emit_boundary(seq, xin, self.xr[0], "stream")
            if seq == 0:
                self.dump_stream("b_in")
            if "B" in self.phases:
                self.phase_ffn(seq, first=True)
            self.emit_boundary(seq, self.xr[0], self.xr[1], "stream")
            if seq == 0:
                self.dump_stream("c_in")
            if "C" in self.phases:
                self.phase_attn_b(seq)
            else:
                self.emit_boundary(seq, self.xr[1], self.xr[0], "stream")
            if seq == 0:
                self.dump_stream("d_in")
            if "D" in self.phases:
                self.phase_ffn(seq, first=False)
            tk = self.emit_boundary(seq, self.xr[0], self.outT[seq], "final")
            final_toks.append(tk)
        P.wait("act", final_toks[-1])
        if self.last_dbg is not None:
            P.wait("act", self.last_dbg)

    def barrier(self):
        toks = [(s, v) for s, v in self.P.counts.items()]
        for e in ENGINES:
            for t in toks:
                self.P.wait(e, t)

    def alloc_zone_a(self):
        A = self.arena
        A.reset_to("phase")
        z = {}
        z["wv"] = A.alloc("za_wv", [128, 8, 1024], BF16)
        z["wo"] = A.alloc("za_wo", [128, 8, 1024], BF16)
        z["slab"] = A.alloc("za_slab", [128, 2, 8, 128], BF16)
        z["bdiag"] = A.alloc("za_bdiag", [128, 128], BF16)
        z["omega"] = A.alloc("za_omega", [64, 32], F32)
        z["qfm"] = A.alloc("za_qfm", [128, 2, 1024], BF16)
        z["kfm"] = A.alloc("za_kfm", [128, 2, 1024], BF16)
        z["Qf"] = A.alloc("za_Qf", [128, 8, 1024], BF16)
        z["Kf"] = A.alloc("za_Kf", [128, 8, 1024], BF16)
        z["vaug"] = A.alloc("za_vaug", [128, 2, 16, 65], BF16)
        z["abuf"] = A.alloc("za_abuf", [128, 8, 1024], BF16)
        z["negh"] = A.alloc("za_negh", [1, 512], BF16)
        z["ktm"] = A.alloc("za_ktm", [128, 128], BF16)
        z["state"] = A.alloc("za_state", [128, 65], BF16)
        z["state_sb"] = A.alloc("za_state_sb", [128, 8, 65], F32)
        z["rec"] = A.alloc("za_rec", [1, 256], BF16)
        z["rb"] = A.alloc("za_rb", [64, 256], BF16)
        z["atsb"] = A.alloc("za_atsb", [128, 256], BF16)
        attn_end = A.off
        A.reset_to("phase")
        z["fslab"] = A.alloc("zaf_slab", [128, 4, 8, 128], BF16)
        z["fslab2"] = A.alloc("zaf_slab2", [128, 4, 32, 128], BF16)
        z["hgr"] = A.alloc("zaf_hgr", [128, 32, 512], BF16)
        A.off = max(A.off, attn_end)
        self.za = z

    def alloc_zone_b(self):
        A = self.arena
        A.reset_to("phase")
        z = {}
        z["wqkv"] = A.alloc("zb_wqkv", [128, 8, 3072], BF16)
        A.mark("bpool")
        z["rot"] = A.alloc("zb_rot", [128, 4, 16, 64], F32)
        z["kroll"] = A.alloc("zb_kroll", [128, 8, 3, 128], BF16)
        z["vroll"] = A.alloc("zb_vroll", [128, 3, 16, 65], BF16)
        z["qcur"] = A.alloc("zb_qcur", [128, 8, 128], BF16)
        z["qhat"] = A.alloc("zb_qhat", [128, 2048], BF16)
        z["sq"] = A.alloc("zb_sq", [128, 1024], BF16)
        z["rsc1"] = A.alloc("zb_rsc1", [128, 1024], F32)
        z["attout"] = A.alloc("zb_attout", [128, 8, 512], BF16)
        z["pt"] = A.alloc("zb_pt", [128, 2, 384], BF16)
        z["sm"] = A.alloc("zb_sm", [128, 768], BF16)
        z["norms"] = A.alloc("zb_norms", [128, 32], F32)
        z["rstd"] = A.alloc("zb_rstd", [128, 32], F32)
        z["qstile"] = A.alloc("zb_qstile", [128, 128], BF16)
        z["woslab"] = A.alloc("zb_woslab", [128, 2, 8, 128], BF16)
        attn_end = A.off
        A.reset_to("bpool")
        z["gg"] = A.alloc("zbf_gg", [128, 22, 512], BF16)
        z["w1slab"] = A.alloc("zbf_w1slab", [128, 2, 8, 256], BF16)
        z["w2slab"] = A.alloc("zbf_w2slab", [128, 2, 22, 128], BF16)
        z["xg"] = A.alloc("zbf_xg", [128, 8, 512], BF16)
        z["cb2"] = A.alloc("zbf_cb", [128, 44], F32)
        A.off = max(A.off, attn_end)
        self.zb = z

    def alloc_zone_c(self):
        A = self.arena
        A.reset_to("phase")
        z = {}
        z["w1"] = A.alloc("zc_w1", [128, 8, 1024], BF16)
        off_w2 = A.off
        z["w2"] = A.alloc("zc_w2", [128, 8, 1024], BF16)
        z["K"] = A.alloc("zc_K", [128, 8, 2048], BF16)
        z["V"] = A.alloc("zc_V", [128, 16, 16, 65], BF16)
        z["Q"] = A.alloc("zc_Q", [128, 8, 128], BF16)
        z["ao"] = A.alloc("zc_ao", [128, 8, 128], BF16)
        z["pt"] = A.alloc("zc_pt", [128, 2, 128], BF16)
        z["m"] = A.alloc("zc_m", [128, 4], F32)
        z["mb"] = A.alloc("zc_mb", [128, 2], BF16)
        z["xp"] = A.alloc("zc_xp", [128, 128], F32)
        end = A.off
        A.off = off_w2
        z["woslab"] = A.alloc("zc_woslab", [128, 2, 8, 128], BF16)
        A.off = max(A.off, end)
        self.zc = z


    def phase_attn_a(self, seq):
        for l in range(3):
            self.perf_layer(seq, l)

    def perf_layer(self, seq, l):
        P = self.P
        z = self.za
        self.barrier()
        twv = P.dma("pool", z["wv"][:], self.W["aWv"][l])
        two = P.dma("sync", z["wo"][:], self.W["aWo"][l])
        tbv = P.dma("pool", self.prow[:], self.W["abv_row"][l])
        tom = P.dma("sync", z["omega"][:], self.W["aomega"][l])
        pc = []
        for i, nm in enumerate(("abq", "abk", "abo", "aln1g", "aln1b",
                                "aln2g", "aln2b", "abf2")):
            pc.append(P.dma("sync", self.pcol[:, 8 * i: 8 * i + 8], self.W[nm][l]))
        tbf1 = P.dma("sync", self.pcol[:, 64:96], self.W["abf1"][l])
        tpc = pc[-1]
        self.ptoks = pc + [tbf1]
        sc = 0.3535533905932738
        tm0 = P.emit("pool", lambda e: e.memset(z["bdiag"][:], 0.0),
                     waits=[], inc="sPL")
        tb1 = P.emit("act", lambda e: e.activation(
            z["bdiag"][0:64, 0:32], z["omega"][:], AF.Copy, bias=0.0, scale=sc),
            waits=[tom, tm0], inc="sAC")
        tb2 = P.emit("act", lambda e: e.activation(
            z["bdiag"][0:64, 32:64], z["omega"][:], AF.Copy, bias=0.0, scale=-sc),
            waits=[tom, tm0], inc="sAC")
        tb3 = P.emit("act", lambda e: e.activation(
            z["bdiag"][64:128, 64:96], z["omega"][:], AF.Copy, bias=0.0, scale=sc),
            waits=[tom, tm0], inc="sAC")
        tb4 = P.emit("act", lambda e: e.activation(
            z["bdiag"][64:128, 96:128], z["omega"][:], AF.Copy, bias=0.0, scale=-sc),
            waits=[tom, tm0], inc="sAC")
        tbd = tb4
        tvo = P.emit("pool", lambda e: e.memset(
            bass.AP(z["vaug"], 64, [[2080, 128], [65, 2 * 16], [1, 1]]), 1.0),
            inc="sPL")
        tsz = P.emit("pool", lambda e: e.memset(z["state_sb"][:], 0.0),
                     waits=self.S_("a_state_sb").war(), inc="sPL")
        self.S_("a_state_sb").set(tsz)

        for half in range(2):
            hb = half * 1024
            for proj, wname, bcol0, fm, Ff in (
                    ("q", "aWq", 0, z["qfm"], z["Qf"]),
                    ("k", "aWk", 8, z["kfm"], z["Kf"])):
                for ncx in range(NCHUNK):
                    sl = ncx % 2
                    slab = z["slab"][:, sl, :, :]
                    ssl = self.S_(f"a_slab{sl}")
                    tw = P.dma("pool" if ncx % 2 == 0 else "sync",
                               slab, self.W[wname][l, ncx], waits=ssl.war())
                    ssl.set(tw)
                    for t2 in range(2):
                        bank = 4 + t2
                        ps = self.ps[bank]
                        pslot = self.ps_slot[bank]
                        mmtok = None
                        for f in range(NCHUNK):
                            first, last = f == 0, f == NCHUNK - 1
                            rhs = self.xb_ap(f, hb + t2 * 512, 512)
                            w = [tw, self.sXB[f][2 * half + t2].ready]
                            if first:
                                w += pslot.war()
                            mmtok = P.emit("pe", lambda e, ps=ps, slab=slab, f=f,
                                           rhs=rhs, first=first, last=last: e.matmul(
                                ps[:, :], slab[:, f, :], rhs,
                                start=first, stop=last),
                                waits=w, inc="sPE" if last else None)
                            if first:
                                pslot.set(mmtok)
                        pslot.set(mmtok)
                        ssl.read(mmtok)
                        for f in range(NCHUNK):
                            self.sXB[f][2 * half + t2].read(mmtok)
                        sfm = self.S_(f"a_fm_{proj}{ncx % 2}")
                        dst = fm[:, ncx % 2, t2 * 512:(t2 + 1) * 512]
                        te = P.emit("act", lambda e, dst=dst, ps=ps, ncx=ncx,
                                    bcol0=bcol0: e.activation(
                            dst, ps[:, :], AF.Identity,
                            bias=self.pcol[:, bcol0 + ncx: bcol0 + ncx + 1],
                            scale=1.0),
                            waits=[mmtok, tpc] + sfm.war(), inc="sAC")
                        sfm.set(te)
                        pslot.read(te)
                        sq2 = self.S_("x2b")
                        tq2 = P.emit("act", lambda e, dst=dst: e.activation(
                            self.x2b[:], dst, AF.Square, bias=0.0, scale=1.0),
                            waits=[te] + sq2.war(), inc="sAC")
                        sq2.set(tq2)
                        ps6 = self.ps[6]
                        ps6s = self.ps_slot[6]
                        th = P.emit("pe", lambda e: e.matmul(
                            ps6[0:33, 0:512], self.cs["c_favor_pat"][:],
                            self.x2b[:], start=True, stop=True),
                            waits=[tq2] + ps6s.war(), inc="sPE")
                        ps6s.set(th)
                        sq2.read(th)
                        snh = self.S_("a_negh")
                        tnA = P.emit("dve", lambda e: e.tensor_copy(
                            self.za["negh"][:], ps6[0:1, 0:512]),
                            waits=[th] + snh.war(), inc="sDV")
                        snh.set(tnA)
                        tu0 = P.emit("pe", lambda e, ps=ps, dst=dst: e.matmul(
                            ps[:, :], self.za["bdiag"][:], dst,
                            start=True, stop=False),
                            waits=[te, tbd] + pslot.war(), inc=None)
                        tu1 = P.emit("pe", lambda e, ps=ps: e.matmul(
                            ps[:, :], self.cs["c_indA"][:],
                            self.za["negh"][:], start=False, stop=False),
                            waits=[tnA], inc="sPE")
                        snh.read(tu1)
                        tnB = P.emit("dve", lambda e: e.tensor_copy(
                            self.za["negh"][:], ps6[32:33, 0:512]),
                            waits=[th, tu1] + snh.war(), inc="sDV")
                        snh.set(tnB)
                        ps6s.read(tnB)
                        tu2 = P.emit("pe", lambda e, ps=ps: e.matmul(
                            ps[:, :], self.cs["c_indB"][:],
                            self.za["negh"][:], start=False, stop=True),
                            waits=[tnB], inc="sPE")
                        pslot.set(tu2)
                        snh.read(tu2)
                        sF = self.S_(f"a_F_{proj}{ncx}")
                        tF = P.emit("act", lambda e, ps=ps, Ff=Ff, ncx=ncx,
                                    t2=t2: e.activation(
                            Ff[:, ncx, t2 * 512:(t2 + 1) * 512], ps[:, :], AF.Exp,
                            bias=self.ccols[:, 1:2], scale=1.0),
                            waits=[tu2, self.ccols_tok] + sF.war(), inc="sAC")
                        sF.set(tF)
                        pslot.read(tF)
                        sfm.read(tq2)
                        sfm.read(tu2)
            self.perf_scan_half(seq, l, half, twv, tbv, tvo)
            for t2 in range(2):
                self.perf_wo_ln1(seq, l, half, t2, two)
        self.barrier()
        for t in range(NT512):
            self.perf_ffn_tile(seq, l, t)

    def perf_scan_half(self, seq, l, half, twv, tbv, tvo):
        P = self.P
        z = self.za
        hb = half * 1024
        ps4, ps5, ps6 = self.ps[4], self.ps[5], self.ps[6]
        s4, s5, s6 = self.ps_slot[4], self.ps_slot[5], self.ps_slot[6]
        psbf, sbf = self.psbf, self.ps_slot[7]
        for c in range(8):
            ts_abs = 8 * half + c
            vslot = c % 2
            sv = self.S_(f"a_vaug{vslot}")
            for nh in range(2):
                mmtok = None
                for f in range(NCHUNK):
                    first = f == 0
                    lhs = self.xb_ap(f, hb + c * 128, 128)
                    w = [twv, self.sXB[f][ts_abs // 4].ready]
                    if first:
                        w += s6.war()
                    mmtok = P.emit("pe", lambda e, f=f, lhs=lhs, nh=nh: e.matmul(
                        ps6[:, 0:512], lhs, z["wv"][:, f, nh * 512:(nh + 1) * 512],
                        start=f == 0, stop=False), waits=w, inc=None)
                    if first:
                        s6.set(mmtok)
                    self.sXB[f][ts_abs // 4].read(mmtok)
                tb = P.emit("pe", lambda e, nh=nh: e.matmul(
                    ps6[:, 0:512], self.cs["c_ones_1x128"][:],
                    self.prow[0:1, nh * 512:(nh + 1) * 512],
                    start=False, stop=True), waits=[tbv], inc="sPE")
                s6.set(tb)
                for f in range(NCHUNK):
                    self.sXB[f][ts_abs // 4].read(tb)
                te = P.emit("dve", lambda e, nh=nh, vslot=vslot: e.tensor_copy(
                    z["vaug"][:, vslot, nh * 8:(nh + 1) * 8, 0:64],
                    ps6[:, 0:512].rearrange("p (h d) -> p h d", h=8)),
                    waits=[tb, tvo] + (sv.war() if nh == 0 else []), inc="sDV")
                if nh == 0:
                    sv.set(te)
                s6.read(te)
            sv.set(te)
            for hp in range(8):
                sQ = self.S_(f"a_F_q{hp}")
                sK = self.S_(f"a_F_k{hp}")
                qA = z["Qf"][0:64, hp, c * 128:(c + 1) * 128]
                qB = z["Qf"][64:128, hp, c * 128:(c + 1) * 128]
                kA = z["Kf"][0:64, hp, c * 128:(c + 1) * 128]
                kB = z["Kf"][64:128, hp, c * 128:(c + 1) * 128]
                first_chunk = (half == 0 and c == 0)
                tt_ = P.emit("pe", lambda e, hp=hp, c=c: e.transpose(
                    psbf[:, hp * 128:(hp + 1) * 128],
                    z["Kf"][:, hp, c * 128:(c + 1) * 128], self.cs["c_ident"][:]),
                    waits=[sK.ready] + sbf.war(), inc="sPE")
                sbf.set(tt_)
                sK.read(tt_)
                sktm = self.S_("a_ktm")
                tk = P.emit("dve", lambda e, hp=hp: e.tensor_copy(
                    z["ktm"][:], psbf[:, hp * 128:(hp + 1) * 128]),
                    waits=[tt_] + sktm.war(), inc="sDV")
                sktm.set(tk)
                sbf.read(tk)
                sst = self.S_("a_state")
                ssb = self.S_("a_state_sb")
                if not first_chunk:
                    tse = P.emit("dve", lambda e, hp=hp: e.tensor_copy(
                        z["state"][:], z["state_sb"][:, hp, :]),
                        waits=[ssb.ready] + sst.war(), inc="sDV")
                    sst.set(tse)
                    ssb.read(tse)
                su1 = P.emit("pe", lambda e, vslot=vslot, hp=hp: e.matmul(
                    ps5[0:64, 256:321], z["ktm"][:, 0:64],
                    z["vaug"][:, vslot, 2 * hp, :],
                    start=True, stop=True),
                    waits=[tk, sv.ready] + s5.war(), inc=None)
                su2 = P.emit("pe", lambda e, vslot=vslot, hp=hp: e.matmul(
                    ps5[0:64, 321:386], z["ktm"][:, 64:128],
                    z["vaug"][:, vslot, 2 * hp + 1, :],
                    start=True, stop=True),
                    waits=[], inc="sPE")
                s5.set(su2)
                sktm.read(su2)
                sv.read(su2)
                tua = P.emit("dve", lambda e, hp=hp: e.tensor_add(
                    z["state_sb"][0:64, hp, :], z["state_sb"][0:64, hp, :],
                    ps5[0:64, 256:321]),
                    waits=[su2] + ([sst.ready] if not first_chunk else []) +
                          ssb.war(), inc="sDV")
                tub = P.emit("dve", lambda e, hp=hp: e.tensor_add(
                    z["state_sb"][64:128, hp, :], z["state_sb"][64:128, hp, :],
                    ps5[0:64, 321:386]),
                    waits=[su2], inc="sDV")
                ssb.set(tub)
                s5.read(tub)
                ta1 = P.emit("pe", lambda e, kA=kA, qA=qA: e.matmul(
                    ps4[:, 0:128], kA, qA, start=True, stop=True,
                    tile_position=(0, 0)),
                    waits=[sK.ready, sQ.ready] + s4.war(), inc="sPE")
                s4.set(ta1)
                ta2 = P.emit("pe", lambda e, kB=kB, qB=qB: e.matmul(
                    ps5[:, 0:128], kB, qB, start=True, stop=True,
                    tile_position=(64, 0)),
                    waits=[sK.ready, sQ.ready] + s5.war(), inc="sPE")
                s5.set(ta2)
                sK.read(ta1); sK.read(ta2); sQ.read(ta1); sQ.read(ta2)
                sat = self.S_("a_atsb")
                tm1 = P.emit("dve", lambda e: e.tensor_mul(
                    z["atsb"][:, 0:128], ps4[:, 0:128], self.cs["c_maskT_diag"][:]),
                    waits=[ta1] + sat.war(), inc="sDV")
                sat.set(tm1)
                s4.read(tm1)
                tm2 = P.emit("dve", lambda e: e.tensor_mul(
                    z["atsb"][:, 128:256], ps5[:, 0:128], self.cs["c_maskT_diag"][:]),
                    waits=[ta2], inc="sDV")
                sat.set(tm2)
                s5.read(tm2)
                tfin = None
                for hh in range(2):
                    c0 = 256 + 128 * hh
                    vh = 2 * hp + hh
                    at0 = 128 * hh
                    qh = qA if hh == 0 else qB
                    last = (hh == 1) and first_chunk
                    tfin = P.emit("pe", lambda e, c0=c0, vh=vh, at0=at0,
                                  vslot=vslot, fc=first_chunk: e.matmul(
                        ps4[0:65, c0:c0 + 128], z["vaug"][:, vslot, vh, :],
                        z["atsb"][:, at0:at0 + 128],
                        start=True, stop=fc),
                        waits=[tm1, tm2], inc="sPE" if last else None)
                    if not first_chunk:
                        last = (hh == 1)
                        tfin = P.emit("pe", lambda e, c0=c0, hh=hh, qh=qh: e.matmul(
                            ps4[0:65, c0:c0 + 128],
                            z["state"][64 * hh:64 * hh + 64, :],
                            qh, start=False, stop=True),
                            waits=[sst.ready], inc="sPE" if last else None)
                sv.read(tfin)
                sat.read(tfin)
                sQ.read(tfin)
                if not first_chunk:
                    sst.read(tfin)
                s4.set(tfin)
                sden = self.S_("sq_t")
                td = P.emit("act", lambda e: e.activation(
                    self.sq_t[0:1, 0:256], ps4[64:65, 256:512], AF.Ln,
                    bias=self.ccols[0:1, 3:4], scale=1.0),
                    waits=[tfin, self.ccols_tok] + sden.war(), inc="sAC")
                sden.set(td)
                s4.read(td)
                srec = self.S_("a_rec")
                trc = P.emit("act", lambda e: e.activation(
                    z["rec"][:], self.sq_t[0:1, 0:256], AF.Exp,
                    bias=0.0, scale=-1.0),
                    waits=[td] + srec.war(), inc="sAC")
                srec.set(trc)
                sden.read(trc)
                trb = P.emit("pe", lambda e: e.matmul(
                    ps5[0:64, 128:384], self.cs["c_ones_1x128"][0:1, 0:64],
                    z["rec"][:], start=True, stop=True),
                    waits=[trc] + s5.war(), inc="sPE")
                s5.set(trb)
                srec.read(trb)
                srb = self.S_("a_rb")
                trbe = P.emit("dve", lambda e: e.tensor_copy(
                    z["rb"][:], ps5[0:64, 128:384]),
                    waits=[trb] + srb.war(), inc="sDV")
                srb.set(trbe)
                s5.read(trbe)
                sab = self.S_(f"a_abuf{hp}")
                tf1 = P.emit("dve", lambda e, hp=hp, c=c: e.tensor_mul(
                    z["abuf"][0:64, hp, c * 128:(c + 1) * 128],
                    ps4[0:64, 256:384], z["rb"][0:64, 0:128]),
                    waits=[trbe] + sab.war(), inc="sDV")
                s4.read(tf1)
                tf2 = P.emit("dve", lambda e, hp=hp, c=c: e.tensor_mul(
                    z["abuf"][64:128, hp, c * 128:(c + 1) * 128],
                    ps4[0:64, 384:512], z["rb"][0:64, 128:256]),
                    waits=[trbe], inc="sDV")
                s4.read(tf2)
                srb.read(tf2)
                sab.set(tf2)

    def perf_wo_ln1(self, seq, l, half, t2, two):
        P = self.P
        z = self.za
        t_abs = 2 * half + t2
        ps4, s4 = self.ps[4], self.ps_slot[4]
        for ncx in range(NCHUNK):
            mmtok = None
            for f in range(NCHUNK):
                first, last = f == 0, f == NCHUNK - 1
                w = [two, self.S_(f"a_abuf{f}").ready]
                if first:
                    w += s4.war()
                mmtok = P.emit("pe", lambda e, f=f, ncx=ncx, t2=t2: e.matmul(
                    ps4[:, 0:512], z["wo"][:, f, ncx * 128:(ncx + 1) * 128],
                    z["abuf"][:, f, t2 * 512:(t2 + 1) * 512],
                    start=f == 0, stop=f == NCHUNK - 1),
                    waits=w, inc="sPE" if last else None)
                if first:
                    s4.set(mmtok)
                self.S_(f"a_abuf{f}").read(mmtok)
            s4.set(mmtok)
            sd = self.S_("d1_t")
            xb = self.xb_ap(ncx, t_abs * 512, 512)
            xl = self.xb_ap(ncx, t_abs * 512, 512, lo=True)
            p1 = P.emit("dve", lambda e, ncx=ncx, xb=xb: e.scalar_tensor_tensor(
                self.d1_t[:], ps4[:, 0:512], self.pcol[:, 16 + ncx:17 + ncx],
                xb, ALU.add, ALU.add),
                waits=[mmtok, self.sXB[ncx][t_abs].ready] + self.ptoks + sd.war(),
                inc="sDV")
            sd.set(p1)
            s4.read(p1)
            self.sXB[ncx][t_abs].read(p1)
            st2 = self.S_(f"t2_{ncx}")
            p2 = P.emit("pool", lambda e, ncx=ncx, xl=xl: e.tensor_add(
                self.t2[:, ncx, :], self.d1_t[:], xl),
                waits=[p1, self.sXL[ncx][t_abs].ready] + st2.war(), inc="sPL")
            st2.set(p2)
            sd.read(p2)
            self.sXL[ncx][t_abs].read(p2)
            tk = self.tile_stats(ncx, self.t2[:, ncx, :], p2, bank=6)
            st2.read(tk)
        tmu, trb, _ = self.ln_rowmath(6, 5)
        for ncx in range(NCHUNK):
            st2 = self.S_(f"t2_{ncx}")
            sd = self.S_("d1_t")
            ta = P.emit("dve", lambda e, ncx=ncx: e.tensor_sub(
                self.d1_t[:], self.t2[:, ncx, :], self.mu_t[:]),
                waits=[st2.ready, tmu] + sd.war(), inc="sDV")
            sd.set(ta)
            st2.read(ta)
            self.S_("mu_t").read(ta)
            sh = self.S_("hs_t")
            tb = P.emit("dve", lambda e: e.tensor_mul(
                self.hs_t[:], self.d1_t[:], self.rb_t[:]),
                waits=[ta, trb] + sh.war(), inc="sDV")
            sd.read(tb)
            self.S_("rb_t").read(tb)
            tc = P.emit("pool", lambda e, ncx=ncx: e.tensor_scalar(
                self.hs_t[:], self.hs_t[:], self.pcol[:, 24 + ncx:25 + ncx],
                self.pcol[:, 32 + ncx:33 + ncx], ALU.mult, ALU.add),
                waits=[tb] + self.ptoks, inc="sPL")
            sh.set(tc)
            self.stream_split(ncx, t_abs, tc)

    def perf_ffn_tile(self, seq, l, t):
        P = self.P
        z = self.za
        for g in range(32):
            sl = g % 4
            slab = z["fslab"][:, sl, :, :]
            ssl = self.S_(f"af1_slab{sl}")
            tw = P.dma("pool" if g % 2 == 0 else "sync",
                       slab, self.W["aW1"][l, g], waits=ssl.war())
            ssl.set(tw)
            bank = g % 2
            ps, pslot = self.ps[bank], self.ps_slot[bank]
            mmtok = None
            for f in range(NCHUNK):
                first, last = f == 0, f == NCHUNK - 1
                rhs = self.xb_ap(f, t * 512, 512)
                w = [tw, self.sXB[f][t].ready]
                if first:
                    w += pslot.war()
                mmtok = P.emit("pe", lambda e, slab=slab, f=f, rhs=rhs,
                               ps=ps: e.matmul(
                    ps[:, 0:512], slab[:, f, :], rhs,
                    start=f == 0, stop=f == NCHUNK - 1),
                    waits=w, inc="sPE" if last else None)
                if first:
                    pslot.set(mmtok)
                self.sXB[f][t].read(mmtok)
            pslot.set(mmtok)
            ssl.read(mmtok)
            sh = self.S_(f"a_hgr{g}")
            te = P.emit("act", lambda e, g=g, ps=ps: e.activation(
                z["hgr"][:, g, :], ps[:, 0:512], AF.Relu,
                bias=self.pcol[:, 64 + g:65 + g], scale=1.0),
                waits=[mmtok] + self.ptoks + sh.war(), inc="sAC")
            sh.set(te)
            pslot.read(te)
        for ncx in range(NCHUNK):
            sl = ncx % 4
            slab2 = z["fslab2"][:, sl, :, :]
            ssl = self.S_(f"af2_slab{sl}")
            tw = P.dma("pool" if ncx % 2 == 0 else "sync",
                       slab2, self.W["aW2"][l, ncx], waits=ssl.war())
            ssl.set(tw)
            bank = 2 + ncx % 2
            ps, pslot = self.ps[bank], self.ps_slot[bank]
            mmtok = None
            for g in range(32):
                first, last = g == 0, g == 31
                w = [tw, self.S_(f"a_hgr{g}").ready]
                if first:
                    w += pslot.war()
                mmtok = P.emit("pe", lambda e, slab2=slab2, g=g, ps=ps: e.matmul(
                    ps[:, 0:512], slab2[:, g, :], z["hgr"][:, g, :],
                    start=g == 0, stop=g == 31),
                    waits=w, inc="sPE" if last else None)
                if first:
                    pslot.set(mmtok)
                self.S_(f"a_hgr{g}").read(mmtok)
            pslot.set(mmtok)
            ssl.read(mmtok)
            sd = self.S_("d1_t")
            xb = self.xb_ap(ncx, t * 512, 512)
            xl = self.xb_ap(ncx, t * 512, 512, lo=True)
            p1 = P.emit("dve", lambda e, ncx=ncx, xb=xb, ps=ps: e.scalar_tensor_tensor(
                self.d1_t[:], ps[:, 0:512], self.pcol[:, 56 + ncx:57 + ncx],
                xb, ALU.add, ALU.add),
                waits=[mmtok, self.sXB[ncx][t].ready] + self.ptoks + sd.war(),
                inc="sDV")
            sd.set(p1)
            pslot.read(p1)
            self.sXB[ncx][t].read(p1)
            st2 = self.S_(f"t2_{ncx}")
            p2 = P.emit("pool", lambda e, ncx=ncx, xl=xl: e.tensor_add(
                self.t2[:, ncx, :], self.d1_t[:], xl),
                waits=[p1, self.sXL[ncx][t].ready] + st2.war(), inc="sPL")
            st2.set(p2)
            sd.read(p2)
            self.sXL[ncx][t].read(p2)
            tk = self.tile_stats(ncx, self.t2[:, ncx, :], p2, bank=6)
            st2.read(tk)
        tmu, trb, _ = self.ln_rowmath(6, 5)
        for ncx in range(NCHUNK):
            st2 = self.S_(f"t2_{ncx}")
            sd = self.S_("d1_t")
            ta = P.emit("dve", lambda e, ncx=ncx: e.tensor_sub(
                self.d1_t[:], self.t2[:, ncx, :], self.mu_t[:]),
                waits=[st2.ready, tmu] + sd.war(), inc="sDV")
            sd.set(ta)
            st2.read(ta)
            self.S_("mu_t").read(ta)
            sh = self.S_("hs_t")
            tb = P.emit("dve", lambda e: e.tensor_mul(
                self.hs_t[:], self.d1_t[:], self.rb_t[:]),
                waits=[ta, trb] + sh.war(), inc="sDV")
            sd.read(tb)
            self.S_("rb_t").read(tb)
            tc = P.emit("pool", lambda e, ncx=ncx: e.tensor_scalar(
                self.hs_t[:], self.hs_t[:], self.pcol[:, 40 + ncx:41 + ncx],
                self.pcol[:, 48 + ncx:49 + ncx], ALU.mult, ALU.add),
                waits=[tb] + self.ptoks, inc="sPL")
            sh.set(tc)
            self.stream_split(ncx, t, tc)

    def phase_ffn(self, seq, first):
        for l in range(6):
            self.ffn_layer(seq, l)

    def ffn_layer(self, seq, l):
        P = self.P
        z = self.zb
        self.barrier()
        wq_toks = []
        for f in range(NCHUNK):
            wq_toks.append(P.dma("pool" if f % 2 == 0 else "sync",
                                 z["wqkv"][:, f, :],
                                 self.W["fWqkv"][l, :, f, :]))
        rot_toks = [P.dma("sync", z["rot"][:, i, :, :], self.C[nm])
                    for i, nm in enumerate(("c_rot_cs_q", "c_rot_sn_q",
                                            "c_rot_cs_k", "c_rot_sn_k"))]
        pc = [P.dma("sync", self.pcol[:, 0:44], self.W["fb1"][l]),
              P.dma("sync", self.pcol[:, 44:52], self.W["fb2"][l]),
              P.dma("sync", self.pcol[:, 52:60], self.W["flng_col"][l]),
              P.dma("sync", self.pcol[:, 60:68], self.W["flnb_col"][l])]
        tgr = P.dma("pool", self.prow[:, 0:1024], self.W["flng_row"][l])
        tlb = P.dma("pool", self.t2b[:, 0:8], self.W["flnb_col"][l])
        self.S_("t2b").set(tlb)
        self.ptoks = pc + [tgr, tlb]
        tq1 = P.dma("pool", self.x2b[0:1, 0:64], self.W["fqs_row"][l])
        tq2 = P.dma("pool", self.x2b[0:1, 64:128], self.W["fks_row"][l])
        ps6, s6 = self.ps[6], self.ps_slot[6]
        tb1 = P.emit("pe", lambda e: e.matmul(
            ps6[:, 0:128], self.cs["c_ones_1x128"][:], self.x2b[0:1, 0:128],
            start=True, stop=True), waits=[tq1, tq2] + s6.war(), inc="sPE")
        s6.set(tb1)
        sqs = self.S_("b_qstile")
        te1 = P.emit("dve", lambda e: e.tensor_copy(z["qstile"][:], ps6[:, 0:128]),
                     waits=[tb1] + sqs.war(), inc="sDV")
        sqs.set(te1)
        s6.read(te1)
        srs = self.S_("b_rsc1")
        tsh = None
        for half0 in (0, 1):
            for j0 in (0, 32):
                src_c = half0 * 64 + ((j0 + 32) % 64)
                dst_c = half0 * 64 + j0
                tsh = P.emit("pool", lambda e, src_c=src_c, dst_c=dst_c: e.tensor_copy(
                    self.zb["rsc1"][:, dst_c:dst_c + 32],
                    z["qstile"][:, src_c:src_c + 32]),
                    waits=[te1] + (srs.war() if (half0 == 0 and j0 == 0) else []),
                    inc="sPL")
        srs.set(tsh)
        sqs.read(tsh)
        srot = self.S_("b_rot")
        trot = None
        for i, (col0, from_sh) in enumerate(((0, False), (0, True),
                                             (64, False), (64, True))):
            srctile = self.zb["rsc1"] if from_sh else z["qstile"]
            trot = P.emit("pool", lambda e, i=i, col0=col0, srctile=srctile: e.tensor_tensor(
                z["rot"][:, i, :, :], z["rot"][:, i, :, :],
                bass.AP(srctile, col0, [[srctile.shape[1], 128], [0, 16], [1, 64]]),
                ALU.mult),
                waits=[rot_toks[i], te1, tsh] + (srot.war() if i == 0 else []),
                inc="sPL")
        srot.set(trot)
        srs.read(trot)
        sqs.read(trot)
        tvo = P.emit("pool", lambda e: e.memset(
            bass.AP(z["vroll"], 64, [[3120, 128], [65, 3 * 16], [1, 1]]), 1.0),
            waits=self.S_("b_vroll").war(), inc="sPL")
        self.S_("b_vroll").set(tvo)
        tsm0 = P.emit("pool", lambda e: e.memset(z["sm"][:], 0.0),
                      waits=self.S_("b_sm").war(), inc="sPL")
        self.S_("b_sm").set(tsm0)
        tpt0 = P.emit("pool", lambda e: e.memset(z["pt"][:], 0.0),
                      waits=self.S_("b_pt").war(), inc="sPL")
        self.S_("b_pt").set(tpt0)

        for ts in range(NTSUB):
            cb = None
            if ts % 4 == 0 and ts > 0:
                cb = (lambda t=ts // 4 - 1: self.ffn_wo_resid(seq, l, t))
            self.ffn_attn_tsub(seq, l, ts, wq_toks, trot, mid_cb=cb)
        self.ffn_wo_resid(seq, l, 3)
        if seq == 0 and l == 0:
            self.dump_stream("b_l0a")
        self.barrier()
        for t in range(NT512 - 1, -1, -1):
            self.ffn_geglu_tile(seq, l, t)
        if seq == 0 and l == 0:
            self.dump_stream("b_l0")

    def phase_attn_b(self, seq):
        P = self.P
        z = self.zc
        self.barrier()
        wk_toks = [P.dma("pool" if f % 2 == 0 else "sync",
                         z["w1"][:, f, :], self.W["bWk"][:, f, :])
                   for f in range(NCHUNK)]
        wv_toks = [P.dma("sync" if f % 2 == 0 else "pool",
                         z["w2"][:, f, :], self.W["bWv"][:, f, :])
                   for f in range(NCHUNK)]
        sK = self.S_("c_K")
        for nc in range(NCHUNK):
            for t in range(NT512):
                bank = (nc * NT512 + t) % 2
                ps, pslot = self.ps[bank], self.ps_slot[bank]
                mmtok = None
                for f in range(NCHUNK):
                    w = [wk_toks[f], self.sXB[f][t].ready]
                    if f == 0:
                        w += pslot.war()
                    mmtok = P.emit("pe", lambda e, f=f, nc=nc, t=t, ps=ps: e.matmul(
                        ps[:, 0:512], z["w1"][:, f, nc * 128:(nc + 1) * 128],
                        self.xb_ap(f, t * 512, 512),
                        start=f == 0, stop=f == NCHUNK - 1),
                        waits=w, inc="sPE" if f == NCHUNK - 1 else None)
                    if f == 0:
                        pslot.set(mmtok)
                pslot.set(mmtok)
                for f in range(NCHUNK):
                    self.sXB[f][t].read(mmtok)
                te = P.emit("act", lambda e, nc=nc, t=t, ps=ps: e.activation(
                    z["K"][:, nc, t * 512:(t + 1) * 512], ps[:, 0:512],
                    AF.Identity, bias=0.0, scale=1.0),
                    waits=[mmtok] + (sK.war() if (nc == 0 and t == 0) else []),
                    inc="sAC")
                pslot.read(te)
        sK.set(te)
        sV = self.S_("c_V")
        tvo = P.emit("pool", lambda e: e.memset(
            bass.AP(z["V"], 64, [[16640, 128], [65, 16 * 16], [1, 1]]), 1.0),
            waits=sV.war(), inc="sPL")
        sV.set(tvo)
        for ts in range(NTSUB):
            for nh in range(2):
                bank = 2 + (ts * 2 + nh) % 2
                ps, pslot = self.ps[bank], self.ps_slot[bank]
                mmtok = None
                for f in range(NCHUNK):
                    w = [wv_toks[f], self.sXB[f][ts // 4].ready]
                    if f == 0:
                        w += pslot.war()
                    mmtok = P.emit("pe", lambda e, f=f, ts=ts, nh=nh, ps=ps: e.matmul(
                        ps[:, 0:512], self.xb_ap(f, ts * 128, 128),
                        z["w2"][:, f, nh * 512:(nh + 1) * 512],
                        start=f == 0, stop=f == NCHUNK - 1),
                        waits=w, inc="sPE" if f == NCHUNK - 1 else None)
                    if f == 0:
                        pslot.set(mmtok)
                pslot.set(mmtok)
                for f in range(NCHUNK):
                    self.sXB[f][ts // 4].read(mmtok)
                tev = P.emit("dve", lambda e, ts=ts, nh=nh, ps=ps: e.tensor_copy(
                    z["V"][:, ts, nh * 8:(nh + 1) * 8, 0:64],
                    ps[:, 0:512].rearrange("p (h d) -> p h d", h=8)),
                    waits=[mmtok, tvo], inc="sDV")
                pslot.read(tev)
                sV.read(tev)
        sV.set(tev)
        self.barrier()
        wq_toks = [P.dma("pool" if f % 2 == 0 else "sync",
                         z["w1"][:, f, :], self.W["bWq"][:, f, :])
                   for f in range(NCHUNK)]
        for T in range(NTSUB):
            t512 = T // 4
            sQ = self.S_("c_Q")
            for nc in range(NCHUNK):
                bank = nc % 2
                ps, pslot = self.ps[bank], self.ps_slot[bank]
                mmtok = None
                for f in range(NCHUNK):
                    w = [wq_toks[f], self.sXB[f][t512].ready]
                    if f == 0:
                        w += pslot.war()
                    mmtok = P.emit("pe", lambda e, f=f, nc=nc, T=T, ps=ps: e.matmul(
                        ps[:, 0:128], z["w1"][:, f, nc * 128:(nc + 1) * 128],
                        self.xb_ap(f, T * 128, 128),
                        start=f == 0, stop=f == NCHUNK - 1),
                        waits=w, inc="sPE" if f == NCHUNK - 1 else None)
                    if f == 0:
                        pslot.set(mmtok)
                pslot.set(mmtok)
                for f in range(NCHUNK):
                    self.sXB[f][t512].read(mmtok)
                tqe = P.emit("act", lambda e, nc=nc, ps=ps: e.activation(
                    z["Q"][:, nc, :], ps[:, 0:128], AF.Identity,
                    bias=0.0, scale=0.125),
                    waits=[mmtok] + (sQ.war() if nc == 0 else []), inc="sAC")
                pslot.read(tqe)
            sQ.set(tqe)
            sao = self.S_("c_ao")
            for hp in range(NCHUNK):
                sm_col = self.S_("c_m")
                ngrp = (T + 4) // 4
                tred = None
                for g in range(ngrp):
                    kb0 = g * 4
                    nk = min(4, T + 1 - kb0)
                    smm = [None, None]
                    for hh in range(2):
                        bank = hh
                        ps, pslot = self.ps[bank], self.ps_slot[bank]
                        mmtok = None
                        for i in range(nk):
                            kb = kb0 + i
                            w = [tqe, sK.ready]
                            if i == 0:
                                w += pslot.war()
                            mmtok = P.emit("pe", lambda e, hh=hh, hp=hp, kb=kb,
                                           i=i, ps=ps: e.matmul(
                                ps[:, i * 128:(i + 1) * 128],
                                z["Q"][64 * hh:64 * hh + 64, hp, :],
                                z["K"][64 * hh:64 * hh + 64, hp,
                                       kb * 128:(kb + 1) * 128],
                                start=True, stop=True,
                                tile_position=(64 * hh, 0)),
                                waits=w, inc="sPE" if i == nk - 1 else None)
                            if i == 0:
                                pslot.set(mmtok)
                        pslot.set(mmtok)
                        sQ.read(mmtok)
                        sK.read(mmtok)
                        smm[hh] = mmtok
                    if kb0 + nk - 1 == T:
                        for hh in range(2):
                            ps, pslot = self.ps[hh], self.ps_slot[hh]
                            tmsk = P.emit("dve", lambda e, hh=hh, nk=nk, ps=ps:
                                          e.scalar_tensor_tensor(
                                ps[:, (nk - 1) * 128:nk * 128],
                                ps[:, (nk - 1) * 128:nk * 128], 1.0,
                                self.cs["c_negmask_rot0"][:, 0:128],
                                ALU.mult, ALU.add),
                                waits=[smm[hh]], inc="sDV")
                            self.ps_slot[hh].set(tmsk)
                            smm[hh] = tmsk
                    for hh in range(2):
                        ps = self.ps[hh]
                        dst = z["m"][:, 2 + hh:3 + hh] if g > 0 else z["m"][:, hh:hh + 1]
                        tred = P.emit("dve", lambda e, hh=hh, nk=nk, g=g,
                                      dst=dst, ps=ps: e.tensor_reduce(
                            dst, ps[:, 0:nk * 128], AX.X, ALU.max),
                            waits=[smm[hh]] + (sm_col.war() if (g == 0 and hh == 0)
                                               else []), inc="sDV")
                        self.ps_slot[hh].read(tred)
                        if g > 0:
                            tred = P.emit("dve", lambda e, hh=hh: e.tensor_max(
                                z["m"][:, hh:hh + 1], z["m"][:, hh:hh + 1],
                                z["m"][:, 2 + hh:3 + hh]),
                                waits=[tred], inc="sDV")
                    sm_col.set(tred)
                tcb = P.emit("pool", lambda e: e.tensor_copy(z["mb"][:], z["m"][:, 0:2]),
                             waits=[tred] + self.S_("c_mb").war(), inc="sPL")
                self.S_("c_mb").set(tcb)
                sm_col.read(tcb)
                ps2, s2 = self.ps[2], self.ps_slot[2]
                tr1 = P.emit("pe", lambda e: e.matmul(
                    ps2[0:1, 0:128], z["mb"][:, 0:1], self.cs["c_negident"][:],
                    start=True, stop=True), waits=[tcb] + s2.war(), inc=None)
                tr2 = P.emit("pe", lambda e: e.matmul(
                    ps2[0:1, 128:256], z["mb"][:, 1:2], self.cs["c_negident"][:],
                    start=True, stop=True), waits=[], inc="sPE")
                s2.set(tr2)
                self.S_("c_mb").read(tr2)
                smr = self.S_("x2b")
                tmr = P.emit("dve", lambda e: e.tensor_copy(
                    self.x2b[0:1, 256:512], ps2[0:1, 0:256]),
                    waits=[tr2] + smr.war(), inc="sDV")
                smr.set(tmr)
                s2.read(tmr)
                ps4, s4 = self.ps[4], self.ps_slot[4]
                ps6b, s6b = self.ps[6], self.ps_slot[6]
                avps = (ps4, ps6b)
                avsl = (s4, s6b)
                spt = self.S_("c_pt")
                av_end = None
                first_av = [True, True]
                for kb in range(T + 1):
                    ex_toks = []
                    for hh in range(2):
                        bank = 2 + hh
                        ps, pslot = self.ps[bank], self.ps_slot[bank]
                        t1 = P.emit("pe", lambda e, hh=hh, hp=hp, kb=kb,
                                    ps=ps: e.matmul(
                            ps[:, 0:128],
                            z["K"][64 * hh:64 * hh + 64, hp,
                                   kb * 128:(kb + 1) * 128],
                            z["Q"][64 * hh:64 * hh + 64, hp, :],
                            start=True, stop=False,
                            tile_position=(64 * hh, 0)),
                            waits=[tqe, sK.ready, tmr] + pslot.war(), inc=None)
                        t2 = P.emit("pe", lambda e, hh=hh, ps=ps: e.matmul(
                            ps[:, 0:128], self.cs["c_ones_1x128"][:],
                            self.x2b[0:1, 256 + hh * 128:384 + hh * 128],
                            start=False, stop=True),
                            waits=[], inc="sPE")
                        pslot.set(t2)
                        sQ.read(t2)
                        sK.read(t2)
                        smr.read(t2)
                        tmask = t2
                        if kb == T:
                            tmask = P.emit("dve", lambda e, ps=ps: e.scalar_tensor_tensor(
                                ps[:, 0:128], ps[:, 0:128], 1.0,
                                self.cs["c_negmaskT_local01"][:, 0:128],
                                ALU.mult, ALU.add),
                                waits=[t2], inc="sDV")
                            pslot.set(tmask)
                        tex = P.emit("act", lambda e, hh=hh, ps=ps: e.activation(
                            z["pt"][:, hh, :], ps[:, 0:128], AF.Exp,
                            bias=0.0, scale=1.0),
                            waits=[tmask] + (spt.war() if hh == 0 else []),
                            inc="sAC")
                        pslot.read(tex)
                        ex_toks.append(tex)
                    spt.set(ex_toks[-1])
                    for hh in range(2):
                        w = [ex_toks[hh], sV.ready]
                        if first_av[hh]:
                            w += avsl[hh].war()
                        av_end = P.emit("pe", lambda e, hh=hh, hp=hp, kb=kb,
                                        T=T, pp=avps[hh]: e.matmul(
                            pp[0:65, 0:128],
                            z["V"][:, kb, 2 * hp + hh, :],
                            z["pt"][:, hh, :],
                            start=kb == 0, stop=kb == T),
                            waits=w, inc="sPE" if (kb == T and hh == 1) else None)
                        if first_av[hh]:
                            avsl[hh].set(av_end)
                            first_av[hh] = False
                    spt.read(av_end)
                    sV.read(av_end)
                sden = self.S_("sq_t")
                tdA = P.emit("act", lambda e: e.activation(
                    self.sq_t[0:1, 0:128], ps4[64:65, 0:128], AF.Ln,
                    bias=0.0, scale=1.0),
                    waits=[av_end] + sden.war(), inc="sAC")
                s4.read(tdA)
                td = P.emit("act", lambda e: e.activation(
                    self.sq_t[0:1, 128:256], ps6b[64:65, 0:128], AF.Ln,
                    bias=0.0, scale=1.0),
                    waits=[av_end], inc="sAC")
                sden.set(td)
                s6b.read(td)
                srec = self.S_("t2b")
                trc = P.emit("act", lambda e: e.activation(
                    self.t2b[0:1, 0:256], self.sq_t[0:1, 0:256], AF.Exp,
                    bias=0.0, scale=-1.0),
                    waits=[tdA, td] + srec.war(), inc="sAC")
                srec.set(trc)
                sden.read(trc)
                ps5, s5 = self.ps[5], self.ps_slot[5]
                trb = P.emit("pe", lambda e: e.matmul(
                    ps5[0:64, 0:256], self.cs["c_ones_1x128"][0:1, 0:64],
                    self.t2b[0:1, 0:256], start=True, stop=True),
                    waits=[trc] + s5.war(), inc="sPE")
                s5.set(trb)
                srec.read(trb)
                srb = self.S_("t2b")
                trbe = P.emit("dve", lambda e: e.tensor_copy(
                    self.t2b[0:64, 256:512], ps5[0:64, 0:256]),
                    waits=[trb] + srb.war(), inc="sDV")
                srb.set(trbe)
                s5.read(trbe)
                for hh in range(2):
                    tf = P.emit("dve", lambda e, hh=hh, hp=hp, pp=avps[hh]:
                                e.tensor_mul(
                        z["ao"][64 * hh:64 * hh + 64, hp, :],
                        pp[0:64, 0:128],
                        self.t2b[0:64, 256 + hh * 128:384 + hh * 128]),
                        waits=[trbe] + (sao.war() if (hp == 0 and hh == 0) else []),
                        inc="sDV")
                    avsl[hh].read(tf)
                    srb.read(tf)
            sao.set(tf)
            for nc in range(NCHUNK):
                sl = nc % 2
                slab = z["woslab"][:, sl, :, :]
                ssl = self.S_(f"c_woslab{sl}")
                tw = P.dma("pool" if nc % 2 == 0 else "sync", slab,
                           self.W["bWo"][:, :, nc * 128:(nc + 1) * 128],
                           waits=ssl.war())
                ssl.set(tw)
                ps5, s5 = self.ps[5], self.ps_slot[5]
                mmtok = None
                for f in range(NCHUNK):
                    w = [tw, sao.ready]
                    if f == 0:
                        w += s5.war()
                    mmtok = P.emit("pe", lambda e, f=f, slab=slab: e.matmul(
                        ps5[:, 0:128], slab[:, f, :], z["ao"][:, f, :],
                        start=f == 0, stop=f == NCHUNK - 1),
                        waits=w, inc="sPE" if f == NCHUNK - 1 else None)
                    if f == 0:
                        s5.set(mmtok)
                s5.set(mmtok)
                ssl.read(mmtok)
                sao.read(mmtok)
                sxp = self.S_("c_xp")
                tld = P.dma("sync", z["xp"][:],
                            self.xr[1][nc * 128:(nc + 1) * 128,
                                       T * 128:(T + 1) * 128], waits=sxp.war())
                sxp.set(tld)
                st2 = self.S_(f"t2_{nc}")
                tt2 = P.emit("dve", lambda e, nc=nc, T=T: e.tensor_add(
                    self.t2[:, nc, (T % 4) * 128:(T % 4 + 1) * 128],
                    z["xp"][:], ps5[:, 0:128]),
                    waits=[mmtok, tld] + (st2.war() if T % 4 == 0 else []),
                    inc="sDV")
                if T % 4 == 0:
                    st2.set(tt2)
                sxp.read(tt2)
                s5.read(tt2)
                st2.set(tt2)
            if T % 4 == 3:
                t = t512
                t2tok = []
                for c in range(NCHUNK):
                    st2 = self.S_(f"t2_{c}")
                    tok = st2.ready
                    t2tok.append(tok)
                    xn_ap = self.xr[0][c * 128:(c + 1) * 128,
                                       t * 512:(t + 1) * 512]
                    tsp = P.dma("act", xn_ap, self.t2[:, c, :], waits=[tok])
                    st2.read(tsp)
                    tk = self.tile_stats(c, self.t2[:, c, :], tok, bank=6)
                    st2.read(tk)
                tmu, trb2, _ = self.ln_rowmath(6, 5)
                for c in range(NCHUNK):
                    st2 = self.S_(f"t2_{c}")
                    sd = self.S_("d1_t")
                    ta = P.emit("dve", lambda e, c=c: e.tensor_sub(
                        self.d1_t[:], self.t2[:, c, :], self.mu_t[:]),
                        waits=[st2.ready, tmu] + sd.war(), inc="sDV")
                    sd.set(ta)
                    st2.read(ta)
                    self.S_("mu_t").read(ta)
                    sh = self.S_("hs_t")
                    tb = P.emit("dve", lambda e: e.tensor_mul(
                        self.hs_t[:], self.d1_t[:], self.rb_t[:]),
                        waits=[ta, trb2] + sh.war(), inc="sDV")
                    sh.set(tb)
                    sd.read(tb)
                    self.S_("rb_t").read(tb)
                    self.stream_split(c, t, tb)

    def finalize(self):
        from contextlib import ExitStack
        nc = self.nc
        with ExitStack() as ctx:
            for s in list(self.P.counts):
                self.P.handles[s] = ctx.enter_context(nc.semaphore(s))
            block = ctx.enter_context(nc.Block())
            block.tensor(lambda e: self.P.run_stream("pe", e))
            block.vector(lambda e: self.P.run_stream("dve", e))
            block.scalar(lambda e: self.P.run_stream("act", e))
            block.gpsimd(lambda e: self.P.run_stream("pool", e))
            block.sync(lambda e: self.P.run_stream("sync", e))
        return nc


    def ffn_attn_tsub(self, seq, l, ts, wq_toks, trot, mid_cb=None):
        P = self.P
        z = self.zb
        t512 = ts // 4
        part_tok = []
        for part in range(3):
            for nh in range(2):
                bank = part * 2 + nh
                ps, pslot = self.ps[bank], self.ps_slot[bank]
                mmtok = None
                for f in range(NCHUNK):
                    first = f == 0
                    lhs = self.xb_ap(f, ts * 128, 128, shifted=(f >= 4))
                    w = [wq_toks[f], self.sXB[f][t512].ready]
                    if f >= 4 and ts % 4 == 0 and t512 > 0:
                        w.append(self.sXB[f][t512 - 1].ready)
                    if first:
                        w += pslot.war()
                    mmtok = P.emit("pe", lambda e, f=f, lhs=lhs, part=part, nh=nh,
                                   ps=ps: e.matmul(
                        ps[:, 0:512], lhs,
                        z["wqkv"][:, f, part * 1024 + nh * 512:
                                  part * 1024 + (nh + 1) * 512],
                        start=f == 0, stop=f == NCHUNK - 1),
                        waits=w, inc="sPE" if f == NCHUNK - 1 else None)
                    if first:
                        pslot.set(mmtok)
                pslot.set(mmtok)
                for f in range(NCHUNK):
                    self.sXB[f][t512].read(mmtok)
                    if f >= 4 and ts % 4 == 0 and t512 > 0:
                        self.sXB[f][t512 - 1].read(mmtok)
                part_tok.append(mmtok)
        if mid_cb is not None:
            mid_cb()
        ssq = self.S_("b_sq")
        snm = self.S_("b_norms")
        tnm = None
        for part in range(2):
            sq_toks = []
            for nh in range(2):
                bank = part * 2 + nh
                tq = P.emit("act", lambda e, bank=bank, nh=nh: e.activation(
                    z["sq"][:, nh * 512:(nh + 1) * 512], self.ps[bank][:, 0:512],
                    AF.Square, bias=0.0, scale=1.0),
                    waits=[part_tok[part * 2 + nh]] + (ssq.war() if nh == 0 else []),
                    inc="sAC")
                sq_toks.append(tq)
                self.ps_slot[bank].read(tq)
            ssq.set(sq_toks[-1])
            tnm = P.emit("dve", lambda e, part=part: e.tensor_reduce(
                z["norms"][:, part * 16:(part + 1) * 16],
                bass.AP(z["sq"], 0, [[1024, 128], [64, 16], [1, 64]]),
                AX.X, ALU.add),
                waits=sq_toks + (snm.war() if part == 0 else []), inc="sDV")
            ssq.read(tnm)
        snm.set(tnm)
        srs = self.S_("b_rstd")
        t1 = P.emit("dve", lambda e: e.tensor_scalar_max(
            z["rstd"][:], z["norms"][:], 1e-24),
            waits=[tnm] + srs.war(), inc="sDV")
        snm.read(t1)
        t2 = P.emit("act", lambda e: e.activation(
            z["rstd"][:], z["rstd"][:], AF.Ln, bias=0.0, scale=1.0),
            waits=[t1], inc="sAC")
        t3 = P.emit("act", lambda e: e.activation(
            z["rstd"][:], z["rstd"][:], AF.Exp, bias=0.0, scale=-0.5),
            waits=[t2], inc="sAC")
        srs.set(t3)
        sqh = self.S_("b_qhat")
        tqh = None
        for part in range(2):
            for nh in range(2):
                bank = part * 2 + nh
                rs = bass.AP(z["rstd"], part * 16 + nh * 8,
                             [[32, 128], [1, 8], [0, 64]])
                tqh = P.emit("dve", lambda e, bank=bank, part=part, nh=nh, rs=rs:
                             e.tensor_tensor(
                    z["qhat"][:, part * 1024 + nh * 512:
                              part * 1024 + (nh + 1) * 512],
                    self.ps[bank][:, 0:512], rs, ALU.mult),
                    waits=[t3, part_tok[part * 2 + nh]] +
                          (sqh.war() if (part == 0 and nh == 0) else []),
                    inc="sDV")
                self.ps_slot[bank].read(tqh)
        sqh.set(tqh)
        srs.read(tqh)
        if seq == 0 and l == 0 and ts == 0:
            self.dump_tile("qpre0", z["qhat"][:], [tqh],
                           slots=("b_qhat", "b_rout"))
        sr1 = self.S_("b_rsc1x")
        sr2 = self.S_("b_sq")
        sro = self.S_("b_rout")
        tro = None
        for part in range(2):
            c1 = bass.AP(z["rot"], (part * 2) * 1024 + ts * 64,
                         [[4 * 1024, 128], [0, 16], [1, 64]])
            c2 = bass.AP(z["rot"], (part * 2 + 1) * 1024 + ts * 64,
                         [[4 * 1024, 128], [0, 16], [1, 64]])
            ta = P.emit("dve", lambda e, part=part, c1=c1: e.tensor_tensor(
                self.zb["rsc1"][:, 0:1024], z["qhat"][:, part * 1024:(part + 1) * 1024],
                c1, ALU.mult),
                waits=[tqh, trot] + sr1.war(), inc="sDV")
            sr1.set(ta)
            sqh.read(ta)
            tb_ = None
            for j0, joff in ((0, 32), (32, -32)):
                qsh = bass.AP(z["qhat"], part * 1024 + j0 + joff,
                              [[2048, 128], [64, 16], [1, 32]])
                c2s = bass.AP(z["rot"], (part * 2 + 1) * 1024 + ts * 64 + j0,
                              [[4 * 1024, 128], [0, 16], [1, 32]])
                tb_ = P.emit("dve", lambda e, qsh=qsh, c2s=c2s, j0=j0: e.tensor_tensor(
                    bass.AP(self.zb["sq"], j0, [[1024, 128], [64, 16], [1, 32]]),
                    qsh, c2s, ALU.mult),
                    waits=[tqh, trot] + (sr2.war() if j0 == 0 else []), inc="sDV")
            sr2.set(tb_)
            sqh.read(tb_)
            tro = P.emit("pool", lambda e, part=part: e.tensor_add(
                z["qhat"][:, part * 1024:(part + 1) * 1024],
                self.zb["rsc1"][:, 0:1024], self.zb["sq"][:, 0:1024]),
                waits=[ta, tb_] + (sro.war() if part == 0 else []), inc="sPL")
            sr1.read(tro)
            sr2.read(tro)
        sro.set(tro)
        if seq == 0 and l == 0 and ts == 0:
            self.dump_tile("qh0", z["qhat"][:], [tro],
                           slots=("b_qhat", "b_rout"))
        psbf, sbf = self.psbf, self.ps_slot[7]
        sqc = self.S_("b_qcur")
        skr = self.S_(f"b_kroll{ts % 3}")
        tts = None
        for hp in range(NCHUNK):
            w = [tro] + (sbf.war() if hp == 0 else [])
            tts = P.emit("pe", lambda e, hp=hp: e.transpose(
                psbf[:, hp * 128:(hp + 1) * 128],
                z["qhat"][:, hp * 128:(hp + 1) * 128], self.cs["c_ident"][:]),
                waits=w, inc="sPE" if hp == NCHUNK - 1 else None)
            sro.read(tts)
        sbf.set(tts)
        tq_end = P.emit("dve", lambda e: e.tensor_copy(
            z["qcur"][:], psbf[:, 0:1024].rearrange("p (h d) -> p h d", h=8)),
            waits=[tts] + sqc.war(), inc="sDV")
        sqc.set(tq_end)
        sbf.read(tq_end)
        for hp in range(NCHUNK):
            w = [tro] + (sbf.war() if hp == 0 else [])
            tts = P.emit("pe", lambda e, hp=hp: e.transpose(
                psbf[:, hp * 128:(hp + 1) * 128],
                z["qhat"][:, 1024 + hp * 128:1024 + (hp + 1) * 128],
                self.cs["c_ident"][:]),
                waits=w, inc="sPE" if hp == NCHUNK - 1 else None)
            sro.read(tts)
        sbf.set(tts)
        tk_end = P.emit("dve", lambda e, ts=ts: e.tensor_copy(
            bass.AP(z["kroll"], (ts % 3) * 128, [[3072, 128], [384, 8], [1, 128]]),
            psbf[:, 0:1024].rearrange("p (h d) -> p h d", h=8)),
            waits=[tts] + skr.war(), inc="sDV")
        skr.set(tk_end)
        sbf.read(tk_end)
        svr = self.S_(f"b_vroll{ts % 3}")
        svra = self.S_("b_vroll")
        tv_end = None
        for nh in range(2):
            bank = 4 + nh
            tv_end = P.emit("dve", lambda e, nh=nh, ts=ts, bank=bank: e.tensor_copy(
                z["vroll"][:, ts % 3, nh * 8:(nh + 1) * 8, 0:64],
                self.ps[bank][:, 0:512].rearrange("p (h d) -> p h d", h=8)),
                waits=[part_tok[4 + nh], svra.ready] +
                      (svr.war() if nh == 0 else []), inc="sDV")
            self.ps_slot[bank].read(tv_end)
        svr.set(tv_end)
        T = ts
        nkb = min(T + 1, 3)
        mask = self.cs["c_negmask_local" if T % 3 == 2
                       else f"c_negmask_rot{T % 3}"]
        ktoks = [self.S_(f"b_kroll{s}").ready for s in range(nkb)]
        sat = self.S_("b_attout")
        sac = self.S_("b_acol")
        spq = self.S_("b_sm")
        spt = self.S_("b_pt")
        for hp in range(NCHUNK):
            s_tok = [None, None]
            for hh in range(2):
                ps, pslot = self.ps[hh], self.ps_slot[hh]
                mm = P.emit("pe", lambda e, hh=hh, hp=hp, nkb=nkb, ps=ps: e.matmul(
                    ps[:, 0:nkb * 128],
                    z["qcur"][64 * hh:64 * hh + 64, hp, :],
                    z["kroll"][64 * hh:64 * hh + 64, hp, 0:nkb, :],
                    start=True, stop=True,
                    tile_position=(64 * hh, 0)),
                    waits=[tq_end] + ktoks + pslot.war(), inc="sPE")
                pslot.set(mm)
                sqc.read(mm)
                for s in range(nkb):
                    self.S_(f"b_kroll{s}").read(mm)
                s_tok[hh] = mm
            tex = [None, None]
            for hh in range(2):
                ps, pslot = self.ps[hh], self.ps_slot[hh]
                tm = P.emit("dve", lambda e, nkb=nkb, ps=ps, mask=mask:
                            e.scalar_tensor_tensor(
                    ps[:, 0:nkb * 128], ps[:, 0:nkb * 128], 1.0,
                    mask[:, 0:nkb * 128], ALU.mult, ALU.add),
                    waits=[s_tok[hh]], inc="sDV")
                pslot.set(tm)
                tmx = P.emit("dve", lambda e, hh=hh, nkb=nkb, ps=ps: e.tensor_reduce(
                    self.pcol[:, 88 + hh:89 + hh], ps[:, 0:nkb * 128], AX.X, ALU.max,
                    negate=True),
                    waits=[tm] + (sac.war() if hh == 0 else []), inc="sDV")
                pslot.read(tmx)
                sac.set(tmx)
                te_ = P.emit("act", lambda e, hh=hh, nkb=nkb, ps=ps: e.activation(
                    z["sm"][:, hh * 384:hh * 384 + nkb * 128], ps[:, 0:nkb * 128],
                    AF.Exp, bias=self.pcol[:, 88 + hh:89 + hh], scale=1.0),
                    waits=[tmx] + (spq.war() if hh == 0 else []), inc="sAC")
                pslot.read(te_)
                sac.read(te_)
                tex[hh] = te_
            spq.set(tex[1])
            tden = []
            for hh in range(2):
                td_ = P.emit("dve", lambda e, hh=hh, nkb=nkb: e.tensor_reduce(
                    self.pcol[:, 90 + hh:91 + hh],
                    z["sm"][:, hh * 384:hh * 384 + nkb * 128], AX.X, ALU.add),
                    waits=[tex[hh]], inc="sDV")
                tden.append(td_)
                spq.read(td_)
            trc = P.emit("dve", lambda e: e.reciprocal(
                self.pcol[:, 92:94], self.pcol[:, 90:92]), waits=tden, inc="sDV")
            sac.set(trc)
            tsc = [None, None]
            for hh in range(2):
                tsc[hh] = P.emit("dve", lambda e, hh=hh, nkb=nkb: e.tensor_scalar(
                    z["sm"][:, hh * 384:hh * 384 + nkb * 128],
                    z["sm"][:, hh * 384:hh * 384 + nkb * 128],
                    self.pcol[:, 92 + hh:93 + hh], self.ccols[:, 2:3],
                    ALU.mult, ALU.add),
                    waits=[trc, self.ccols_tok], inc="sDV")
                sac.read(tsc[hh])
            spq.set(tsc[1])
            tts = None
            for hh in range(2):
                for kb in range(nkb):
                    slot = hh * 3 + kb
                    first = hh == 0 and kb == 0
                    last = hh == 1 and kb == nkb - 1
                    w = [tsc[hh]] + (sbf.war() if first else [])
                    tts = P.emit("pe", lambda e, hh=hh, kb=kb, slot=slot:
                                 e.transpose(
                        psbf[:, slot * 128:(slot + 1) * 128],
                        z["sm"][:, hh * 384 + kb * 128:hh * 384 + (kb + 1) * 128],
                        self.cs["c_ident"][:]),
                        waits=w, inc="sPE" if last else None)
                    spq.read(tts)
            sbf.set(tts)
            tev = P.emit("dve", lambda e, nkb=nkb: e.tensor_copy(
                z["pt"][:, 0:2, 0:nkb * 128],
                psbf[:, 0:768].rearrange("p (h d) -> p h d", h=2)[:, :, 0:nkb * 128]),
                waits=[tts] + spt.war(), inc="sDV")
            spt.set(tev)
            sbf.read(tev)
            av_end = [None, None]
            for hh in range(2):
                bank = 2 + hh
                ps, pslot = self.ps[bank], self.ps_slot[bank]
                t_ = None
                for kb in range(nkb):
                    w = [tev, self.S_(f"b_vroll{kb}").ready]
                    if kb == 0:
                        w += pslot.war()
                    t_ = P.emit("pe", lambda e, hh=hh, kb=kb, hp=hp, nkb=nkb,
                                ps=ps: e.matmul(
                        ps[0:65, 0:128], z["vroll"][:, kb, 2 * hp + hh, :],
                        z["pt"][:, hh, kb * 128:(kb + 1) * 128],
                        start=kb == 0, stop=kb == nkb - 1),
                        waits=w, inc="sPE" if kb == nkb - 1 else None)
                    if kb == 0:
                        pslot.set(t_)
                pslot.set(t_)
                av_end[hh] = t_
                spt.read(t_)
                for kb in range(nkb):
                    self.S_(f"b_vroll{kb}").read(t_)
            tf = None
            for hh in range(2):
                bank = 2 + hh
                tf = P.emit("act", lambda e, hh=hh, hp=hp, T=T, pp=self.ps[bank]:
                            e.activation(
                    z["attout"][64 * hh:64 * hh + 64, hp,
                                (T % 4) * 128:(T % 4 + 1) * 128],
                    pp[0:64, 0:128], AF.Copy, bias=0.0, scale=1.0),
                    waits=[av_end[hh]] + (sat.war() if (hp == 0 and hh == 0 and
                                                        T % 4 == 0) else []),
                    inc="sAC")
                self.ps_slot[bank].read(tf)
            sat.set(tf)

    def ffn_wo_resid(self, seq, l, t):
        P = self.P
        z = self.zb
        sat = self.S_("b_attout")
        ps5, s5 = self.ps[6], self.ps_slot[6]
        for ncx in range(NCHUNK):
            sl = ncx % 2
            slab = z["woslab"][:, sl, :, :]
            ssl = self.S_(f"b_woslab{sl}")
            tw = P.dma("pool" if ncx % 2 == 0 else "sync",
                       slab, self.W["fWo"][l, ncx], waits=ssl.war())
            ssl.set(tw)
            mmtok = None
            for f in range(NCHUNK):
                w = [tw, sat.ready]
                if f == 0:
                    w += s5.war()
                mmtok = P.emit("pe", lambda e, f=f, slab=slab: e.matmul(
                    ps5[:, 0:512], slab[:, f, :], z["attout"][:, f, :],
                    start=f == 0, stop=f == NCHUNK - 1),
                    waits=w, inc="sPE" if f == NCHUNK - 1 else None)
                if f == 0:
                    s5.set(mmtok)
            s5.set(mmtok)
            ssl.read(mmtok)
            sat.read(mmtok)
            sd = self.S_("d1_t")
            xb = self.xb_ap(ncx, t * 512, 512)
            xl = self.xb_ap(ncx, t * 512, 512, lo=True)
            p1 = P.emit("dve", lambda e, xb=xb: e.tensor_add(
                self.d1_t[:], ps5[:, 0:512], xb),
                waits=[mmtok, self.sXB[ncx][t].ready] + sd.war(), inc="sDV")
            sd.set(p1)
            s5.read(p1)
            self.sXB[ncx][t].read(p1)
            sh = self.S_("hs_t")
            p2 = P.emit("pool", lambda e, xl=xl: e.tensor_add(
                self.hs_t[:], self.d1_t[:], xl),
                waits=[p1, self.sXL[ncx][t].ready] + sh.war(), inc="sPL")
            sh.set(p2)
            sd.read(p2)
            self.sXL[ncx][t].read(p2)
            self.stream_split(ncx, t, p2)

    def ffn_geglu_tile(self, seq, l, t):
        P = self.P
        z = self.zb
        for c in range(NCHUNK):
            xbs = self.xb_ap(c, t * 512, 512, shifted=(c >= 4))
            prev = [self.sXB[c][t - 1].ready] if (c >= 4 and t > 0) else []
            s2 = self.S_("x2b")
            tq = P.emit("act", lambda e, xbs=xbs: e.activation(
                self.x2b[:], xbs, AF.Square, bias=0.0, scale=1.0),
                waits=[self.sXB[c][t].ready] + prev + s2.war(), inc="sAC")
            s2.set(tq)
            tk = self.stats_mms(6, c, xbs, self.x2b[:],
                                [tq, self.sXB[c][t].ready] + prev)
            s2.read(tk)
            self.sXB[c][t].read(tk)
            if c >= 4 and t > 0:
                self.sXB[c][t - 1].read(tk)
        tmu, trb, tnm = self.ln_rowmath(6, 5)
        ps4, s4 = self.ps[4], self.ps_slot[4]
        sxg = self.S_("b_xg")
        txg = None
        for c in range(NCHUNK):
            tb = P.emit("pe", lambda e, c=c: e.matmul(
                ps4[:, 0:512], self.prow[0:1, c * 128:(c + 1) * 128],
                self.negmu_row[:], start=True, stop=True),
                waits=[tnm, self.ptoks[-1]] + s4.war(), inc="sPE")
            s4.set(tb)
            self.S_("negmu_row").read(tb)
            xbs = self.xb_ap(c, t * 512, 512, shifted=(c >= 4))
            prev = [self.sXB[c][t - 1].ready] if (c >= 4 and t > 0) else []
            txg = P.emit("dve", lambda e, c=c, xbs=xbs: e.scalar_tensor_tensor(
                z["xg"][:, c, :], xbs, self.pcol[:, 52 + c:53 + c], ps4[:, 0:512],
                ALU.mult, ALU.add),
                waits=[tb, self.sXB[c][t].ready] + prev + self.ptoks +
                      (sxg.war() if c == 0 else []), inc="sDV")
            s4.read(txg)
            self.sXB[c][t].read(txg)
            if c >= 4 and t > 0:
                self.sXB[c][t - 1].read(txg)
        sxg.set(txg)
        first_t = t == NT512 - 1
        scb = self.S_("b_cb")
        for g in range(22):
            sl = g % 2
            slab = z["w1slab"][:, sl, :, :]
            ssl = self.S_(f"b_w1slab{sl}")
            tw = P.dma("pool" if g % 2 == 0 else "sync",
                       slab, self.W["fW1"][l, g], waits=ssl.war())
            ssl.set(tw)
            atok = gtok = None
            for half in range(2):
                bank = half * 2 + g % 2
                ps, pslot = self.ps[bank], self.ps_slot[bank]
                mmtok = None
                for f in range(NCHUNK):
                    w = [tw, txg]
                    if f == 0:
                        w += pslot.war()
                    mmtok = P.emit("pe", lambda e, f=f, slab=slab, half=half,
                                   ps=ps: e.matmul(
                        ps[:, 0:512], slab[:, f, half * 128:(half + 1) * 128],
                        z["xg"][:, f, :],
                        start=f == 0, stop=f == NCHUNK - 1),
                        waits=w, inc="sPE" if f == NCHUNK - 1 else None)
                    if f == 0:
                        pslot.set(mmtok)
                pslot.set(mmtok)
                sxg.read(mmtok)
                ssl.read(mmtok)
                if half == 0:
                    atok = mmtok
                else:
                    gtok = mmtok
            if first_t:
                cbtok = None
                for half in range(2):
                    for f in range(NCHUNK):
                        w = [tw] + self.ptoks
                        if half == 0 and f == 0:
                            w += s4.war()
                        cbtok = P.emit("pe", lambda e, f=f, half=half, slab=slab:
                                       e.matmul(
                            ps4[:, half:half + 1],
                            slab[:, f, half * 128:(half + 1) * 128],
                            self.t2b[:, f:f + 1],
                            start=f == 0, stop=f == NCHUNK - 1),
                            waits=w, inc="sPE" if (half == 1 and f == NCHUNK - 1)
                            else None)
                        if half == 0 and f == 0:
                            s4.set(cbtok)
                s4.set(cbtok)
                self.S_("t2b").read(cbtok)
                ssl.read(cbtok)
                tcb = P.emit("dve", lambda e, g=g: e.tensor_copy(
                    bass.AP(z["cb2"], g, [[44, 128], [22, 2], [1, 1]]),
                    ps4[:, 0:2]),
                    waits=[cbtok] + (scb.war() if g == 0 else []), inc="sDV")
                s4.read(tcb)
                if g == 21:
                    tcb = P.emit("pool", lambda e: e.tensor_add(
                        z["cb2"][:], z["cb2"][:], self.pcol[:, 0:44]),
                        waits=[tcb] + self.ptoks, inc="sPL")
                scb.set(tcb)
            sd = self.S_("d1_t")
            gbank = 2 + g % 2
            t1 = P.emit("dve", lambda e, gbank=gbank: e.tensor_mul(
                self.d1_t[:], self.ps[gbank][:, 0:512], self.rb_t[:]),
                waits=[gtok, trb, scb.ready] + sd.war(), inc="sDV")
            sd.set(t1)
            self.ps_slot[gbank].read(t1)
            self.S_("rb_t").read(t1)
            sh = self.S_("hs_t")
            t2 = P.emit("act", lambda e, g=g: e.activation(
                self.hs_t[:], self.d1_t[:], AF.Gelu,
                bias=z["cb2"][:, 22 + g:23 + g], scale=1.0),
                waits=[t1, scb.ready] + sh.war(), inc="sAC")
            sh.set(t2)
            sd.read(t2)
            scb.read(t2)
            abank = g % 2
            t3 = P.emit("dve", lambda e, abank=abank: e.tensor_mul(
                self.d1_t[:], self.ps[abank][:, 0:512], self.rb_t[:]),
                waits=[atok, trb, t2] + sd.war(), inc="sDV")
            sd.set(t3)
            self.ps_slot[abank].read(t3)
            self.S_("rb_t").read(t3)
            sgg = self.S_("b_gg")
            t4 = P.emit("dve", lambda e, g=g: e.scalar_tensor_tensor(
                z["gg"][:, g, :], self.d1_t[:], z["cb2"][:, g:g + 1],
                self.hs_t[:], ALU.add, ALU.mult),
                waits=[t3, t2, scb.ready] + (sgg.war() if g == 0 else []),
                inc="sDV")
            sd.read(t4)
            sh.read(t4)
            scb.read(t4)
        sgg = self.S_("b_gg")
        sgg.set(t4)
        for ncx in range(NCHUNK):
            sl = ncx % 2
            slab = z["w2slab"][:, sl, :, :]
            ssl = self.S_(f"b_w2slab{sl}")
            tw = P.dma("pool" if ncx % 2 == 0 else "sync",
                       slab, self.W["fW2"][l, ncx], waits=ssl.war())
            ssl.set(tw)
            bank = 4 + ncx % 2
            ps, pslot = self.ps[bank], self.ps_slot[bank]
            mmtok = None
            for gc in range(22):
                w = [tw, sgg.ready]
                if gc == 0:
                    w += pslot.war()
                mmtok = P.emit("pe", lambda e, gc=gc, slab=slab, ps=ps: e.matmul(
                    ps[:, 0:512], slab[:, gc, :], z["gg"][:, gc, :],
                    start=gc == 0, stop=gc == 21),
                    waits=w, inc="sPE" if gc == 21 else None)
                if gc == 0:
                    pslot.set(mmtok)
            pslot.set(mmtok)
            ssl.read(mmtok)
            sgg.read(mmtok)
            sd = self.S_("d1_t")
            xb = self.xb_ap(ncx, t * 512, 512)
            xl = self.xb_ap(ncx, t * 512, 512, lo=True)
            p1 = P.emit("dve", lambda e, ncx=ncx, xb=xb, ps=ps: e.scalar_tensor_tensor(
                self.d1_t[:], ps[:, 0:512], self.pcol[:, 44 + ncx:45 + ncx],
                xb, ALU.add, ALU.add),
                waits=[mmtok, self.sXB[ncx][t].ready] + self.ptoks + sd.war(),
                inc="sDV")
            sd.set(p1)
            pslot.read(p1)
            self.sXB[ncx][t].read(p1)
            sh = self.S_("hs_t")
            p2 = P.emit("pool", lambda e, xl=xl: e.tensor_add(
                self.hs_t[:], self.d1_t[:], xl),
                waits=[p1, self.sXL[ncx][t].ready] + sh.war(), inc="sPL")
            sh.set(p2)
            sd.read(p2)
            self.sXL[ncx][t].read(p2)
            self.stream_split(ncx, t, p2)


def build_kernel(nseq=SEQ_PER_CORE, phases="ABCD", debug=()):
    mk = MK(nseq=nseq, phases=phases, debug=debug)
    mk.build()
    nc = mk.finalize()
    return mk, nc


def kernel(**inputs):
    I = {k: np.asarray(v) for k, v in inputs.items()}
    H = prep_inputs(I)
    x = I["L0_attn_a_x"]
    mk, nc = build_kernel()
    in_maps = []
    for core in range(NCORES):
        sh = dict(H)
        xs = x[core * SEQ_PER_CORE:(core + 1) * SEQ_PER_CORE]
        sh["xT"] = np.ascontiguousarray(xs.transpose(0, 2, 1))
        in_maps.append(sh)
    res = run_bass_kernel_spmd(nc, in_maps, core_ids=list(range(NCORES)))
    outs = []
    for core in range(NCORES):
        oT = np.asarray(res.results[core]["outT"])
        outs.append(oT.transpose(0, 2, 1))
    return np.ascontiguousarray(np.concatenate(outs, axis=0)).astype(np.float32)
